# version 49
# baseline (speedup 1.0000x reference)
import numpy as np
from contextlib import ExitStack

import concourse.bass as bass
import concourse.mybir as mybir
from concourse.bass_utils import run_bass_kernel_spmd

F32 = mybir.dt.float32
BF16 = mybir.dt.bfloat16
U32 = mybir.dt.uint32
AF = mybir.ActivationFunctionType
ALU = mybir.AluOpType

D = 1024
S = 4096
SQ = 2048
NH = 16
HD = 64
WA = 256
NWKV = S // WA
NWQ = SQ // WA
NKC = S // 128
EW = 6016
NE = 18
DFF = 512
EPS = 1e-5

DEBUG_TAPS = ()
CACHE_SALT = 1


class Sem:
    def __init__(self, ctx, nc, name):
        self.h = ctx.enter_context(nc.semaphore(name))
        self.name = name
        self.n = 0

    def inc(self, k=1):
        self.n += k
        return self.n


class Plan:
    def __init__(self):
        self.items = []
        self.waited = {}

    def em(self, eng, fn):
        self.items.append((eng, fn))

    def wait(self, eng, sem, val):
        if val <= 0:
            return
        key = (eng, sem.name)
        if self.waited.get(key, 0) >= val:
            return
        self.waited[key] = val
        h = sem.h

        def fn(e, h=h, val=val):
            e.wait_ge(h, val)
        self.items.append((eng, fn))

    def run(self, eng, obj):
        for en, fn in self.items:
            if en == eng:
                fn(obj)


def build_nc():
    nc = bass.Bass("TRN2", target_bir_lowering=False)

    xt = nc.dram_tensor("xt", [D, S], F32, kind="ExternalInput").ap()
    xq = nc.dram_tensor("xq", [D, SQ], F32, kind="ExternalInput").ap()
    wqt = nc.dram_tensor("wqt", [D, D], F32, kind="ExternalInput").ap()
    wkt = nc.dram_tensor("wkt", [D, D], F32, kind="ExternalInput").ap()
    wvt = nc.dram_tensor("wvt", [D, D], F32, kind="ExternalInput").ap()
    extw = nc.dram_tensor("extw", [NH, 128, EW], F32, kind="ExternalInput").ap()
    chs = nc.dram_tensor("chs", [1, 16], F32, kind="ExternalInput").ap()
    gwt = nc.dram_tensor("gwt", [D, 16], F32, kind="ExternalInput").ap()
    w1t = nc.dram_tensor("w1t", [NE, D, DFF], F32, kind="ExternalInput").ap()
    w3t = nc.dram_tensor("w3t", [NE, D, DFF], F32, kind="ExternalInput").ap()
    w2t = nc.dram_tensor("w2t", [NE, DFF, D], F32, kind="ExternalInput").ap()
    b1s = nc.dram_tensor("b1s", [NE, 128, 4], F32, kind="ExternalInput").ap()
    b3s = nc.dram_tensor("b3s", [NE, 128, 4], F32, kind="ExternalInput").ap()
    b2a = nc.dram_tensor("b2a", [17, D], F32, kind="ExternalInput").ap()
    ebias = nc.dram_tensor("ebias", [16], F32, kind="ExternalInput").ap()
    gam8 = nc.dram_tensor("gam8", [128, 8], F32, kind="ExternalInput").ap()
    bet8 = nc.dram_tensor("bet8", [128, 8], F32, kind="ExternalInput").ap()

    y = nc.dram_tensor("y", [8, 128, SQ], F32, kind="ExternalOutput").ap()
    taps = {}
    for t_ in DEBUG_TAPS:
        shp, dt = {
            "k": ([8, 128, S], BF16), "atn": ([8, 128, SQ], F32),
            "kdup": ([128, 2 * S], BF16), "qdup": ([128, 2 * SQ], BF16),
            "aow": ([128, 8 * 512], F32), "gdbg": ([128, 36], F32),
            "extb": ([128, 2 * EW], BF16),
            "x2": ([128, 8 * SQ], F32), "lnx2": ([128, 8 * SQ], BF16),
            "gate": ([17, SQ], BF16), "logits": ([16, SQ], F32),
            "q": ([128, 8 * SQ], BF16), "v": ([128, 32 * 1040 + 64], BF16),
        }[t_]
        taps[t_] = nc.dram_tensor("tap_" + t_, shp, dt, kind="ExternalOutput").ap()

    kscr = nc.dram_tensor("kscr", [8, 128, S], BF16, kind="Internal").ap()
    ascr = nc.dram_tensor("ascr", [8, 128, SQ], F32, kind="Internal").ap()

    ctx = ExitStack()
    e0 = ctx.enter_context

    ones1f = e0(nc.sbuf_tensor("ones1f", [1, 128], F32))
    ones128f = e0(nc.sbuf_tensor("ones128f", [128, 1], F32))
    ones128b = e0(nc.sbuf_tensor("ones128b", [128, 1], BF16))
    identf = e0(nc.sbuf_tensor("identf", [128, 128], F32))
    iota16 = e0(nc.sbuf_tensor("iota16", [128, 16], F32))
    gam8s = e0(nc.sbuf_tensor("gam8s", [128, 8], F32))
    bet8s = e0(nc.sbuf_tensor("bet8s", [128, 8], F32))
    eb_s = e0(nc.sbuf_tensor("eb_s", [16, 1], F32))
    gwt_s = e0(nc.sbuf_tensor("gwt_s", [128, 128], BF16))
    epsc = e0(nc.sbuf_tensor("epsc", [1, 1], F32))
    b1s_s = e0(nc.sbuf_tensor("b1s_s", [128, NE * 4], F32))
    b3s_s = e0(nc.sbuf_tensor("b3s_s", [128, NE * 4], F32))

    sems = {}
    for name in ["ldi", "ldig", "gpi", "vvi", "dve_a", "act_a", "pe_a",
                 "pe_sc", "act_x", "pe_av", "pe_bc", "dve_ep", "act_ep",
                 "dve_b", "act_b", "pe_b", "pool_b",
                 "dve_c", "act_c", "pe_c", "sty",
                 "lda0", "lda1", "lda2", "kst0", "kst1", "ldh0", "ldh1",
                 "ldx0", "ldx1", "ast0", "ast1", "ldb0", "ldb1", "ldc0", "ldc1",
                 "gbd0", "gbd1"]:
        sems[name] = Sem(ctx, nc, name)
    ldi = sems["ldi"]
    ldig = sems["ldig"]
    gpi, vvi = sems["gpi"], sems["vvi"]
    dve_a, act_a, pe_a = sems["dve_a"], sems["act_a"], sems["pe_a"]
    pe_sc, act_x, pe_av = sems["pe_sc"], sems["act_x"], sems["pe_av"]
    pe_bc, dve_ep, act_ep = sems["pe_bc"], sems["dve_ep"], sems["act_ep"]
    dve_b, act_b, pe_b = sems["dve_b"], sems["act_b"], sems["pe_b"]
    pool_b = sems["pool_b"]
    dve_c, act_c, pe_c, sty = sems["dve_c"], sems["act_c"], sems["pe_c"], sems["sty"]
    lda = [sems["lda0"], sems["lda1"], sems["lda2"]]
    kst = [sems["kst0"], sems["kst1"]]
    ldh = [sems["ldh0"], sems["ldh1"]]
    ldx = [sems["ldx0"], sems["ldx1"]]
    ast = [sems["ast0"], sems["ast1"]]
    ldb2 = [sems["ldb0"], sems["ldb1"]]
    ldc = [sems["ldc0"], sems["ldc1"]]
    gbd = [sems["gbd0"], sems["gbd1"]]

    ctxAB = ExitStack()
    eAB = ctxAB.enter_context
    v_sb = eAB(nc.sbuf_tensor("v_sb", [128, 32 * 1040 + 64], BF16))
    q_sb = eAB(nc.sbuf_tensor("q_sb", [128, 8 * SQ], BF16))

    ctxA = ExitStack()
    eA = ctxA.enter_context
    wq_s = eA(nc.sbuf_tensor("wq_s", [128, 8 * D], BF16))
    wk_s = eA(nc.sbuf_tensor("wk_s", [128, 8 * D], BF16))
    wv_s = eA(nc.sbuf_tensor("wv_s", [128, 8 * D], BF16))
    xw = eA(nc.sbuf_tensor("xw", [128, 3 * 8 * WA], F32))
    lnw = eA(nc.sbuf_tensor("lnw", [128, 2 * 8 * WA], BF16))
    sqw = eA(nc.sbuf_tensor("sqw", [128, 8 * WA], BF16))
    kstg = eA(nc.sbuf_tensor("kstg", [128, 2 * 8 * WA], BF16))
    murow = eA(nc.sbuf_tensor("murow", [1, WA], F32))
    rbA = eA(nc.sbuf_tensor("rbA", [128, 2 * WA], F32))
    mbA = eA(nc.sbuf_tensor("mbA", [128, 2 * WA], F32))
    musq = eA(nc.sbuf_tensor("musq", [1, WA], F32))
    varrow = eA(nc.sbuf_tensor("varrow", [1, WA], F32))
    rstdrow = eA(nc.sbuf_tensor("rstdrow", [1, 2 * WA], F32))
    negmurow = eA(nc.sbuf_tensor("negmurow", [1, 2 * WA], F32))

    psA = ExitStack()
    eP = psA.enter_context
    sx_ps = eP(nc.psum_tensor("sx_ps", [1, WA], F32))
    sq_ps = eP(nc.psum_tensor("sq_ps", [1, WA], F32))
    kp_ps = [eP(nc.psum_tensor("kp_ps%d" % i, [128, WA], F32)) for i in range(3)]
    vp_ps = [eP(nc.psum_tensor("vp_ps%d" % i, [128, 512], F32)) for i in range(2)]

    with nc.Block("init") as blk:
        @blk.sync
        def _(sync):
            sync.dma_start(out=gam8s[:, :], in_=gam8).then_inc(ldi.h, 16); ldi.inc(16)
            sync.dma_start(out=bet8s[:, :], in_=bet8).then_inc(ldi.h, 16); ldi.inc(16)
            sync.dma_start(out=eb_s[:, 0:1], in_=ebias[:, None]).then_inc(ldi.h, 16); ldi.inc(16)
            sync.dma_start(out=b1s_s[:, :].rearrange("p (e c) -> p e c", e=NE),
                           in_=b1s.rearrange("e p c -> p e c")).then_inc(ldi.h, 16); ldi.inc(16)
            sync.dma_start(out=b3s_s[:, :].rearrange("p (e c) -> p e c", e=NE),
                           in_=b3s.rearrange("e p c -> p e c")).then_inc(ldi.h, 16); ldi.inc(16)

        @blk.gpsimd
        def _(gp):
            gp.dma_start(out=wq_s[:, :].rearrange("p (dc e) -> p dc e", dc=8),
                         in_=wqt.rearrange("(dc p) e -> p dc e", p=128)).then_inc(ldig.h, 16); ldig.inc(16)
            gp.dma_start(out=wk_s[:, :].rearrange("p (dc e) -> p dc e", dc=8),
                         in_=wkt.rearrange("(dc p) e -> p dc e", p=128)).then_inc(ldig.h, 16); ldig.inc(16)
            gp.dma_start(out=wv_s[:, :].rearrange("p (dc e) -> p dc e", dc=8),
                         in_=wvt.rearrange("(dc p) e -> p dc e", p=128)).then_inc(ldig.h, 16); ldig.inc(16)
            gp.dma_start(out=gwt_s[:, :].rearrange("p (dc e) -> p dc e", dc=8),
                         in_=gwt.rearrange("(dc p) e -> p dc e", p=128)).then_inc(ldig.h, 16); ldig.inc(16)
            gp.memset(ones1f[0:1, :], 1.0)
            gp.memset(epsc[0:1, 0:1], EPS)
            gp.memset(ones128f[:, 0:1], 1.0)
            gp.memset(ones128b[:, 0:1], 1.0)
            gp.memset(identf[:, :], 0.0).then_inc(gpi.h, 1)
            gp.wait_ge(gpi.h, 1)
            gp.affine_select(out=identf[:, :], in_=identf[:, :], compare_op=ALU.not_equal,
                             fill=1.0, base=0, pattern=[[-1, 128]], channel_multiplier=1)
            gp.iota(iota16[:, :], [[1, 16]], channel_multiplier=0,
                    allow_small_or_imprecise_dtypes=True).then_inc(gpi.h, 1)

        @blk.vector
        def _(v):
            ins = None
            for tc in range(32):
                ins = v.memset(bass.AP(v_sb, tc * 1040 + 64, [[32 * 1040 + 64, 128], [65, 16], [1, 1]]), 1.0)
            ins.then_inc(vvi.h, 1)
            for _i in range(CACHE_SALT):
                v.engine_nop()

    gpi.inc(2)
    vvi.inc()
    LDI_FINAL = ldi.n
    LDIG_FINAL = ldig.n

    pl = Plan()
    ck_apply = {}
    ck_stats = {}
    ck_qkv = {}
    ck_var = {}
    ck_bcast = {}
    ck_kevac = {}
    ck_kdma = {}
    ck_evac_kq = {}
    ck_vevac = {}

    for eng in ("t", "v", "s", "y"):
        pl.wait(eng, ldi, LDI_FINAL)
        pl.wait(eng, ldig, LDIG_FINAL)
        pl.wait(eng, gpi, 2)
        pl.wait(eng, vvi, 1)

    NW = NWKV + NWQ

    def st_dma(w):
        isq = w >= NWKV
        src_ = xq if isq else xt
        col0 = ((w - NWKV) if isq else w) * WA
        ncols = SQ if isq else S
        xbuf = (w % 3) * 8 * WA
        pl.wait("y", dve_a, ck_apply.get(w - 3, 0))

        def f(sy, src_=src_, col0=col0, ncols=ncols, xbuf=xbuf):
            sy.dma_start(out=bass.AP(xw, xbuf, [[3 * 8 * WA, 128], [WA, 8], [1, WA]]),
                         in_=bass.AP(src_.tensor, col0, [[ncols, 128], [128 * ncols, 8], [1, WA]])
                         ).then_inc(lda[w % 3].h, 16)
        pl.em("y", f); lda[w % 3].inc(16)
        ck_lda[w] = (lda[w % 3], lda[w % 3].n)

    def st_chain(w):
        xbuf = (w % 3) * 8 * WA
        pl.wait("v", *ck_lda[w])
        pl.wait("v", pe_a, ck_stats.get(w - 1, 0))

        def f(v, xbuf=xbuf):
            v.tensor_tensor(sqw[:, :], xw[:, xbuf:xbuf + 8 * WA],
                            xw[:, xbuf:xbuf + 8 * WA], ALU.mult).then_inc(dve_a.h, 1)
        pl.em("v", f); dve_a.inc()
        ck_sq = dve_a.n

        pl.wait("t", *ck_lda[w])
        pl.wait("t", dve_a, ck_sq)
        pl.wait("t", dve_a, ck_var.get(w - 1, 0))

        def f(t, xbuf=xbuf):
            for c in range(8):
                t.matmul(sx_ps[0:1, :], ones128f[:, 0:1], xw[:, xbuf + c * WA: xbuf + (c + 1) * WA],
                         start=(c == 0), stop=(c == 7))
            ins = None
            for c in range(8):
                ins = t.matmul(sq_ps[0:1, :], ones128b[:, 0:1], sqw[:, c * WA:(c + 1) * WA],
                               start=(c == 0), stop=(c == 7))
            ins.then_inc(pe_a.h, 1)
        pl.em("t", f); pe_a.inc()
        ck_stats[w] = pe_a.n

        pl.wait("v", pe_a, ck_stats[w])

        def f(v):
            v.tensor_scalar(murow[0:1, :], sx_ps[0:1, :], 1.0 / D, None, ALU.mult).then_inc(dve_a.h, 1)
        pl.em("v", f); dve_a.inc()
        pl.wait("v", dve_a, dve_a.n)

        def f(v):
            v.tensor_tensor(musq[0:1, :], murow[0:1, :], murow[0:1, :], ALU.mult).then_inc(dve_a.h, 1)
        pl.em("v", f); dve_a.inc()
        pl.wait("v", dve_a, dve_a.n)

        def f(v):
            v.scalar_tensor_tensor(varrow[0:1, :], sq_ps[0:1, :], 1.0 / D, musq[0:1, :],
                                   ALU.mult, ALU.subtract).then_inc(dve_a.h, 1)
        pl.em("v", f); dve_a.inc()
        ck_var[w] = dve_a.n

        pl.wait("s", dve_a, ck_var[w])

        def f(s):
            s.activation(varrow[0:1, :], varrow[0:1, :], AF.Ln, bias=epsc[0:1, 0:1], scale=1.0
                         ).then_inc(act_a.h, 1)
        pl.em("s", f); act_a.inc()
        pl.wait("s", act_a, act_a.n)
        if w - 2 in ck_rowbc:
            pl.wait("s", *ck_rowbc[w - 2])

        def f(s, w=w):
            s.activation(rstdrow[0:1, (w % 2) * WA:(w % 2 + 1) * WA], varrow[0:1, :],
                         AF.Exp, bias=0.0, scale=-0.5).then_inc(act_a.h, 1)
        pl.em("s", f); act_a.inc()
        ck_rstd = act_a.n

        pl.wait("v", act_a, ck_rstd)
        if w - 2 in ck_rowbc:
            pl.wait("v", *ck_rowbc[w - 2])

        def f(v, w=w):
            v.tensor_scalar(negmurow[0:1, (w % 2) * WA:(w % 2 + 1) * WA], murow[0:1, :],
                            -1.0, None, ALU.mult).then_inc(dve_a.h, 1)
        pl.em("v", f); dve_a.inc()
        ck_negmur[w] = dve_a.n

        pl.wait("s", dve_a, ck_negmur[w])
        if w - 2 in ck_apply:
            pl.wait("s", dve_a, ck_apply[w - 2])

        def f(s, w=w):
            s.dma_start(out=rbA[:, (w % 2) * WA:(w % 2 + 1) * WA],
                        in_=bass.AP(rstdrow, (w % 2) * WA, [[2 * WA, 1], [0, 128], [1, WA]])
                        ).then_inc(gbd[w % 2].h, 16)
            s.dma_start(out=mbA[:, (w % 2) * WA:(w % 2 + 1) * WA],
                        in_=bass.AP(negmurow, (w % 2) * WA, [[2 * WA, 1], [0, 128], [1, WA]])
                        ).then_inc(gbd[w % 2].h, 16)
        pl.em("s", f); gbd[w % 2].inc(32)
        ck_rowbc[w] = (gbd[w % 2], gbd[w % 2].n)

    def st_apply(w):
        xbuf = (w % 3) * 8 * WA
        lbuf = (w % 2) * 8 * WA
        pl.wait("v", *ck_rowbc[w])
        pl.wait("v", pe_a, ck_qkv.get(w - 2, 0))

        def f(v, w=w, xbuf=xbuf, lbuf=lbuf):
            lnsl = bass.AP(lnw, lbuf, [[2 * 8 * WA, 128], [WA, 8], [1, WA]])
            v.tensor_tensor(lnsl,
                            bass.AP(xw, xbuf, [[3 * 8 * WA, 128], [WA, 8], [1, WA]]),
                            bass.AP(mbA, (w % 2) * WA, [[2 * WA, 128], [0, 8], [1, WA]]), ALU.add)
            v.tensor_tensor(lnsl, lnsl,
                            bass.AP(rbA, (w % 2) * WA, [[2 * WA, 128], [0, 8], [1, WA]]), ALU.mult
                            ).then_inc(dve_a.h, 1)
        pl.em("v", f); dve_a.inc()
        ck_apply[w] = dve_a.n

    def st_qkv(w):
        isq = w >= NWKV
        wq_i = w - NWKV
        lbuf = (w % 2) * 8 * WA
        pl.wait("t", dve_a, ck_apply[w])
        if isq:
            for ec in range(8):
                par = ec % 3
                pl.wait("t", act_a, ck_evac_kq.get(par, 0))

                def f(t, ec=ec, lbuf=lbuf, par=par):
                    ins = None
                    for dc in range(8):
                        ins = t.matmul(kp_ps[par][:, :], wq_s[:, dc * D + ec * 128: dc * D + ec * 128 + 128],
                                       lnw[:, lbuf + dc * WA: lbuf + (dc + 1) * WA],
                                       start=(dc == 0), stop=(dc == 7))
                    ins.then_inc(pe_a.h, 1)
                pl.em("t", f); pe_a.inc()
                mmn = pe_a.n

                pl.wait("s", pe_a, mmn)

                def f(s, ec=ec, wq_i=wq_i, par=par):
                    s.copy(q_sb[:, ec * SQ + wq_i * WA: ec * SQ + (wq_i + 1) * WA],
                           kp_ps[par][:, :]).then_inc(act_a.h, 1)
                pl.em("s", f); act_a.inc()
                ck_evac_kq[par] = act_a.n
            ck_qkv[w] = pe_a.n
        else:
            kbuf = (w % 2) * 8 * WA
            if w - 2 in ck_kdma:
                pl.wait("s", *ck_kdma[w - 2])
            for ec in range(8):
                par = ec % 3
                pl.wait("t", act_a, ck_evac_kq.get(par, 0))

                def f(t, ec=ec, lbuf=lbuf, par=par):
                    ins = None
                    for dc in range(8):
                        ins = t.matmul(kp_ps[par][:, :], wk_s[:, dc * D + ec * 128: dc * D + ec * 128 + 128],
                                       lnw[:, lbuf + dc * WA: lbuf + (dc + 1) * WA],
                                       start=(dc == 0), stop=(dc == 7))
                    ins.then_inc(pe_a.h, 1)
                pl.em("t", f); pe_a.inc()
                mmn = pe_a.n

                pl.wait("s", pe_a, mmn)

                def f(s, ec=ec, kbuf=kbuf, par=par):
                    s.copy(kstg[:, kbuf + ec * WA: kbuf + (ec + 1) * WA],
                           kp_ps[par][:, :]).then_inc(act_a.h, 1)
                pl.em("s", f); act_a.inc()
                ck_evac_kq[par] = act_a.n
            ck_kevac = act_a.n

            pl.wait("y", act_a, ck_kevac)

            def f(sy, w=w, kbuf=kbuf):
                sy.dma_start(out=bass.AP(kscr.tensor, w * WA, [[S, 128], [128 * S, 8], [1, WA]]),
                             in_=bass.AP(kstg, kbuf, [[2 * 8 * WA, 128], [WA, 8], [1, WA]])
                             ).then_inc(kst[w % 2].h, 16)
            pl.em("y", f); kst[w % 2].inc(16)
            ck_kdma[w] = (kst[w % 2], kst[w % 2].n)

            for tc2 in range(2):
                tile = 2 * w + tc2
                for half in range(2):
                    pl.wait("t", act_a, ck_vevac.get(half, 0))

                    def f(t, lbuf=lbuf, tc2=tc2, half=half):
                        ins = None
                        for dc in range(8):
                            ins = t.matmul(vp_ps[half][:, :],
                                           lnw[:, lbuf + dc * WA + tc2 * 128: lbuf + dc * WA + tc2 * 128 + 128],
                                           wv_s[:, dc * D + half * 512: dc * D + (half + 1) * 512],
                                           start=(dc == 0), stop=(dc == 7))
                        ins.then_inc(pe_a.h, 1)
                    pl.em("t", f); pe_a.inc()
                    mmn = pe_a.n

                    pl.wait("s", pe_a, mmn)

                    def f(s, tile=tile, half=half):
                        s.copy(
                            bass.AP(v_sb, tile * 1040 + half * 8 * 65, [[32 * 1040 + 64, 128], [65, 8], [1, 64]]),
                            bass.AP(vp_ps[half], 0, [[512, 128], [64, 8], [1, 64]])
                        ).then_inc(act_a.h, 1)
                    pl.em("s", f); act_a.inc()
                    ck_vevac[half] = act_a.n
            ck_qkv[w] = pe_a.n

    ck_lda = {}
    ck_negmur = {}
    ck_rowbc = {}
    for it in range(NW + 3):
        if it < NW:
            st_dma(it)
        if 2 <= it <= NW + 1:
            st_apply(it - 2)
        if 3 <= it <= NW + 2:
            st_qkv(it - 3)
        if 1 <= it <= NW:
            st_chain(it - 1)

    A_PE_FINAL = pe_a.n
    A_DVE_FINAL = dve_a.n
    A_ACT_FINAL = act_a.n

    with nc.Block("phaseA") as blk:
        @blk.sync
        def _(sync):
            pl.run("y", sync)
            for s_ in lda:
                sync.wait_ge(s_.h, s_.n)
            for s_ in kst:
                sync.wait_ge(s_.h, s_.n)

        @blk.tensor
        def _(t):
            pl.run("t", t)

        @blk.vector
        def _(v):
            pl.run("v", v)

        @blk.scalar
        def _(s):
            pl.run("s", s)

    ctxA.close()
    psA.close()

    ctxB = ExitStack()
    eB = ctxB.enter_context
    kdup = eB(nc.sbuf_tensor("kdup", [128, 2 * S], BF16))
    qdup = eB(nc.sbuf_tensor("qdup", [128, 2 * SQ], BF16))
    extb = eB(nc.sbuf_tensor("extb", [128, 2 * EW], BF16))
    pbuf = eB(nc.sbuf_tensor("pbuf", [128, 4 * 1024], BF16))
    avsb = eB(nc.sbuf_tensor("avsb", [65, 512], F32))
    lrow = eB(nc.sbuf_tensor("lrow", [1, 512], F32))
    lrinv = eB(nc.sbuf_tensor("lrinv", [64, 512], F32))
    atn = eB(nc.sbuf_tensor("atn", [64, 2 * SQ], F32))
    pgs = eB(nc.sbuf_tensor("pgs", [1, 3 * 2048], BF16))
    chs_sb = eB(nc.sbuf_tensor("chs_sb", [1, 16], F32))
    ones512 = eB(nc.sbuf_tensor("ones512", [1, 512], BF16))

    psB = ExitStack()
    eP = psB.enter_context
    sc_ps = [eP(nc.psum_tensor("sc_ps%d" % i, [128, 1024], F32)) for i in range(2)]
    av_ps = [eP(nc.psum_tensor("av_ps%d" % i, [128, 512], F32)) for i in range(2)]
    bv_ps = [eP(nc.psum_tensor("bv_ps%d" % i, [128, 512], F32)) for i in range(2)]

    pl = Plan()
    for eng in ("t", "v", "s", "y", "g"):
        for s_ in kst:
            pl.wait(eng, s_, s_.n)
        pl.wait(eng, dve_a, A_DVE_FINAL)
        pl.wait(eng, act_a, A_ACT_FINAL)
        pl.wait(eng, pe_a, A_PE_FINAL)
        pl.wait(eng, gpi, 2)
        pl.wait(eng, vvi, 1)

    pending_ep = None
    ck_sc_done_h = {}
    ck_av_done_h = {}
    ck_ldh = {}
    exp_of_scbuf = {}
    avbv_of_pbuf = {}
    ck_ep_dve = {}
    ck_ep_act = {}
    ck_ast = {}
    ck_ldx = {}

    def emit_loads(h):
        pl.wait("y", pe_sc, ck_sc_done_h.get(h - 2, 0))
        pl.wait("y", pe_av, ck_av_done_h.get(h - 2, 0))
        kb = (h % 2) * S
        qb = (h % 2) * SQ

        def f(sy, h=h, kb=kb, qb=qb):
            src = kscr.tensor
            sy.dma_start(out=kdup[0:64, kb:kb + S],
                         in_=kscr[h // 2, (h % 2) * 64:(h % 2) * 64 + 64, :]).then_inc(ldh[h % 2].h, 16)
            sy.dma_start(out=kdup[64:128, kb:kb + S],
                         in_=kscr[h // 2, (h % 2) * 64:(h % 2) * 64 + 64, :]).then_inc(ldh[h % 2].h, 16)
            qsrc = q_sb[(h % 2) * 64:(h % 2) * 64 + 64, (h // 2) * SQ:(h // 2 + 1) * SQ]
            sy.dma_start(out=qdup[0:64, qb:qb + SQ], in_=qsrc).then_inc(ldh[h % 2].h, 16)
            sy.dma_start(out=qdup[64:128, qb:qb + SQ], in_=qsrc).then_inc(ldh[h % 2].h, 16)
        pl.em("y", f)
        ldh[h % 2].inc(64)

        pl.wait("g", pe_av, ck_av_done_h.get(h - 2, 0))

        def f(gp, h=h):
            gp.dma_start(out=extb[:, (h % 2) * EW:(h % 2 + 1) * EW], in_=extw[h]).then_inc(ldx[h % 2].h, 16)
        pl.em("g", f)
        ldx[h % 2].inc(16)
        ck_ldh[h] = (ldh[h % 2], ldh[h % 2].n)
        ck_ldx[h] = (ldx[h % 2], ldx[h % 2].n)

    if "q" in DEBUG_TAPS:
        def f(sy):
            sy.dma_start(out=taps["q"], in_=q_sb[:, :]).then_inc(sty.h, 16)
        pl.em("y", f); sty.inc(16)
    if "v" in DEBUG_TAPS:
        def f(sy):
            sy.dma_start(out=taps["v"], in_=v_sb[:, :]).then_inc(sty.h, 16)
        pl.em("y", f); sty.inc(16)

    def f(gp):
        gp.dma_start(out=chs_sb[0:1, :], in_=chs).then_inc(ldx[0].h, 16)
        gp.memset(ones512[0:1, :], 1.0).then_inc(ldx[0].h, 1)
    pl.em("g", f); ldx[0].inc(17)
    for eng in ("t", "v"):
        pl.wait(eng, ldx[0], ldx[0].n)

    pp_evac = {}
    ppc = 0
    for h in range(NH):
        off = min(65 * h, 1040 - 256)
        for j in range(3):
            par = ppc % 2
            pl.wait("t", dve_ep, pp_evac.get(ppc - 2, 0))

            def f(t, j=j, off=off, par=par):
                ins = None
                for c in range(4):
                    kc = 4 * j + c
                    ins = t.matmul(bv_ps[par][0:1, 0:256], ones128b[:, 0:1],
                                   v_sb[:, kc * 1040 + off: kc * 1040 + off + 256],
                                   start=(c == 0), stop=(c == 3))
                ins.then_inc(pe_av.h, 1)
            pl.em("t", f); pe_av.inc()
            pl.wait("v", pe_av, pe_av.n)

            def f(v, h=h, j=j, off=off, par=par):
                src_ = bv_ps[par][0:1, 65 * h - off: 65 * h - off + 128]
                dst = pgs[0:1, j * 2048 + h * 128: j * 2048 + (h + 1) * 128]
                if j == 0:
                    v.tensor_scalar(dst, src_, chs_sb[0:1, h:h + 1], None,
                                    ALU.mult).then_inc(dve_ep.h, 1)
                else:
                    v.scalar_tensor_tensor(dst, src_, chs_sb[0:1, h:h + 1],
                                           pgs[0:1, (j - 1) * 2048 + h * 128: (j - 1) * 2048 + (h + 1) * 128],
                                           ALU.mult, ALU.add).then_inc(dve_ep.h, 1)
            pl.em("v", f); dve_ep.inc()
            pl.wait("v", dve_ep, dve_ep.n)
            pp_evac[ppc] = dve_ep.n
            ppc += 1
    PREPASS_DVE = dve_ep.n

    emit_loads(0)
    emit_loads(1)
    if "kdup" in DEBUG_TAPS:
        pl.wait("y", *ck_ldh[1])

        def f(sy):
            sy.dma_start(out=taps["kdup"], in_=kdup[:, :]).then_inc(sty.h, 16)
            sy.dma_start(out=taps["qdup"], in_=qdup[:, :]).then_inc(sty.h, 16)
            sy.dma_start(out=taps["extb"], in_=extb[:, :]).then_inc(sty.h, 16)
        pl.em("y", f); sty.inc(48)

    units = [(h, qc, s) for h in range(NH) for qc in range(4) for s in range(16)]
    NU = len(units)
    sc_count = {}

    def emit_sc(u):
        h, qc, s = units[u]
        par = s % 2
        kb = (h % 2) * S
        qb = (h % 2) * SQ
        pl.wait("t", *ck_ldh[h])
        pl.wait("t", *ck_ldx[h])
        pl.wait("t", act_x, exp_of_scbuf.get(par, 0))
        kcA, kcB = 2 * s, 2 * s + 1

        def f(t, par=par, kcA=kcA, kcB=kcB, qc=qc, kb=kb, qb=qb):
            t.matmul(sc_ps[par][:, 0:512],
                     kdup[0:64, kb + kcA * 128: kb + kcA * 128 + 128],
                     qdup[0:64, qb + qc * 512: qb + (qc + 1) * 512],
                     start=True, stop=True, tile_position=(0, 0))
            t.matmul(sc_ps[par][:, 512:1024],
                     kdup[64:128, kb + kcB * 128: kb + kcB * 128 + 128],
                     qdup[64:128, qb + qc * 512: qb + (qc + 1) * 512],
                     start=True, stop=True, tile_position=(64, 0)
                     ).then_inc(pe_sc.h, 1)
        pl.em("t", f)
        pe_sc.inc()
        sc_count[u] = pe_sc.n
        if qc == 3 and s == 15:
            ck_sc_done_h[h] = pe_sc.n

    emit_sc(0)
    emit_sc(1)

    for u in range(NU):
        h, qc, s = units[u]
        par = s % 2
        par4 = s % 4
        xb = (h % 2) * EW
        bvp = qc % 2

        pl.wait("s", pe_sc, sc_count[u])
        pl.wait("s", pe_av, avbv_of_pbuf.get(par4, 0))

        def f(sc, par=par, par4=par4):
            sc.activation(pbuf[:, par4 * 1024: par4 * 1024 + 1024],
                          sc_ps[par][:, :], AF.Exp, bias=0.0, scale=1.0 / 32.0
                          ).then_inc(act_x.h, 1)
        pl.em("s", f)
        act_x.inc()
        expn = act_x.n
        exp_of_scbuf[par] = expn

        if u + 2 < NU:
            emit_sc(u + 2)
        if s == 0 and pending_ep is not None:
            pending_ep()
            pending_ep = None

        pl.wait("t", act_x, expn)
        if s == 0:
            prev2key = (h, qc - 2) if qc >= 2 else ((h - 1, 2 + qc) if h else None)
            if prev2key is not None:
                pl.wait("t", dve_ep, ck_ep_dve.get(prev2key, 0))
        kcA, kcB = 2 * s, 2 * s + 1

        if u == 0:
            pl.wait("t", dve_ep, PREPASS_DVE)

        def f(t, s_=s, par4=par4, kcA=kcA, kcB=kcB, qc=qc, xb=xb, h=h, bvp=bvp):
            ins = None
            if s_ == 2 * qc and qc > 0:
                t.matmul(bv_ps[bvp][:, :],
                         pgs[0:1, (qc - 1) * 2048 + h * 128: (qc - 1) * 2048 + (h + 1) * 128],
                         ones512[0:1, :], start=True, stop=False)
            for j, kc in enumerate((kcA, kcB)):
                ins = t.matmul(av_ps[bvp][:, :],
                               v_sb[:, kc * 1040 + 65 * h: kc * 1040 + 65 * h + 128],
                               pbuf[:, par4 * 1024 + j * 512: par4 * 1024 + (j + 1) * 512],
                               start=(s_ == 0 and j == 0), stop=(s_ == 15 and j == 1))
                if s_ >= 2 * qc:
                    ins = t.matmul(bv_ps[bvp][:, :],
                                   v_sb[:, kc * 1040 + 65 * h: kc * 1040 + 65 * h + 128],
                                   extb[:, xb + 3968 - 128 * kc + 512 * qc: xb + 3968 - 128 * kc + 512 * qc + 512],
                                   start=(qc == 0 and s_ == 0 and j == 0), stop=(s_ == 15 and j == 1))
            ins.then_inc(pe_av.h, 1)
        pl.em("t", f)
        pe_av.inc()
        avbv_of_pbuf[par4] = pe_av.n

        if s == 15:
            av_end = pe_av.n
            pl.wait("v", pe_av, av_end)
            pl.wait("v", dve_ep, ck_ep_dve.get((h, qc - 1) if qc else (h - 1, 3), 0))

            def f(v, bvp=bvp):
                v.tensor_copy(avsb[0:64, :], av_ps[bvp][0:64, :])
                v.tensor_copy(lrow[0:1, :], av_ps[bvp][64:65, :]).then_inc(act_ep.h, 1)
            pl.em("v", f); act_ep.inc()
            ck_ep_act[(h, qc)] = act_ep.n

            def mk_bc(ck_copy=ck_ep_act[(h, qc)], hq=(h, qc)):
                def emit_bc():
                    pl.wait("t", act_ep, ck_copy)

                    def f(t, bvp=hq[1] % 2):
                        t.matmul(av_ps[bvp][0:64, :], ones1f[0:1, 0:64], lrow[0:1, :], start=True, stop=True
                                 ).then_inc(pe_bc.h, 1)
                    pl.em("t", f); pe_bc.inc()
                    ck_bcmm = pe_bc.n

                    pl.wait("v", pe_bc, ck_bcmm)
                    if hq[1] == 0 and hq[0] - 2 in ck_ast:
                        pl.wait("v", *ck_ast[hq[0] - 2])

                    def f(v, bvp=hq[1] % 2):
                        v.reciprocal(lrinv[:, :], av_ps[bvp][0:64, :]).then_inc(dve_ep.h, 1)
                    pl.em("v", f); dve_ep.inc()
                    pl.wait("v", dve_ep, dve_ep.n)

                    def f(v):
                        v.tensor_tensor(avsb[0:64, :], avsb[0:64, :], lrinv[:, :], ALU.mult).then_inc(dve_ep.h, 1)
                    pl.em("v", f); dve_ep.inc()
                    pl.wait("v", dve_ep, dve_ep.n)

                    def f(v, ab=(hq[0] % 2) * SQ, qc=hq[1], bvp=hq[1] % 2):
                        v.tensor_tensor(atn[:, ab + qc * 512: ab + (qc + 1) * 512],
                                        avsb[0:64, :], bv_ps[bvp][0:64, :], ALU.add).then_inc(dve_ep.h, 1)
                    pl.em("v", f); dve_ep.inc()
                    ck_ep_dve[hq] = dve_ep.n
                    if hq[1] == 3:
                        pl.wait("y", dve_ep, ck_ep_dve[hq])

                        def f(sy, h2=hq[0]):
                            sy.dma_start(out=ascr[h2 // 2, (h2 % 2) * 64:(h2 % 2) * 64 + 64, :],
                                         in_=atn[:, (h2 % 2) * SQ:(h2 % 2) * SQ + SQ]).then_inc(ast[h2 % 2].h, 16)
                        pl.em("y", f); ast[hq[0] % 2].inc(16)
                        ck_ast[hq[0]] = (ast[hq[0] % 2], ast[hq[0] % 2].n)
                return emit_bc
            pending_ep = mk_bc()

            if qc == 3:
                ck_av_done_h[h] = pe_av.n
                if h == NH - 1:
                    pending_ep()
                    pending_ep = None
                if h + 2 < NH:
                    emit_loads(h + 2)



    with nc.Block("phaseB") as blk:
        @blk.sync
        def _(sync):
            pl.run("y", sync)
            for s_ in ast + ldh + ldx:
                sync.wait_ge(s_.h, s_.n)

        @blk.gpsimd
        def _(gp):
            pl.run("g", gp)

        @blk.tensor
        def _(t):
            pl.run("t", t)

        @blk.vector
        def _(v):
            pl.run("v", v)

        @blk.scalar
        def _(s):
            pl.run("s", s)

    psB.close()
    ctxB.close()
    ctxAB.close()

    ctxC = ExitStack()
    eC = ctxC.enter_context
    x2 = eC(nc.sbuf_tensor("x2", [128, 8 * SQ], F32))
    b2a_s = eC(nc.sbuf_tensor("b2a_s", [17, D], BF16))
    gaugT = eC(nc.sbuf_tensor("gaugT", [17, SQ], BF16))
    sigall_sb = eC(nc.sbuf_tensor("sigall_sb", [16, 512], BF16))
    identb16 = eC(nc.sbuf_tensor("identb16", [16, 16], BF16))
    logit_sb = eC(nc.sbuf_tensor("logit_sb", [16, 512], F32))
    lnx2 = eC(nc.sbuf_tensor("lnx2", [128, 8 * SQ], BF16))
    w1_s = eC(nc.sbuf_tensor("w1_s", [128, 2 * 8 * DFF], BF16))
    w3_s = eC(nc.sbuf_tensor("w3_s", [128, 2 * 8 * DFF], BF16))
    w2_s = eC(nc.sbuf_tensor("w2_s", [128, 2 * 4 * D], BF16))
    h_s = eC(nc.sbuf_tensor("h_s", [128, 2 * 4 * 512], BF16))
    silu_s = eC(nc.sbuf_tensor("silu_s", [128, 2 * 512], F32))
    gb_sb = eC(nc.sbuf_tensor("gb_sb", [128, 2 * 512], BF16))
    aow = eC(nc.sbuf_tensor("aow", [128, 2 * 8 * 256], F32))
    sq2b = eC(nc.sbuf_tensor("sq2b", [128, 2 * 8 * 256], BF16))
    tmp2 = eC(nc.sbuf_tensor("tmp2", [128, 256], F32))
    tmp2g = eC(nc.sbuf_tensor("tmp2g", [128, 256], F32))
    negmurow2 = eC(nc.sbuf_tensor("negmurow2", [1, 2 * 256], F32))
    varrow2 = eC(nc.sbuf_tensor("varrow2", [1, 2 * 256], F32))
    musq2 = eC(nc.sbuf_tensor("musq2", [1, 256], F32))
    sqrow2 = eC(nc.sbuf_tensor("sqrow2", [1, 2 * 256], F32))
    mbc_sb = eC(nc.sbuf_tensor("mbc_sb", [128, 2 * 256], F32))
    rbc_sb = eC(nc.sbuf_tensor("rbc_sb", [128, 2 * 256], F32))
    biasc_g = eC(nc.sbuf_tensor("biasc_g", [16, 128], F32))
    m8_g = eC(nc.sbuf_tensor("m8_g", [128, 8], F32))
    i8_g = eC(nc.sbuf_tensor("i8_g", [128, 8], U32))
    idxf_g = eC(nc.sbuf_tensor("idxf_g", [128, 2], F32))
    mask1_g = eC(nc.sbuf_tensor("mask1_g", [128, 16], F32))
    mask2_g = eC(nc.sbuf_tensor("mask2_g", [128, 16], F32))
    g_g = eC(nc.sbuf_tensor("g_g", [128, 17], F32))

    psC = ExitStack()
    eP = psC.enter_context
    a_ps = [eP(nc.psum_tensor("a_ps%d" % i, [128, 512], F32)) for i in range(2)]
    b_ps = [eP(nc.psum_tensor("b_ps%d" % i, [128, 512], F32)) for i in range(2)]
    y_ps = [eP(nc.psum_tensor("y_ps%d" % i, [128, 512], F32)) for i in range(2)]
    gate_ps = eP(nc.psum_tensor("gate_ps", [128, 512], F32))
    st_ps = eP(nc.psum_tensor("st_ps", [1, 512], F32))
    lg_ps = gate_ps[0:16, 0:512]
    trb_ps = gate_ps[:, 0:16]
    trs_ps = gate_ps[:, 256:264].bitcast(BF16)
    gtr_ps = gate_ps[0:17, 128:256]
    wsum_g = idxf_g[:, 0:1]
    winv_g = idxf_g[:, 1:2]

    pl = Plan()
    for eng in ("t", "v", "s", "y", "g"):
        for s_ in ast:
            pl.wait(eng, s_, s_.n)
        pl.wait(eng, gpi, 2)
        pl.wait(eng, vvi, 1)
        pl.wait(eng, ldi, LDI_FINAL)

    def f(gp):
        gp.dma_start(out=b2a_s[:, :], in_=b2a).then_inc(ldc[0].h, 16)
    pl.em("g", f); ldc[0].inc(16)

    def f(v):
        v.tensor_copy(identb16[:, :], identf[0:16, 0:16])
        v.memset(g_g[:, 16:17], 1.0).then_inc(dve_b.h, 1)
    pl.em("v", f); dve_b.inc()

    ck_x2ld = {}
    for tc in range(4):
        def f(sy, tc=tc):
            sy.dma_start(out=bass.AP(x2, tc * 512, [[8 * SQ, 128], [SQ, 8], [1, 512]]),
                         in_=bass.AP(xq.tensor, tc * 512, [[SQ, 128], [128 * SQ, 8], [1, 512]])
                         ).then_inc(ldb2[tc % 2].h, 16)
        pl.em("y", f); ldb2[tc % 2].inc(16)
        ck_x2ld[tc] = (ldb2[tc % 2], ldb2[tc % 2].n)

    def emit_wload(e_, pe_gate):
        buf = e_ % 2
        pl.wait("g", pe_c, pe_gate)

        def f(gp, e_=e_, buf=buf):
            gp.dma_start(out=bass.AP(w1_s, buf * 8 * DFF, [[2 * 8 * DFF, 128], [DFF, 8], [1, DFF]]),
                         in_=w1t[e_].rearrange("(dc p) f -> p dc f", p=128)).then_inc(ldc[buf].h, 16)
            gp.dma_start(out=bass.AP(w3_s, buf * 8 * DFF, [[2 * 8 * DFF, 128], [DFF, 8], [1, DFF]]),
                         in_=w3t[e_].rearrange("(dc p) f -> p dc f", p=128)).then_inc(ldc[buf].h, 16)
            gp.dma_start(out=bass.AP(w2_s, buf * 4 * D, [[2 * 4 * D, 128], [D, 4], [1, D]]),
                         in_=w2t[e_].rearrange("(fc p) d -> p fc d", p=128)).then_inc(ldc[buf].h, 16)
        pl.em("g", f)
        ldc[buf].inc(48)
        return (ldc[buf], ldc[buf].n)

    ESEQ = [16, 17] + list(range(16))
    ck_wld = {16: emit_wload(16, 0), 17: emit_wload(17, 0)}

    uctr = [0]
    ck_sq_free = {}
    ck_rows_read = {}
    ck_row_buf = {}
    ck_mrb_buf = {}
    ck_ap_p0 = {}
    ck_lnx2 = {}
    ck_aow_free = {}
    ck_aowld = {}
    ck_sqd = {}
    ck_sig = {}
    ck_gaug = {}
    ck_gate_done = {}
    ck_lnx2_half = {}
    gstate = {"trb_read": 0, "chunk_done": 0, "g2": 0}

    def emit_gate_chunk(tc, j, ck_lgev):
        pl.wait("v", dve_b, ck_lgev)
        pl.wait("v", pe_b, gstate["trb_read"])

        def f(v, tc=tc, j=j):
            v.tensor_scalar(biasc_g[0:16, :], logit_sb[0:16, j * 128:(j + 1) * 128],
                            eb_s[0:16, 0:1], None, ALU.add).then_inc(dve_b.h, 1)
        pl.em("v", f); dve_b.inc()
        ck_bfm = dve_b.n

        pl.wait("t", dve_b, ck_bfm)
        pl.wait("t", act_b, ck_sig[tc])
        pl.wait("t", dve_b, gstate["chunk_done"])

        def f(t, j=j):
            t.transpose(trb_ps, biasc_g[0:16, :], identf[0:16, 0:16])
            t.transpose(trs_ps, sigall_sb[0:16, j * 128:(j + 1) * 128], identb16[:, :]
                        ).then_inc(pe_b.h, 1)
        pl.em("t", f); pe_b.inc()
        ck_trn = pe_b.n
        gstate["trb_read"] = ck_trn

        pl.wait("v", pe_b, ck_trn)

        def f(v):
            v.max(m8_g[:, :], trb_ps).then_inc(dve_b.h, 1)
        pl.em("v", f); dve_b.inc()
        pl.wait("v", dve_b, dve_b.n)

        def f(v):
            v.max_index(i8_g[:, :], m8_g[:, :], trb_ps).then_inc(dve_b.h, 1)
        pl.em("v", f); dve_b.inc()
        pl.wait("v", dve_b, dve_b.n)

        gate_steps = [
            lambda v: v.tensor_copy(idxf_g[:, :], i8_g[:, 0:2]),
            lambda v: v.tensor_tensor(mask1_g[:, :], iota16[:, :],
                                      idxf_g[:, 0:1].broadcast_to([128, 16]), ALU.is_equal),
            lambda v: v.tensor_tensor(mask2_g[:, :], iota16[:, :],
                                      idxf_g[:, 1:2].broadcast_to([128, 16]), ALU.is_equal),
            lambda v: v.tensor_tensor(mask1_g[:, :], mask1_g[:, :], mask2_g[:, :], ALU.add),
            lambda v: v.tensor_tensor(mask2_g[:, :], trs_ps, mask1_g[:, :], ALU.mult),
            lambda v: v.reduce_sum(wsum_g, mask2_g[:, :], axis=mybir.AxisListType.X),
            lambda v: v.tensor_scalar(wsum_g, wsum_g, 1e-10, None, ALU.add),
            lambda v: v.reciprocal(winv_g, wsum_g),
        ]
        for step in gate_steps:
            def f(v, step=step):
                step(v).then_inc(dve_b.h, 1)
            pl.em("v", f); dve_b.inc()
            pl.wait("v", dve_b, dve_b.n)

        pl.wait("v", pe_b, gstate["g2"])

        def f(v):
            v.tensor_tensor(g_g[:, 0:16], mask2_g[:, :],
                            winv_g.broadcast_to([128, 16]), ALU.mult).then_inc(dve_b.h, 1)
        pl.em("v", f); dve_b.inc()
        ck_g = dve_b.n

        pl.wait("t", dve_b, ck_g)

        def f(t):
            t.transpose(gtr_ps, g_g[:, 0:17], identf[:, :]).then_inc(pe_b.h, 1)
        pl.em("t", f); pe_b.inc()
        ck_g2 = pe_b.n
        gstate["g2"] = ck_g2

        pl.wait("v", pe_b, ck_g2)

        def f(v, tc=tc, j=j):
            v.tensor_copy(gaugT[:, tc * 512 + j * 128: tc * 512 + (j + 1) * 128], gtr_ps
                          ).then_inc(dve_b.h, 1)
        pl.em("v", f); dve_b.inc()
        gstate["chunk_done"] = dve_b.n

    def emit_b2_sq(tc, p):
        for h2 in range(2):
            u = 4 * tc + p * 2 + h2
            slot = u % 2
            T0 = tc * 512 + h2 * 256
            pl.wait("g", pe_b, ck_sq_free.get(slot, 0))
            if p == 0:
                pl.wait("g", *ck_aowld[(tc, h2)])
            else:
                pl.wait("g", *ck_x2ld[tc])
                pl.wait("g", dve_b, ck_ap_p0[tc][0])
                pl.wait("g", pool_b, ck_ap_p0[tc][1])

            def f(gp, p=p, h2=h2, T0=T0, slot=slot):
                if p == 0:
                    src = bass.AP(aow, h2 * 2048, [[2 * 2048, 128], [256, 8], [1, 256]])
                else:
                    src = bass.AP(x2, T0, [[8 * SQ, 128], [SQ, 8], [1, 256]])
                dst1 = bass.AP(sq2b, slot * 2048, [[2 * 2048, 128], [256, 8], [1, 256]])
                gp.tensor_tensor(dst1, src, src, ALU.mult).then_inc(pool_b.h, 1)
            pl.em("g", f); pool_b.inc()
            ck_sqd[u] = pool_b.n

    def emit_b2_loads(tc):
        for h2 in range(2):
            if (tc - 1, h2) in ck_aow_free:
                for sv in ck_aow_free[(tc - 1, h2)]:
                    pl.wait("y", *sv)

            def f(sy, tc=tc, h2=h2):
                sy.dma_start(out=bass.AP(aow, h2 * 2048, [[2 * 2048, 128], [256, 8], [1, 256]]),
                             in_=bass.AP(ascr.tensor, tc * 512 + h2 * 256,
                                         [[SQ, 128], [128 * SQ, 8], [1, 256]])
                             ).then_inc(ldb2[h2].h, 16)
            pl.em("y", f); ldb2[h2].inc(16)
            ck_aowld[(tc, h2)] = (ldb2[h2], ldb2[h2].n)
        emit_b2_sq(tc, 0)

    def emit_b2_half(tc, p, h2):
        u = 4 * tc + p * 2 + h2
        rpar = u % 2
        slot = u % 2
        T0 = tc * 512 + h2 * 256

        pl.wait("t", pool_b, ck_sqd[u])
        if p == 0:
            pl.wait("t", *ck_aowld[(tc, h2)])
        else:
            pl.wait("t", *ck_x2ld[tc])
            pl.wait("t", dve_b, ck_ap_p0[tc][0])
            pl.wait("t", pool_b, ck_ap_p0[tc][1])
        if u - 1 in ck_rows_read:
            pl.wait("t", act_b, ck_rows_read[u - 1][0])

        def f(t, p=p, h2=h2, T0=T0, slot=slot):
            for c in range(8):
                if p == 0:
                    xsrc = aow[:, h2 * 2048 + c * 256: h2 * 2048 + (c + 1) * 256]
                else:
                    xsrc = x2[:, c * SQ + T0: c * SQ + T0 + 256]
                t.matmul(st_ps[0:1, 0:256], ones128f[:, 0:1], xsrc, start=(c == 0), stop=(c == 7))
            ins = None
            for c in range(8):
                ins = t.matmul(st_ps[0:1, 256:512], ones128b[:, 0:1],
                               sq2b[:, slot * 2048 + c * 256: slot * 2048 + (c + 1) * 256],
                               start=(c == 0), stop=(c == 7))
            ins.then_inc(pe_b.h, 1)
        pl.em("t", f); pe_b.inc()
        ck_stats_u = pe_b.n
        ck_sq_free[slot] = pe_b.n

        pl.wait("s", pe_b, ck_stats_u)
        if u - 2 in ck_row_buf:
            pl.wait("s", *ck_row_buf[u - 2])

        def f(s, rpar=rpar):
            s.mul(negmurow2[0:1, rpar * 256:(rpar + 1) * 256], st_ps[0:1, 0:256], -1.0 / D)
            s.mul(sqrow2[0:1, rpar * 256:(rpar + 1) * 256], st_ps[0:1, 256:512], 1.0 / D
                  ).then_inc(act_b.h, 1)
        pl.em("s", f); act_b.inc()
        ck_negmu = act_b.n

        pl.wait("s", act_b, ck_negmu)
        if u - 2 in ck_mrb_buf:
            for sv in ck_mrb_buf[u - 2]:
                pl.wait("s", *sv)

        def f(s, rpar=rpar):
            s.dma_start(out=mbc_sb[:, rpar * 256:(rpar + 1) * 256],
                        in_=bass.AP(negmurow2, rpar * 256, [[512, 1], [0, 128], [1, 256]])
                        ).then_inc(gbd[rpar].h, 16)
        pl.em("s", f); gbd[rpar].inc(16)

        pl.wait("v", act_b, ck_negmu)

        def f(v, rpar=rpar):
            v.tensor_tensor(musq2[0:1, :], negmurow2[0:1, rpar * 256:(rpar + 1) * 256],
                            negmurow2[0:1, rpar * 256:(rpar + 1) * 256], ALU.mult)
            v.tensor_tensor(varrow2[0:1, rpar * 256:(rpar + 1) * 256],
                            sqrow2[0:1, rpar * 256:(rpar + 1) * 256],
                            musq2[0:1, :], ALU.subtract).then_inc(dve_b.h, 1)
        pl.em("v", f); dve_b.inc()
        ck_var = dve_b.n
        ck_rows_read[u] = (ck_negmu, ck_var)

        pl.wait("s", dve_b, ck_var)

        def f(s, rpar=rpar):
            sl = varrow2[0:1, rpar * 256:(rpar + 1) * 256]
            s.activation(sl, sl, AF.Ln, bias=epsc[0:1, 0:1], scale=1.0)
            s.activation(sl, sl, AF.Exp, bias=0.0, scale=-0.5).then_inc(act_b.h, 1)
        pl.em("s", f); act_b.inc()
        ck_rstd = act_b.n

        pl.wait("s", act_b, ck_rstd)

        def f(s, rpar=rpar):
            s.dma_start(out=rbc_sb[:, rpar * 256:(rpar + 1) * 256],
                        in_=bass.AP(varrow2, rpar * 256, [[512, 1], [0, 128], [1, 256]])
                        ).then_inc(gbd[rpar].h, 16)
        pl.em("s", f); gbd[rpar].inc(16)
        ck_row_buf[u] = (gbd[rpar], gbd[rpar].n)

        esplit = ((("g", tmp2g, tuple(range(8))),) if p == 1 else
                  (("v", tmp2, tuple(range(8))),))
        for eng, tmpb, csel in esplit:
            pl.wait(eng, *ck_row_buf[u])
            pl.wait(eng, *ck_x2ld[tc])

            if p == 0:
                pl.wait(eng, *ck_aowld[(tc, h2)])

            def f(e, p=p, h2=h2, T0=T0, rpar=rpar, tmpb=tmpb, csel=csel):
                mb = mbc_sb[:, rpar * 256:(rpar + 1) * 256]
                rb = rbc_sb[:, rpar * 256:(rpar + 1) * 256]
                ins = None
                for c in csel:
                    x2sl = x2[:, c * SQ + T0: c * SQ + T0 + 256]
                    if p == 0:
                        aowsl = aow[:, h2 * 2048 + c * 256: h2 * 2048 + (c + 1) * 256]
                        e.tensor_tensor(tmpb[:, :], aowsl, mb, ALU.add)
                        e.tensor_tensor(tmpb[:, :], tmpb[:, :], rb, ALU.mult)
                        e.tensor_scalar(tmpb[:, :], tmpb[:, :], gam8s[:, c:c + 1], bet8s[:, c:c + 1],
                                        ALU.mult, ALU.add)
                        ins = e.tensor_tensor(x2sl, x2sl, tmpb[:, :], ALU.add)
                    else:
                        e.tensor_tensor(tmpb[:, :], x2sl, mb, ALU.add)
                        ins = e.tensor_tensor(lnx2[:, c * SQ + T0: c * SQ + T0 + 256],
                                              tmpb[:, :], rb, ALU.mult)
                ins.then_inc(dve_b.h if eng == "v" else pool_b.h, 1)
            pl.em(eng, f)
            if eng == "v":
                dve_b.inc()
            else:
                pool_b.inc()
        ck_mrb_buf[u] = ((dve_b, dve_b.n), (pool_b, pool_b.n))
        if p == 0:
            ck_aow_free[(tc, h2)] = ((dve_b, dve_b.n), (pool_b, pool_b.n))
            if h2 == 1:
                ck_ap_p0[tc] = (dve_b.n, pool_b.n)
                emit_b2_sq(tc, 1)
        else:
            ck_lnx2_half[(tc, h2)] = ((dve_b, dve_b.n), (pool_b, pool_b.n))
            if h2 == 1:
                ck_lnx2[tc] = ((dve_b, dve_b.n), (pool_b, pool_b.n))

    def emit_b2_gate(tc):
        for sv in ck_lnx2[tc]:
            pl.wait("t", *sv)
        pl.wait("t", dve_b, ck_gate_done.get(tc - 1, 0))
        pl.wait("t", act_b, ck_sig.get(tc - 1, 0))

        def f(t, tc=tc):
            ins = None
            for c in range(8):
                ins = t.matmul(lg_ps, gwt_s[:, c * 16:(c + 1) * 16],
                               lnx2[:, c * SQ + tc * 512: c * SQ + tc * 512 + 512],
                               start=(c == 0), stop=(c == 7))
            ins.then_inc(pe_b.h, 1)
        pl.em("t", f); pe_b.inc()
        ck_lg = pe_b.n

        pl.wait("v", pe_b, ck_lg)

        def f(v):
            v.tensor_copy(logit_sb[:, :], lg_ps).then_inc(dve_b.h, 1)
        pl.em("v", f); dve_b.inc()
        ck_lgev = dve_b.n

        pl.wait("s", pe_b, ck_lg)

        def f(s):
            s.activation(sigall_sb[0:16, :], lg_ps, AF.Sigmoid).then_inc(act_b.h, 1)
        pl.em("s", f); act_b.inc()
        ck_sig[tc] = act_b.n

        for j in range(4):
            emit_gate_chunk(tc, j, ck_lgev)
        ck_gate_done[tc] = gstate["chunk_done"]
        ck_gaug[tc] = gstate["chunk_done"]

    emit_b2_loads(0)
    for p in range(2):
        for h2 in range(2):
            emit_b2_half(0, p, h2)

    ck_wld_c = dict(ck_wld)
    ck_ydone = {}
    ck_silu_par = {}
    ck_silu_cons = {}
    ck_b_par = {}
    ck_huse = {}
    ck_wuse = {}
    ck_gbd = {}
    ck_gb_free = {}

    pending_y = None

    def emit_y(prev):
        e_p, tc_p, hb_p, wb2_p, hbuf_p, h_ready_p = prev
        pl.wait("t", dve_c, h_ready_p)
        for sv in ck_lnx2[tc_p]:
            pl.wait("v", *sv)
        for dc in range(8):
            ypar = dc % 2
            pl.wait("t", dve_c, ck_ydone.get(ypar, 0))

            def f(t, wb2=wb2_p, dc=dc, hb=hb_p):
                ins = None
                for fc in range(4):
                    ins = t.matmul(y_ps[dc % 2][:, :],
                                   w2_s[:, wb2 + fc * D + dc * 128: wb2 + fc * D + dc * 128 + 128],
                                   h_s[:, hb + fc * 512: hb + (fc + 1) * 512],
                                   start=(fc == 0), stop=(fc == 3))
                ins.then_inc(pe_c.h, 1)
            pl.em("t", f); pe_c.inc()
            ymm = pe_c.n
            if dc == 7:
                ck_huse[hbuf_p] = pe_c.n

            pl.wait("v", pe_c, ymm)

            def f(v, dc=dc, tc=tc_p, ypar=ypar):
                v.tensor_tensor(x2[:, dc * SQ + tc * 512: dc * SQ + tc * 512 + 512],
                                x2[:, dc * SQ + tc * 512: dc * SQ + tc * 512 + 512],
                                y_ps[ypar][:, :], ALU.add).then_inc(dve_c.h, 1)
            pl.em("v", f); dve_c.inc()
            ck_ydone[ypar] = dve_c.n
        if tc_p == 3:
            ck_wuse[e_p] = pe_c.n
            pos = ESEQ.index(e_p)
            if pos + 2 < NE:
                ck_wld_c[ESEQ[pos + 2]] = emit_wload(ESEQ[pos + 2], ck_wuse[e_p])

    kctr = [0]

    def emit_c_unit(e_, tc):
        nonlocal pending_y
        buf = e_ % 2
        wb1 = buf * 8 * DFF
        wb2 = buf * 4 * D
        k = kctr[0]; kctr[0] += 1
        hbuf = k % 2
        hb = hbuf * 4 * 512
        pl.wait("t", *ck_wld_c[e_])
        if e_ < 16:
            gbuf = k % 2
            pl.wait("s", dve_b, ck_gaug[tc])
            pl.wait("s", dve_c, ck_gb_free.get(gbuf, 0))

            def f(s, e_=e_, tc=tc, gbuf=gbuf):
                s.dma_start(out=gb_sb[:, gbuf * 512:(gbuf + 1) * 512],
                            in_=bass.AP(gaugT, e_ * SQ + tc * 512, [[SQ, 1], [0, 128], [1, 512]])
                            ).then_inc(gbd[gbuf].h, 16)
            pl.em("s", f); gbd[gbuf].inc(16)
            ck_gbd[k] = (gbd[gbuf], gbd[gbuf].n)

        for sv in ck_lnx2[tc]:
            pl.wait("t", *sv)
        for fc in range(4):
            spar = fc % 2
            pl.wait("t", act_c, ck_silu_par.get(spar, 0))
            pl.wait("t", dve_c, ck_b_par.get(spar, 0))

            def f(t, wb1=wb1, fc=fc, tc=tc, spar=spar):
                for dc in range(8):
                    t.matmul(a_ps[spar][:, :], w1_s[:, wb1 + dc * DFF + fc * 128: wb1 + dc * DFF + fc * 128 + 128],
                             lnx2[:, dc * SQ + tc * 512: dc * SQ + tc * 512 + 512],
                             start=(dc == 0), stop=(dc == 7))
                ins = None
                for dc in range(8):
                    ins = t.matmul(b_ps[spar][:, :], w3_s[:, wb1 + dc * DFF + fc * 128: wb1 + dc * DFF + fc * 128 + 128],
                                   lnx2[:, dc * SQ + tc * 512: dc * SQ + tc * 512 + 512],
                                   start=(dc == 0), stop=(dc == 7))
                ins.then_inc(pe_c.h, 1)
            pl.em("t", f); pe_c.inc()
            ck_ab = pe_c.n

            pl.wait("s", pe_c, ck_ab)
            pl.wait("s", dve_c, ck_silu_cons.get(spar, 0))

            def f(s, e_=e_, fc=fc, spar=spar):
                s.activation(silu_s[:, spar * 512:(spar + 1) * 512], a_ps[spar][:, :], AF.Silu,
                             bias=b1s_s[:, e_ * 4 + fc: e_ * 4 + fc + 1], scale=1.0
                             ).then_inc(act_c.h, 1)
            pl.em("s", f); act_c.inc()
            ck_silu_par[spar] = act_c.n

            pl.wait("v", act_c, ck_silu_par[spar])
            if e_ < 16:
                pl.wait("v", *ck_gbd[k])
            pl.wait("v", pe_c, ck_huse.get(hbuf, 0))

            def f(v, e_=e_, fc=fc, hb=hb, spar=spar, gbuf=k % 2):
                if e_ < 16:
                    hsl = h_s[:, hb + fc * 512: hb + (fc + 1) * 512]
                    v.scalar_tensor_tensor(hsl, b_ps[spar][:, :], b3s_s[:, e_ * 4 + fc: e_ * 4 + fc + 1],
                                           gb_sb[:, gbuf * 512:(gbuf + 1) * 512], ALU.add, ALU.mult)
                    v.tensor_tensor(hsl, hsl, silu_s[:, spar * 512:(spar + 1) * 512], ALU.mult
                                    ).then_inc(dve_c.h, 1)
                else:
                    v.scalar_tensor_tensor(h_s[:, hb + fc * 512: hb + (fc + 1) * 512],
                                           b_ps[spar][:, :], b3s_s[:, e_ * 4 + fc: e_ * 4 + fc + 1],
                                           silu_s[:, spar * 512:(spar + 1) * 512], ALU.add, ALU.mult
                                           ).then_inc(dve_c.h, 1)
            pl.em("v", f); dve_c.inc()
            ck_b_par[spar] = dve_c.n
            ck_silu_cons[spar] = dve_c.n
            if e_ < 16 and fc == 3:
                ck_gb_free[k % 2] = dve_c.n
        h_ready = dve_c.n

        if pending_y is not None:
            emit_y(pending_y)
        pending_y = (e_, tc, hb, wb2, hbuf, h_ready)

    for tc in range(3):
        emit_b2_loads(tc + 1)
        emit_c_unit(16, tc)
        emit_b2_half(tc + 1, 0, 0)
        emit_b2_half(tc + 1, 0, 1)
        emit_c_unit(17, tc)
        emit_b2_half(tc + 1, 1, 0)
        emit_b2_half(tc + 1, 1, 1)
    emit_c_unit(16, 3)
    emit_b2_gate(0)
    emit_c_unit(17, 3)
    emit_b2_gate(1)
    emit_c_unit(0, 0)
    emit_b2_gate(2)
    emit_c_unit(0, 1)
    emit_b2_gate(3)
    emit_c_unit(0, 2)
    emit_c_unit(0, 3)
    for e_ in range(1, 16):
        for tc in range(4):
            emit_c_unit(e_, tc)

    emit_y(pending_y)
    pending_y = None

    for tc in range(4):
        pl.wait("t", dve_b, ck_gaug[tc])
        for dc in range(8):
            ypar = dc % 2
            pl.wait("t", dve_c, ck_ydone.get(ypar, 0))

            def f(t, dc=dc, tc=tc, ypar=ypar):
                t.matmul(y_ps[ypar][:, :], b2a_s[0:17, dc * 128:(dc + 1) * 128],
                         gaugT[0:17, tc * 512:(tc + 1) * 512], start=True, stop=True
                         ).then_inc(pe_c.h, 1)
            pl.em("t", f); pe_c.inc()
            mmn = pe_c.n

            pl.wait("v", pe_c, mmn)

            def f(v, dc=dc, tc=tc, ypar=ypar):
                v.tensor_tensor(x2[:, dc * SQ + tc * 512: dc * SQ + tc * 512 + 512],
                                x2[:, dc * SQ + tc * 512: dc * SQ + tc * 512 + 512],
                                y_ps[ypar][:, :], ALU.add).then_inc(dve_c.h, 1)
            pl.em("v", f); dve_c.inc()
            ck_ydone[ypar] = dve_c.n

            pl.wait("y", dve_c, ck_ydone[ypar])

            def f(sy, dc=dc, tc=tc):
                sy.dma_start(out=y[dc][:, tc * 512:(tc + 1) * 512],
                             in_=x2[:, dc * SQ + tc * 512: dc * SQ + tc * 512 + 512]
                             ).then_inc(sty.h, 16)
            pl.em("y", f); sty.inc(16)

    C_DVE_FINAL = dve_c.n

    with nc.Block("phaseBC") as blk:
        @blk.sync
        def _(sync):
            pl.run("y", sync)
            for s_ in ldb2:
                sync.wait_ge(s_.h, s_.n)

        @blk.gpsimd
        def _(gp):
            pl.run("g", gp)

        @blk.tensor
        def _(t):
            pl.run("t", t)

        @blk.vector
        def _(v):
            pl.run("v", v)

        @blk.scalar
        def _(s):
            pl.run("s", s)

    psC.close()

    with nc.Block("phaseD") as blk:
        @blk.sync
        def _(sync):
            sync.wait_ge(dve_c.h, C_DVE_FINAL)
            for s_ in ldc + gbd:
                sync.wait_ge(s_.h, s_.n)
            if "k" in DEBUG_TAPS:
                sync.dma_start(out=taps["k"], in_=kscr).then_inc(sty.h, 16); sty.inc(16)
            if "atn" in DEBUG_TAPS:
                sync.dma_start(out=taps["atn"], in_=ascr).then_inc(sty.h, 16); sty.inc(16)
            if "lnx2" in DEBUG_TAPS:
                sync.dma_start(out=taps["lnx2"], in_=lnx2[:, :]).then_inc(sty.h, 16); sty.inc(16)
            if "gate" in DEBUG_TAPS:
                sync.dma_start(out=taps["gate"], in_=gaugT[:, :]).then_inc(sty.h, 16); sty.inc(16)
            sync.wait_ge(sty.h, sty.n)

    ctxC.close()
    ctx.close()
    return nc


def _stage(inputs):
    f32 = np.float32
    x = np.asarray(inputs["L0_attn_x"], f32)
    Wq = np.asarray(inputs["L0_attn_Wq"], f32)
    Wk = np.asarray(inputs["L0_attn_Wk"], f32)
    Wv = np.asarray(inputs["L0_attn_Wv"], f32)
    tbl = np.asarray(inputs["L0_attn_rel_bias_table"], f32)
    gam = np.asarray(inputs["L0_attn_ln_gamma"], f32)
    bet = np.asarray(inputs["L0_attn_ln_beta"], f32)
    gate_w = np.asarray(inputs["L0_ffn_gate_w"], f32)
    eb = np.asarray(inputs["L0_ffn_expert_bias"], f32)
    w1 = np.asarray(inputs["L0_ffn_w1"], f32)
    b1 = np.asarray(inputs["L0_ffn_b1"], f32)
    w2 = np.asarray(inputs["L0_ffn_w2"], f32)
    b2 = np.asarray(inputs["L0_ffn_b2"], f32)
    w3 = np.asarray(inputs["L0_ffn_w3"], f32)
    b3 = np.asarray(inputs["L0_ffn_b3"], f32)
    sw1 = np.asarray(inputs["L0_ffn_sw1"], f32)
    sb1 = np.asarray(inputs["L0_ffn_sb1"], f32)
    sw2 = np.asarray(inputs["L0_ffn_sw2"], f32)
    sb2 = np.asarray(inputs["L0_ffn_sb2"], f32)
    sw3 = np.asarray(inputs["L0_ffn_sw3"], f32)
    sb3 = np.asarray(inputs["L0_ffn_sb3"], f32)

    shared = {}
    shared["wqt"] = np.ascontiguousarray(Wq.T)
    shared["wkt"] = np.ascontiguousarray(Wk.T)
    shared["wvt"] = np.ascontiguousarray(Wv.T)
    shared["gwt"] = np.ascontiguousarray(gate_w.T)
    w1a = np.concatenate([w1, sw1.reshape(2, 512, D)], 0)
    w3a = np.concatenate([w3, sw3.reshape(2, 512, D)], 0)
    w2a = np.concatenate([w2.transpose(0, 2, 1), sw2.T.reshape(2, 512, D)], 0)
    shared["w1t"] = np.ascontiguousarray(w1a.transpose(0, 2, 1))
    shared["w3t"] = np.ascontiguousarray(w3a.transpose(0, 2, 1))
    shared["w2t"] = np.ascontiguousarray(w2a)
    b1a = np.concatenate([b1, sb1.reshape(2, 512)], 0)
    b3a = np.concatenate([b3, sb3.reshape(2, 512)], 0)
    shared["b1s"] = np.ascontiguousarray(b1a.reshape(NE, 4, 128).transpose(0, 2, 1))
    shared["b3s"] = np.ascontiguousarray(b3a.reshape(NE, 4, 128).transpose(0, 2, 1))
    shared["b2a"] = np.ascontiguousarray(np.concatenate([b2, sb2[None]], 0))
    shared["ebias"] = np.ascontiguousarray(eb)
    shared["gam8"] = np.ascontiguousarray(gam.reshape(8, 128).T)
    shared["bet8"] = np.ascontiguousarray(bet.reshape(8, 128).T)

    ext = np.ascontiguousarray(tbl[np.minimum(np.arange(2 * S - 1), tbl.shape[0] - 1)].T)
    extw_by_q0 = {}
    for Q0 in (0, SQ):
        ew = np.empty((NH, 128, EW), f32)
        for h in range(NH):
            sw = np.lib.stride_tricks.sliding_window_view(ext[h], EW)
            ew[h] = sw[Q0 + 127 - np.arange(128)]
        extw_by_q0[Q0] = np.ascontiguousarray(ew)

    in_maps = []
    for core in range(8):
        b = core // 2
        Q0 = (core % 2) * SQ
        xtb = np.ascontiguousarray(x[b].T)
        m = dict(shared)
        m["chs"] = np.ascontiguousarray(tbl[4094, :].reshape(1, 16))
        m["xt"] = xtb
        m["xq"] = np.ascontiguousarray(xtb[:, Q0:Q0 + SQ])
        m["extw"] = extw_by_q0[Q0]
        in_maps.append(m)
    return in_maps


_NC_CACHE = {}


def _get_nc():
    key = (tuple(DEBUG_TAPS), CACHE_SALT)
    if key not in _NC_CACHE:
        _NC_CACHE[key] = build_nc()
    return _NC_CACHE[key]


def run_cores(inputs, core_ids=None, **kw):
    in_maps = _stage(inputs)
    if core_ids is None:
        core_ids = list(range(8))
    nc = _get_nc()
    return run_bass_kernel_spmd(nc, [in_maps[c] for c in core_ids], core_ids=core_ids, **kw)


def kernel(**inputs):
    res = run_cores(inputs)
    out = np.empty((4, S, D), np.float32)
    for core in range(8):
        b = core // 2
        Q0 = (core % 2) * SQ
        yfm = res.results[core]["y"].reshape(D, SQ)
        out[b, Q0:Q0 + SQ, :] = yfm.T
    return out


# version 50
# speedup vs baseline: 1.0278x; 1.0278x over previous
import numpy as np
from contextlib import ExitStack

import concourse.bass as bass
import concourse.mybir as mybir
from concourse.bass_utils import run_bass_kernel_spmd

F32 = mybir.dt.float32
BF16 = mybir.dt.bfloat16
U32 = mybir.dt.uint32
AF = mybir.ActivationFunctionType
ALU = mybir.AluOpType

D = 1024
S = 4096
SQ = 2048
NH = 16
HD = 64
WA = 256
NWKV = S // WA
NWQ = SQ // WA
NKC = S // 128
EW = 6016
NE = 18
DFF = 512
EPS = 1e-5

DEBUG_TAPS = ()
CACHE_SALT = 1


class Sem:
    def __init__(self, ctx, nc, name):
        self.h = ctx.enter_context(nc.semaphore(name))
        self.name = name
        self.n = 0

    def inc(self, k=1):
        self.n += k
        return self.n


class Plan:
    def __init__(self):
        self.items = []
        self.waited = {}

    def em(self, eng, fn):
        self.items.append((eng, fn))

    def wait(self, eng, sem, val):
        if val <= 0:
            return
        key = (eng, sem.name)
        if self.waited.get(key, 0) >= val:
            return
        self.waited[key] = val
        h = sem.h

        def fn(e, h=h, val=val):
            e.wait_ge(h, val)
        self.items.append((eng, fn))

    def run(self, eng, obj):
        for en, fn in self.items:
            if en == eng:
                fn(obj)


def build_nc():
    nc = bass.Bass("TRN2", target_bir_lowering=False)

    xt = nc.dram_tensor("xt", [D, S], F32, kind="ExternalInput").ap()
    xq = nc.dram_tensor("xq", [D, SQ], F32, kind="ExternalInput").ap()
    wqt = nc.dram_tensor("wqt", [D, D], F32, kind="ExternalInput").ap()
    wkt = nc.dram_tensor("wkt", [D, D], F32, kind="ExternalInput").ap()
    wvt = nc.dram_tensor("wvt", [D, D], F32, kind="ExternalInput").ap()
    extw = nc.dram_tensor("extw", [NH, 128, EW], F32, kind="ExternalInput").ap()
    chs = nc.dram_tensor("chs", [1, 16], F32, kind="ExternalInput").ap()
    gwt = nc.dram_tensor("gwt", [D, 16], F32, kind="ExternalInput").ap()
    w1t = nc.dram_tensor("w1t", [NE, D, DFF], F32, kind="ExternalInput").ap()
    w3t = nc.dram_tensor("w3t", [NE, D, DFF], F32, kind="ExternalInput").ap()
    w2t = nc.dram_tensor("w2t", [NE, DFF, D], F32, kind="ExternalInput").ap()
    b1s = nc.dram_tensor("b1s", [NE, 128, 4], F32, kind="ExternalInput").ap()
    b3s = nc.dram_tensor("b3s", [NE, 128, 4], F32, kind="ExternalInput").ap()
    b2a = nc.dram_tensor("b2a", [17, D], F32, kind="ExternalInput").ap()
    ebias = nc.dram_tensor("ebias", [16], F32, kind="ExternalInput").ap()
    gam8 = nc.dram_tensor("gam8", [128, 8], F32, kind="ExternalInput").ap()
    bet8 = nc.dram_tensor("bet8", [128, 8], F32, kind="ExternalInput").ap()

    y = nc.dram_tensor("y", [8, 128, SQ], F32, kind="ExternalOutput").ap()
    taps = {}
    for t_ in DEBUG_TAPS:
        shp, dt = {
            "k": ([8, 128, S], BF16), "atn": ([8, 128, SQ], F32),
            "kdup": ([128, 2 * S], BF16), "qdup": ([128, 2 * SQ], BF16),
            "aow": ([128, 8 * 512], F32), "gdbg": ([128, 36], F32),
            "extb": ([128, 2 * EW], BF16),
            "x2": ([128, 8 * SQ], F32), "lnx2": ([128, 8 * SQ], BF16),
            "gate": ([17, SQ], BF16), "logits": ([16, SQ], F32),
            "q": ([128, 8 * SQ], BF16), "v": ([128, 32 * 1040 + 64], BF16),
        }[t_]
        taps[t_] = nc.dram_tensor("tap_" + t_, shp, dt, kind="ExternalOutput").ap()

    kscr = nc.dram_tensor("kscr", [8, 128, S], BF16, kind="Internal").ap()
    ascr = nc.dram_tensor("ascr", [8, 128, SQ], F32, kind="Internal").ap()

    ctx = ExitStack()
    e0 = ctx.enter_context

    ones1f = e0(nc.sbuf_tensor("ones1f", [1, 128], F32))
    ones128f = e0(nc.sbuf_tensor("ones128f", [128, 1], F32))
    ones128b = e0(nc.sbuf_tensor("ones128b", [128, 1], BF16))
    identf = e0(nc.sbuf_tensor("identf", [128, 128], F32))
    iota16 = e0(nc.sbuf_tensor("iota16", [128, 16], F32))
    gam8s = e0(nc.sbuf_tensor("gam8s", [128, 8], F32))
    bet8s = e0(nc.sbuf_tensor("bet8s", [128, 8], F32))
    eb_s = e0(nc.sbuf_tensor("eb_s", [16, 1], F32))
    gwt_s = e0(nc.sbuf_tensor("gwt_s", [128, 128], BF16))
    epsc = e0(nc.sbuf_tensor("epsc", [1, 1], F32))
    b1s_s = e0(nc.sbuf_tensor("b1s_s", [128, NE * 4], F32))
    b3s_s = e0(nc.sbuf_tensor("b3s_s", [128, NE * 4], F32))

    sems = {}
    for name in ["ldi", "ldig", "gpi", "vvi", "dve_a", "act_a", "pe_a",
                 "pe_sc", "act_x", "pe_av", "pe_bc", "dve_ep", "act_ep",
                 "dve_b", "act_b", "pe_b", "pool_b",
                 "dve_c", "act_c", "pe_c", "sty",
                 "lda0", "lda1", "lda2", "kst0", "kst1", "ldh0", "ldh1",
                 "ldx0", "ldx1", "ast0", "ast1", "ldb0", "ldb1", "ldc0", "ldc1",
                 "gbd0", "gbd1"]:
        sems[name] = Sem(ctx, nc, name)
    ldi = sems["ldi"]
    ldig = sems["ldig"]
    gpi, vvi = sems["gpi"], sems["vvi"]
    dve_a, act_a, pe_a = sems["dve_a"], sems["act_a"], sems["pe_a"]
    pe_sc, act_x, pe_av = sems["pe_sc"], sems["act_x"], sems["pe_av"]
    pe_bc, dve_ep, act_ep = sems["pe_bc"], sems["dve_ep"], sems["act_ep"]
    dve_b, act_b, pe_b = sems["dve_b"], sems["act_b"], sems["pe_b"]
    pool_b = sems["pool_b"]
    dve_c, act_c, pe_c, sty = sems["dve_c"], sems["act_c"], sems["pe_c"], sems["sty"]
    lda = [sems["lda0"], sems["lda1"], sems["lda2"]]
    kst = [sems["kst0"], sems["kst1"]]
    ldh = [sems["ldh0"], sems["ldh1"]]
    ldx = [sems["ldx0"], sems["ldx1"]]
    ast = [sems["ast0"], sems["ast1"]]
    ldb2 = [sems["ldb0"], sems["ldb1"]]
    ldc = [sems["ldc0"], sems["ldc1"]]
    gbd = [sems["gbd0"], sems["gbd1"]]

    ctxAB = ExitStack()
    eAB = ctxAB.enter_context
    v_sb = eAB(nc.sbuf_tensor("v_sb", [128, 32 * 1040 + 64], BF16))
    q_sb = eAB(nc.sbuf_tensor("q_sb", [128, 8 * SQ], BF16))

    ctxA = ExitStack()
    eA = ctxA.enter_context
    wq_s = eA(nc.sbuf_tensor("wq_s", [128, 8 * D], BF16))
    wk_s = eA(nc.sbuf_tensor("wk_s", [128, 8 * D], BF16))
    wv_s = eA(nc.sbuf_tensor("wv_s", [128, 8 * D], BF16))
    xw = eA(nc.sbuf_tensor("xw", [128, 3 * 8 * WA], F32))
    lnw = eA(nc.sbuf_tensor("lnw", [128, 2 * 8 * WA], BF16))
    sqw = eA(nc.sbuf_tensor("sqw", [128, 8 * WA], BF16))
    tmpa = eA(nc.sbuf_tensor("tmpa", [128, 8 * WA], F32))
    kstg = eA(nc.sbuf_tensor("kstg", [128, 2 * 8 * WA], BF16))
    murow = eA(nc.sbuf_tensor("murow", [1, WA], F32))
    musq = eA(nc.sbuf_tensor("musq", [1, WA], F32))
    varrow = eA(nc.sbuf_tensor("varrow", [1, WA], F32))
    rstdrow = eA(nc.sbuf_tensor("rstdrow", [1, WA], F32))
    negmurow = eA(nc.sbuf_tensor("negmurow", [1, WA], F32))

    psA = ExitStack()
    eP = psA.enter_context
    sx_ps = eP(nc.psum_tensor("sx_ps", [1, WA], F32))
    sq_ps = eP(nc.psum_tensor("sq_ps", [1, WA], F32))
    rb_ps = eP(nc.psum_tensor("rb_ps", [128, WA], F32))
    mb_ps = eP(nc.psum_tensor("mb_ps", [128, WA], F32))
    kp_ps = [eP(nc.psum_tensor("kp_ps%d" % i, [128, WA], F32)) for i in range(2)]
    vp_ps = [eP(nc.psum_tensor("vp_ps%d" % i, [128, 512], F32)) for i in range(2)]

    with nc.Block("init") as blk:
        @blk.sync
        def _(sync):
            sync.dma_start(out=gam8s[:, :], in_=gam8).then_inc(ldi.h, 16); ldi.inc(16)
            sync.dma_start(out=bet8s[:, :], in_=bet8).then_inc(ldi.h, 16); ldi.inc(16)
            sync.dma_start(out=eb_s[:, 0:1], in_=ebias[:, None]).then_inc(ldi.h, 16); ldi.inc(16)
            sync.dma_start(out=b1s_s[:, :].rearrange("p (e c) -> p e c", e=NE),
                           in_=b1s.rearrange("e p c -> p e c")).then_inc(ldi.h, 16); ldi.inc(16)
            sync.dma_start(out=b3s_s[:, :].rearrange("p (e c) -> p e c", e=NE),
                           in_=b3s.rearrange("e p c -> p e c")).then_inc(ldi.h, 16); ldi.inc(16)

        @blk.gpsimd
        def _(gp):
            gp.dma_start(out=wq_s[:, :].rearrange("p (dc e) -> p dc e", dc=8),
                         in_=wqt.rearrange("(dc p) e -> p dc e", p=128)).then_inc(ldig.h, 16); ldig.inc(16)
            gp.dma_start(out=wk_s[:, :].rearrange("p (dc e) -> p dc e", dc=8),
                         in_=wkt.rearrange("(dc p) e -> p dc e", p=128)).then_inc(ldig.h, 16); ldig.inc(16)
            gp.dma_start(out=wv_s[:, :].rearrange("p (dc e) -> p dc e", dc=8),
                         in_=wvt.rearrange("(dc p) e -> p dc e", p=128)).then_inc(ldig.h, 16); ldig.inc(16)
            gp.dma_start(out=gwt_s[:, :].rearrange("p (dc e) -> p dc e", dc=8),
                         in_=gwt.rearrange("(dc p) e -> p dc e", p=128)).then_inc(ldig.h, 16); ldig.inc(16)
            gp.memset(ones1f[0:1, :], 1.0)
            gp.memset(epsc[0:1, 0:1], EPS)
            gp.memset(ones128f[:, 0:1], 1.0)
            gp.memset(ones128b[:, 0:1], 1.0)
            gp.memset(identf[:, :], 0.0).then_inc(gpi.h, 1)
            gp.wait_ge(gpi.h, 1)
            gp.affine_select(out=identf[:, :], in_=identf[:, :], compare_op=ALU.not_equal,
                             fill=1.0, base=0, pattern=[[-1, 128]], channel_multiplier=1)
            gp.iota(iota16[:, :], [[1, 16]], channel_multiplier=0,
                    allow_small_or_imprecise_dtypes=True).then_inc(gpi.h, 1)

        @blk.vector
        def _(v):
            ins = None
            for tc in range(32):
                ins = v.memset(bass.AP(v_sb, tc * 1040 + 64, [[32 * 1040 + 64, 128], [65, 16], [1, 1]]), 1.0)
            ins.then_inc(vvi.h, 1)
            for _i in range(CACHE_SALT):
                v.engine_nop()

    gpi.inc(2)
    vvi.inc()
    LDI_FINAL = ldi.n
    LDIG_FINAL = ldig.n

    pl = Plan()
    ck_apply = {}
    ck_stats = {}
    ck_qkv = {}
    ck_var = {}
    ck_bcast = {}
    ck_kevac = {}
    ck_kdma = {}
    ck_evac_kq = {}
    ck_vevac = {}

    for eng in ("t", "v", "s", "y"):
        pl.wait(eng, ldi, LDI_FINAL)
        pl.wait(eng, ldig, LDIG_FINAL)
        pl.wait(eng, gpi, 2)
        pl.wait(eng, vvi, 1)

    NW = NWKV + NWQ

    def st_dma(w):
        isq = w >= NWKV
        src_ = xq if isq else xt
        col0 = ((w - NWKV) if isq else w) * WA
        ncols = SQ if isq else S
        xbuf = (w % 3) * 8 * WA
        pl.wait("y", dve_a, ck_apply.get(w - 3, 0))

        def f(sy, src_=src_, col0=col0, ncols=ncols, xbuf=xbuf):
            sy.dma_start(out=bass.AP(xw, xbuf, [[3 * 8 * WA, 128], [WA, 8], [1, WA]]),
                         in_=bass.AP(src_.tensor, col0, [[ncols, 128], [128 * ncols, 8], [1, WA]])
                         ).then_inc(lda[w % 3].h, 16)
        pl.em("y", f); lda[w % 3].inc(16)
        ck_lda[w] = (lda[w % 3], lda[w % 3].n)

    def st_chain(w):
        xbuf = (w % 3) * 8 * WA
        pl.wait("v", *ck_lda[w])
        pl.wait("v", pe_a, ck_stats.get(w - 1, 0))

        def f(v, xbuf=xbuf):
            v.tensor_tensor(sqw[:, :], xw[:, xbuf:xbuf + 8 * WA],
                            xw[:, xbuf:xbuf + 8 * WA], ALU.mult).then_inc(dve_a.h, 1)
        pl.em("v", f); dve_a.inc()
        ck_sq = dve_a.n

        pl.wait("t", *ck_lda[w])
        pl.wait("t", dve_a, ck_sq)
        pl.wait("t", dve_a, ck_var.get(w - 1, 0))

        def f(t, xbuf=xbuf):
            for c in range(8):
                t.matmul(sx_ps[0:1, :], ones128f[:, 0:1], xw[:, xbuf + c * WA: xbuf + (c + 1) * WA],
                         start=(c == 0), stop=(c == 7))
            ins = None
            for c in range(8):
                ins = t.matmul(sq_ps[0:1, :], ones128b[:, 0:1], sqw[:, c * WA:(c + 1) * WA],
                               start=(c == 0), stop=(c == 7))
            ins.then_inc(pe_a.h, 1)
        pl.em("t", f); pe_a.inc()
        ck_stats[w] = pe_a.n

        pl.wait("v", pe_a, ck_stats[w])
        pl.wait("v", pe_a, ck_bcast.get(w - 1, 0))

        def f(v):
            v.tensor_scalar(murow[0:1, :], sx_ps[0:1, :], 1.0 / D, None, ALU.mult).then_inc(dve_a.h, 1)
        pl.em("v", f); dve_a.inc()
        pl.wait("v", dve_a, dve_a.n)

        def f(v):
            v.tensor_tensor(musq[0:1, :], murow[0:1, :], murow[0:1, :], ALU.mult).then_inc(dve_a.h, 1)
        pl.em("v", f); dve_a.inc()
        pl.wait("v", dve_a, dve_a.n)

        def f(v):
            v.scalar_tensor_tensor(varrow[0:1, :], sq_ps[0:1, :], 1.0 / D, musq[0:1, :],
                                   ALU.mult, ALU.subtract).then_inc(dve_a.h, 1)
        pl.em("v", f); dve_a.inc()
        ck_var[w] = dve_a.n

        pl.wait("s", dve_a, ck_var[w])

        def f(s):
            s.activation(varrow[0:1, :], varrow[0:1, :], AF.Ln, bias=epsc[0:1, 0:1], scale=1.0
                         ).then_inc(act_a.h, 1)
        pl.em("s", f); act_a.inc()
        pl.wait("s", act_a, act_a.n)

        def f(s):
            s.activation(rstdrow[0:1, :], varrow[0:1, :], AF.Exp, bias=0.0, scale=-0.5
                         ).then_inc(act_a.h, 1)
        pl.em("s", f); act_a.inc()
        ck_rstd = act_a.n

        pl.wait("v", act_a, ck_rstd)

        def f(v):
            v.tensor_scalar(negmurow[0:1, :], murow[0:1, :], -1.0, None, ALU.mult
                            ).then_inc(dve_a.h, 1)
        pl.em("v", f); dve_a.inc()
        ck_negmur[w] = dve_a.n

    def st_apply(w):
        xbuf = (w % 3) * 8 * WA
        lbuf = (w % 2) * 8 * WA
        pl.wait("t", dve_a, ck_negmur[w])
        pl.wait("t", dve_a, ck_apply.get(w - 1, 0))

        def f(t):
            t.matmul(rb_ps[:, :], ones1f[0:1, :], rstdrow[0:1, :], start=True, stop=True)
            t.matmul(mb_ps[:, :], ones1f[0:1, :], negmurow[0:1, :], start=True, stop=True
                     ).then_inc(pe_a.h, 1)
        pl.em("t", f); pe_a.inc()
        ck_bcast[w] = pe_a.n

        pl.wait("v", pe_a, ck_bcast[w])
        pl.wait("v", pe_a, ck_qkv.get(w - 2, 0))

        def f(v, xbuf=xbuf, lbuf=lbuf):
            v.tensor_tensor(bass.AP(tmpa, 0, [[8 * WA, 128], [WA, 8], [1, WA]]),
                            bass.AP(xw, xbuf, [[3 * 8 * WA, 128], [WA, 8], [1, WA]]),
                            bass.AP(mb_ps, 0, [[WA, 128], [0, 8], [1, WA]]), ALU.add)
            v.tensor_tensor(bass.AP(lnw, lbuf, [[2 * 8 * WA, 128], [WA, 8], [1, WA]]),
                            bass.AP(tmpa, 0, [[8 * WA, 128], [WA, 8], [1, WA]]),
                            bass.AP(rb_ps, 0, [[WA, 128], [0, 8], [1, WA]]), ALU.mult
                            ).then_inc(dve_a.h, 1)
        pl.em("v", f); dve_a.inc()
        ck_apply[w] = dve_a.n

    def st_qkv(w):
        isq = w >= NWKV
        wq_i = w - NWKV
        lbuf = (w % 2) * 8 * WA
        pl.wait("t", dve_a, ck_apply[w])
        if isq:
            for ec in range(8):
                par = ec % 2
                pl.wait("t", act_a, ck_evac_kq.get(par, 0))

                def f(t, ec=ec, lbuf=lbuf, par=par):
                    ins = None
                    for dc in range(8):
                        ins = t.matmul(kp_ps[par][:, :], wq_s[:, dc * D + ec * 128: dc * D + ec * 128 + 128],
                                       lnw[:, lbuf + dc * WA: lbuf + (dc + 1) * WA],
                                       start=(dc == 0), stop=(dc == 7))
                    ins.then_inc(pe_a.h, 1)
                pl.em("t", f); pe_a.inc()
                mmn = pe_a.n

                pl.wait("s", pe_a, mmn)

                def f(s, ec=ec, wq_i=wq_i, par=par):
                    s.copy(q_sb[:, ec * SQ + wq_i * WA: ec * SQ + (wq_i + 1) * WA],
                           kp_ps[par][:, :]).then_inc(act_a.h, 1)
                pl.em("s", f); act_a.inc()
                ck_evac_kq[par] = act_a.n
            ck_qkv[w] = pe_a.n
        else:
            kbuf = (w % 2) * 8 * WA
            if w - 2 in ck_kdma:
                pl.wait("s", *ck_kdma[w - 2])
            for ec in range(8):
                par = ec % 2
                pl.wait("t", act_a, ck_evac_kq.get(par, 0))

                def f(t, ec=ec, lbuf=lbuf, par=par):
                    ins = None
                    for dc in range(8):
                        ins = t.matmul(kp_ps[par][:, :], wk_s[:, dc * D + ec * 128: dc * D + ec * 128 + 128],
                                       lnw[:, lbuf + dc * WA: lbuf + (dc + 1) * WA],
                                       start=(dc == 0), stop=(dc == 7))
                    ins.then_inc(pe_a.h, 1)
                pl.em("t", f); pe_a.inc()
                mmn = pe_a.n

                pl.wait("s", pe_a, mmn)

                def f(s, ec=ec, kbuf=kbuf, par=par):
                    s.copy(kstg[:, kbuf + ec * WA: kbuf + (ec + 1) * WA],
                           kp_ps[par][:, :]).then_inc(act_a.h, 1)
                pl.em("s", f); act_a.inc()
                ck_evac_kq[par] = act_a.n
            ck_kevac = act_a.n

            pl.wait("y", act_a, ck_kevac)

            def f(sy, w=w, kbuf=kbuf):
                sy.dma_start(out=bass.AP(kscr.tensor, w * WA, [[S, 128], [128 * S, 8], [1, WA]]),
                             in_=bass.AP(kstg, kbuf, [[2 * 8 * WA, 128], [WA, 8], [1, WA]])
                             ).then_inc(kst[w % 2].h, 16)
            pl.em("y", f); kst[w % 2].inc(16)
            ck_kdma[w] = (kst[w % 2], kst[w % 2].n)

            for tc2 in range(2):
                tile = 2 * w + tc2
                for half in range(2):
                    pl.wait("t", act_a, ck_vevac.get(half, 0))

                    def f(t, lbuf=lbuf, tc2=tc2, half=half):
                        ins = None
                        for dc in range(8):
                            ins = t.matmul(vp_ps[half][:, :],
                                           lnw[:, lbuf + dc * WA + tc2 * 128: lbuf + dc * WA + tc2 * 128 + 128],
                                           wv_s[:, dc * D + half * 512: dc * D + (half + 1) * 512],
                                           start=(dc == 0), stop=(dc == 7))
                        ins.then_inc(pe_a.h, 1)
                    pl.em("t", f); pe_a.inc()
                    mmn = pe_a.n

                    pl.wait("s", pe_a, mmn)

                    def f(s, tile=tile, half=half):
                        s.copy(
                            bass.AP(v_sb, tile * 1040 + half * 8 * 65, [[32 * 1040 + 64, 128], [65, 8], [1, 64]]),
                            bass.AP(vp_ps[half], 0, [[512, 128], [64, 8], [1, 64]])
                        ).then_inc(act_a.h, 1)
                    pl.em("s", f); act_a.inc()
                    ck_vevac[half] = act_a.n
            ck_qkv[w] = pe_a.n

    ck_lda = {}
    ck_negmur = {}
    for it in range(NW + 3):
        if it < NW:
            st_dma(it)
        if 2 <= it <= NW + 1:
            st_apply(it - 2)
        if 3 <= it <= NW + 2:
            st_qkv(it - 3)
        if 1 <= it <= NW:
            st_chain(it - 1)

    A_PE_FINAL = pe_a.n
    A_DVE_FINAL = dve_a.n
    A_ACT_FINAL = act_a.n

    with nc.Block("phaseA") as blk:
        @blk.sync
        def _(sync):
            pl.run("y", sync)
            for s_ in lda:
                sync.wait_ge(s_.h, s_.n)
            for s_ in kst:
                sync.wait_ge(s_.h, s_.n)

        @blk.tensor
        def _(t):
            pl.run("t", t)

        @blk.vector
        def _(v):
            pl.run("v", v)

        @blk.scalar
        def _(s):
            pl.run("s", s)

    ctxA.close()
    psA.close()

    ctxB = ExitStack()
    eB = ctxB.enter_context
    kdup = eB(nc.sbuf_tensor("kdup", [128, 2 * S], BF16))
    qdup = eB(nc.sbuf_tensor("qdup", [128, 2 * SQ], BF16))
    extb = eB(nc.sbuf_tensor("extb", [128, 2 * EW], BF16))
    pbuf = eB(nc.sbuf_tensor("pbuf", [128, 4 * 1024], BF16))
    avsb = eB(nc.sbuf_tensor("avsb", [65, 512], F32))
    lrow = eB(nc.sbuf_tensor("lrow", [1, 512], F32))
    lrinv = eB(nc.sbuf_tensor("lrinv", [64, 512], F32))
    atn = eB(nc.sbuf_tensor("atn", [64, 2 * SQ], F32))
    pgs = eB(nc.sbuf_tensor("pgs", [1, 3 * 2048], BF16))
    chs_sb = eB(nc.sbuf_tensor("chs_sb", [1, 16], F32))
    ones512 = eB(nc.sbuf_tensor("ones512", [1, 512], BF16))

    psB = ExitStack()
    eP = psB.enter_context
    sc_ps = [eP(nc.psum_tensor("sc_ps%d" % i, [128, 1024], F32)) for i in range(2)]
    av_ps = [eP(nc.psum_tensor("av_ps%d" % i, [128, 512], F32)) for i in range(2)]
    bv_ps = [eP(nc.psum_tensor("bv_ps%d" % i, [128, 512], F32)) for i in range(2)]

    pl = Plan()
    for eng in ("t", "v", "s", "y", "g"):
        for s_ in kst:
            pl.wait(eng, s_, s_.n)
        pl.wait(eng, dve_a, A_DVE_FINAL)
        pl.wait(eng, act_a, A_ACT_FINAL)
        pl.wait(eng, pe_a, A_PE_FINAL)
        pl.wait(eng, gpi, 2)
        pl.wait(eng, vvi, 1)

    pending_ep = None
    ck_sc_done_h = {}
    ck_av_done_h = {}
    ck_ldh = {}
    exp_of_scbuf = {}
    avbv_of_pbuf = {}
    ck_ep_dve = {}
    ck_ep_act = {}
    ck_ast = {}
    ck_ldx = {}

    def emit_loads(h):
        pl.wait("y", pe_sc, ck_sc_done_h.get(h - 2, 0))
        pl.wait("y", pe_av, ck_av_done_h.get(h - 2, 0))
        kb = (h % 2) * S
        qb = (h % 2) * SQ

        def f(sy, h=h, kb=kb, qb=qb):
            src = kscr.tensor
            sy.dma_start(out=kdup[0:64, kb:kb + S],
                         in_=kscr[h // 2, (h % 2) * 64:(h % 2) * 64 + 64, :]).then_inc(ldh[h % 2].h, 16)
            sy.dma_start(out=kdup[64:128, kb:kb + S],
                         in_=kscr[h // 2, (h % 2) * 64:(h % 2) * 64 + 64, :]).then_inc(ldh[h % 2].h, 16)
            qsrc = q_sb[(h % 2) * 64:(h % 2) * 64 + 64, (h // 2) * SQ:(h // 2 + 1) * SQ]
            sy.dma_start(out=qdup[0:64, qb:qb + SQ], in_=qsrc).then_inc(ldh[h % 2].h, 16)
            sy.dma_start(out=qdup[64:128, qb:qb + SQ], in_=qsrc).then_inc(ldh[h % 2].h, 16)
        pl.em("y", f)
        ldh[h % 2].inc(64)

        pl.wait("g", pe_av, ck_av_done_h.get(h - 2, 0))

        def f(gp, h=h):
            gp.dma_start(out=extb[:, (h % 2) * EW:(h % 2 + 1) * EW], in_=extw[h]).then_inc(ldx[h % 2].h, 16)
        pl.em("g", f)
        ldx[h % 2].inc(16)
        ck_ldh[h] = (ldh[h % 2], ldh[h % 2].n)
        ck_ldx[h] = (ldx[h % 2], ldx[h % 2].n)

    if "q" in DEBUG_TAPS:
        def f(sy):
            sy.dma_start(out=taps["q"], in_=q_sb[:, :]).then_inc(sty.h, 16)
        pl.em("y", f); sty.inc(16)
    if "v" in DEBUG_TAPS:
        def f(sy):
            sy.dma_start(out=taps["v"], in_=v_sb[:, :]).then_inc(sty.h, 16)
        pl.em("y", f); sty.inc(16)

    def f(gp):
        gp.dma_start(out=chs_sb[0:1, :], in_=chs).then_inc(ldx[0].h, 16)
        gp.memset(ones512[0:1, :], 1.0).then_inc(ldx[0].h, 1)
    pl.em("g", f); ldx[0].inc(17)
    for eng in ("t", "v"):
        pl.wait(eng, ldx[0], ldx[0].n)

    pp_evac = {}
    ppc = 0
    for h in range(NH):
        off = min(65 * h, 1040 - 256)
        for j in range(3):
            par = ppc % 2
            pl.wait("t", dve_ep, pp_evac.get(ppc - 2, 0))

            def f(t, j=j, off=off, par=par):
                ins = None
                for c in range(4):
                    kc = 4 * j + c
                    ins = t.matmul(bv_ps[par][0:1, 0:256], ones128b[:, 0:1],
                                   v_sb[:, kc * 1040 + off: kc * 1040 + off + 256],
                                   start=(c == 0), stop=(c == 3))
                ins.then_inc(pe_av.h, 1)
            pl.em("t", f); pe_av.inc()
            pl.wait("v", pe_av, pe_av.n)

            def f(v, h=h, j=j, off=off, par=par):
                src_ = bv_ps[par][0:1, 65 * h - off: 65 * h - off + 128]
                dst = pgs[0:1, j * 2048 + h * 128: j * 2048 + (h + 1) * 128]
                if j == 0:
                    v.tensor_scalar(dst, src_, chs_sb[0:1, h:h + 1], None,
                                    ALU.mult).then_inc(dve_ep.h, 1)
                else:
                    v.scalar_tensor_tensor(dst, src_, chs_sb[0:1, h:h + 1],
                                           pgs[0:1, (j - 1) * 2048 + h * 128: (j - 1) * 2048 + (h + 1) * 128],
                                           ALU.mult, ALU.add).then_inc(dve_ep.h, 1)
            pl.em("v", f); dve_ep.inc()
            pl.wait("v", dve_ep, dve_ep.n)
            pp_evac[ppc] = dve_ep.n
            ppc += 1
    PREPASS_DVE = dve_ep.n

    emit_loads(0)
    emit_loads(1)
    if "kdup" in DEBUG_TAPS:
        pl.wait("y", *ck_ldh[1])

        def f(sy):
            sy.dma_start(out=taps["kdup"], in_=kdup[:, :]).then_inc(sty.h, 16)
            sy.dma_start(out=taps["qdup"], in_=qdup[:, :]).then_inc(sty.h, 16)
            sy.dma_start(out=taps["extb"], in_=extb[:, :]).then_inc(sty.h, 16)
        pl.em("y", f); sty.inc(48)

    units = [(h, qc, s) for h in range(NH) for qc in range(4) for s in range(16)]
    NU = len(units)
    sc_count = {}

    def emit_sc(u):
        h, qc, s = units[u]
        par = s % 2
        kb = (h % 2) * S
        qb = (h % 2) * SQ
        pl.wait("t", *ck_ldh[h])
        pl.wait("t", *ck_ldx[h])
        pl.wait("t", act_x, exp_of_scbuf.get(par, 0))
        kcA, kcB = 2 * s, 2 * s + 1

        def f(t, par=par, kcA=kcA, kcB=kcB, qc=qc, kb=kb, qb=qb):
            t.matmul(sc_ps[par][:, 0:512],
                     kdup[0:64, kb + kcA * 128: kb + kcA * 128 + 128],
                     qdup[0:64, qb + qc * 512: qb + (qc + 1) * 512],
                     start=True, stop=True, tile_position=(0, 0))
            t.matmul(sc_ps[par][:, 512:1024],
                     kdup[64:128, kb + kcB * 128: kb + kcB * 128 + 128],
                     qdup[64:128, qb + qc * 512: qb + (qc + 1) * 512],
                     start=True, stop=True, tile_position=(64, 0)
                     ).then_inc(pe_sc.h, 1)
        pl.em("t", f)
        pe_sc.inc()
        sc_count[u] = pe_sc.n
        if qc == 3 and s == 15:
            ck_sc_done_h[h] = pe_sc.n

    emit_sc(0)
    emit_sc(1)

    for u in range(NU):
        h, qc, s = units[u]
        par = s % 2
        par4 = s % 4
        xb = (h % 2) * EW
        bvp = qc % 2

        pl.wait("s", pe_sc, sc_count[u])
        pl.wait("s", pe_av, avbv_of_pbuf.get(par4, 0))

        def f(sc, par=par, par4=par4):
            sc.activation(pbuf[:, par4 * 1024: par4 * 1024 + 1024],
                          sc_ps[par][:, :], AF.Exp, bias=0.0, scale=1.0 / 32.0
                          ).then_inc(act_x.h, 1)
        pl.em("s", f)
        act_x.inc()
        expn = act_x.n
        exp_of_scbuf[par] = expn

        if u + 2 < NU:
            emit_sc(u + 2)
        if s == 0 and pending_ep is not None:
            pending_ep()
            pending_ep = None

        pl.wait("t", act_x, expn)
        if s == 0:
            prev2key = (h, qc - 2) if qc >= 2 else ((h - 1, 2 + qc) if h else None)
            if prev2key is not None:
                pl.wait("t", dve_ep, ck_ep_dve.get(prev2key, 0))
        kcA, kcB = 2 * s, 2 * s + 1

        if u == 0:
            pl.wait("t", dve_ep, PREPASS_DVE)

        def f(t, s_=s, par4=par4, kcA=kcA, kcB=kcB, qc=qc, xb=xb, h=h, bvp=bvp):
            ins = None
            if s_ == 2 * qc and qc > 0:
                t.matmul(bv_ps[bvp][:, :],
                         pgs[0:1, (qc - 1) * 2048 + h * 128: (qc - 1) * 2048 + (h + 1) * 128],
                         ones512[0:1, :], start=True, stop=False)
            for j, kc in enumerate((kcA, kcB)):
                ins = t.matmul(av_ps[bvp][:, :],
                               v_sb[:, kc * 1040 + 65 * h: kc * 1040 + 65 * h + 128],
                               pbuf[:, par4 * 1024 + j * 512: par4 * 1024 + (j + 1) * 512],
                               start=(s_ == 0 and j == 0), stop=(s_ == 15 and j == 1))
                if s_ >= 2 * qc:
                    ins = t.matmul(bv_ps[bvp][:, :],
                                   v_sb[:, kc * 1040 + 65 * h: kc * 1040 + 65 * h + 128],
                                   extb[:, xb + 3968 - 128 * kc + 512 * qc: xb + 3968 - 128 * kc + 512 * qc + 512],
                                   start=(qc == 0 and s_ == 0 and j == 0), stop=(s_ == 15 and j == 1))
            ins.then_inc(pe_av.h, 1)
        pl.em("t", f)
        pe_av.inc()
        avbv_of_pbuf[par4] = pe_av.n

        if s == 15:
            av_end = pe_av.n
            pl.wait("v", pe_av, av_end)
            pl.wait("v", dve_ep, ck_ep_dve.get((h, qc - 1) if qc else (h - 1, 3), 0))

            def f(v, bvp=bvp):
                v.tensor_copy(avsb[0:64, :], av_ps[bvp][0:64, :])
                v.tensor_copy(lrow[0:1, :], av_ps[bvp][64:65, :]).then_inc(act_ep.h, 1)
            pl.em("v", f); act_ep.inc()
            ck_ep_act[(h, qc)] = act_ep.n

            def mk_bc(ck_copy=ck_ep_act[(h, qc)], hq=(h, qc)):
                def emit_bc():
                    pl.wait("t", act_ep, ck_copy)

                    def f(t, bvp=hq[1] % 2):
                        t.matmul(av_ps[bvp][0:64, :], ones1f[0:1, 0:64], lrow[0:1, :], start=True, stop=True
                                 ).then_inc(pe_bc.h, 1)
                    pl.em("t", f); pe_bc.inc()
                    ck_bcmm = pe_bc.n

                    pl.wait("v", pe_bc, ck_bcmm)
                    if hq[1] == 0 and hq[0] - 2 in ck_ast:
                        pl.wait("v", *ck_ast[hq[0] - 2])

                    def f(v, bvp=hq[1] % 2):
                        v.reciprocal(lrinv[:, :], av_ps[bvp][0:64, :]).then_inc(dve_ep.h, 1)
                    pl.em("v", f); dve_ep.inc()
                    pl.wait("v", dve_ep, dve_ep.n)

                    def f(v):
                        v.tensor_tensor(avsb[0:64, :], avsb[0:64, :], lrinv[:, :], ALU.mult).then_inc(dve_ep.h, 1)
                    pl.em("v", f); dve_ep.inc()
                    pl.wait("v", dve_ep, dve_ep.n)

                    def f(v, ab=(hq[0] % 2) * SQ, qc=hq[1], bvp=hq[1] % 2):
                        v.tensor_tensor(atn[:, ab + qc * 512: ab + (qc + 1) * 512],
                                        avsb[0:64, :], bv_ps[bvp][0:64, :], ALU.add).then_inc(dve_ep.h, 1)
                    pl.em("v", f); dve_ep.inc()
                    ck_ep_dve[hq] = dve_ep.n
                    if hq[1] == 3:
                        pl.wait("y", dve_ep, ck_ep_dve[hq])

                        def f(sy, h2=hq[0]):
                            sy.dma_start(out=ascr[h2 // 2, (h2 % 2) * 64:(h2 % 2) * 64 + 64, :],
                                         in_=atn[:, (h2 % 2) * SQ:(h2 % 2) * SQ + SQ]).then_inc(ast[h2 % 2].h, 16)
                        pl.em("y", f); ast[hq[0] % 2].inc(16)
                        ck_ast[hq[0]] = (ast[hq[0] % 2], ast[hq[0] % 2].n)
                return emit_bc
            pending_ep = mk_bc()

            if qc == 3:
                ck_av_done_h[h] = pe_av.n
                if h == NH - 1:
                    pending_ep()
                    pending_ep = None
                if h + 2 < NH:
                    emit_loads(h + 2)



    with nc.Block("phaseB") as blk:
        @blk.sync
        def _(sync):
            pl.run("y", sync)
            for s_ in ast + ldh + ldx:
                sync.wait_ge(s_.h, s_.n)

        @blk.gpsimd
        def _(gp):
            pl.run("g", gp)

        @blk.tensor
        def _(t):
            pl.run("t", t)

        @blk.vector
        def _(v):
            pl.run("v", v)

        @blk.scalar
        def _(s):
            pl.run("s", s)

    psB.close()
    ctxB.close()
    ctxAB.close()

    ctxC = ExitStack()
    eC = ctxC.enter_context
    x2 = eC(nc.sbuf_tensor("x2", [128, 8 * SQ], F32))
    b2a_s = eC(nc.sbuf_tensor("b2a_s", [17, D], BF16))
    gaugT = eC(nc.sbuf_tensor("gaugT", [17, SQ], BF16))
    sigall_sb = eC(nc.sbuf_tensor("sigall_sb", [16, 512], BF16))
    identb16 = eC(nc.sbuf_tensor("identb16", [16, 16], BF16))
    logit_sb = eC(nc.sbuf_tensor("logit_sb", [16, 512], F32))
    lnx2 = eC(nc.sbuf_tensor("lnx2", [128, 8 * SQ], BF16))
    w1_s = eC(nc.sbuf_tensor("w1_s", [128, 2 * 8 * DFF], BF16))
    w3_s = eC(nc.sbuf_tensor("w3_s", [128, 2 * 8 * DFF], BF16))
    w2_s = eC(nc.sbuf_tensor("w2_s", [128, 2 * 4 * D], BF16))
    h_s = eC(nc.sbuf_tensor("h_s", [128, 2 * 4 * 512], BF16))
    silu_s = eC(nc.sbuf_tensor("silu_s", [128, 2 * 512], F32))
    gb_sb = eC(nc.sbuf_tensor("gb_sb", [128, 2 * 512], BF16))
    aow = eC(nc.sbuf_tensor("aow", [128, 2 * 8 * 256], F32))
    sq2b = eC(nc.sbuf_tensor("sq2b", [128, 2 * 8 * 256], BF16))
    tmp2 = eC(nc.sbuf_tensor("tmp2", [128, 256], F32))
    tmp2g = eC(nc.sbuf_tensor("tmp2g", [128, 256], F32))
    negmurow2 = eC(nc.sbuf_tensor("negmurow2", [1, 2 * 256], F32))
    varrow2 = eC(nc.sbuf_tensor("varrow2", [1, 2 * 256], F32))
    musq2 = eC(nc.sbuf_tensor("musq2", [1, 256], F32))
    sqrow2 = eC(nc.sbuf_tensor("sqrow2", [1, 2 * 256], F32))
    mbc_sb = eC(nc.sbuf_tensor("mbc_sb", [128, 2 * 256], F32))
    rbc_sb = eC(nc.sbuf_tensor("rbc_sb", [128, 2 * 256], F32))
    biasc_g = eC(nc.sbuf_tensor("biasc_g", [16, 2 * 128], F32))
    m8_g = eC(nc.sbuf_tensor("m8_g", [128, 2 * 8], F32))
    i8_g = eC(nc.sbuf_tensor("i8_g", [128, 2 * 8], U32))
    idxf_g = eC(nc.sbuf_tensor("idxf_g", [128, 2 * 2], F32))
    mask1_g = eC(nc.sbuf_tensor("mask1_g", [128, 2 * 16], F32))
    mask2_g = eC(nc.sbuf_tensor("mask2_g", [128, 2 * 16], F32))
    g_g = eC(nc.sbuf_tensor("g_g", [128, 2 * 17], F32))

    psC = ExitStack()
    eP = psC.enter_context
    a_ps = [eP(nc.psum_tensor("a_ps%d" % i, [128, 512], F32)) for i in range(2)]
    b_ps = [eP(nc.psum_tensor("b_ps%d" % i, [128, 512], F32)) for i in range(2)]
    y_ps = [eP(nc.psum_tensor("y_ps%d" % i, [128, 512], F32)) for i in range(2)]
    gate_ps = eP(nc.psum_tensor("gate_ps", [128, 512], F32))
    st_ps = eP(nc.psum_tensor("st_ps", [1, 512], F32))
    lg_ps = gate_ps[0:16, 0:512]
    trb_p = [gate_ps[:, 0:16], gate_ps[:, 16:32]]
    trs_p = [gate_ps[:, 256:264].bitcast(BF16), gate_ps[:, 264:272].bitcast(BF16)]
    gtr_p = [gate_ps[0:17, 128:256], gate_ps[0:17, 384:512]]


    pl = Plan()
    for eng in ("t", "v", "s", "y", "g"):
        for s_ in ast:
            pl.wait(eng, s_, s_.n)
        pl.wait(eng, gpi, 2)
        pl.wait(eng, vvi, 1)
        pl.wait(eng, ldi, LDI_FINAL)

    def f(gp):
        gp.dma_start(out=b2a_s[:, :], in_=b2a).then_inc(ldc[0].h, 16)
    pl.em("g", f); ldc[0].inc(16)

    def f(v):
        v.tensor_copy(identb16[:, :], identf[0:16, 0:16])
        v.memset(g_g[:, 16:17], 1.0)
        v.memset(g_g[:, 33:34], 1.0).then_inc(dve_b.h, 1)
    pl.em("v", f); dve_b.inc()

    ck_x2ld = {}
    for tc in range(4):
        def f(sy, tc=tc):
            sy.dma_start(out=bass.AP(x2, tc * 512, [[8 * SQ, 128], [SQ, 8], [1, 512]]),
                         in_=bass.AP(xq.tensor, tc * 512, [[SQ, 128], [128 * SQ, 8], [1, 512]])
                         ).then_inc(ldb2[tc % 2].h, 16)
        pl.em("y", f); ldb2[tc % 2].inc(16)
        ck_x2ld[tc] = (ldb2[tc % 2], ldb2[tc % 2].n)

    def emit_wload(e_, pe_gate):
        buf = e_ % 2
        pl.wait("g", pe_c, pe_gate)

        def f(gp, e_=e_, buf=buf):
            gp.dma_start(out=bass.AP(w1_s, buf * 8 * DFF, [[2 * 8 * DFF, 128], [DFF, 8], [1, DFF]]),
                         in_=w1t[e_].rearrange("(dc p) f -> p dc f", p=128)).then_inc(ldc[buf].h, 16)
            gp.dma_start(out=bass.AP(w3_s, buf * 8 * DFF, [[2 * 8 * DFF, 128], [DFF, 8], [1, DFF]]),
                         in_=w3t[e_].rearrange("(dc p) f -> p dc f", p=128)).then_inc(ldc[buf].h, 16)
            gp.dma_start(out=bass.AP(w2_s, buf * 4 * D, [[2 * 4 * D, 128], [D, 4], [1, D]]),
                         in_=w2t[e_].rearrange("(fc p) d -> p fc d", p=128)).then_inc(ldc[buf].h, 16)
        pl.em("g", f)
        ldc[buf].inc(48)
        return (ldc[buf], ldc[buf].n)

    ESEQ = [16, 17] + list(range(16))
    ck_wld = {16: emit_wload(16, 0), 17: emit_wload(17, 0)}

    uctr = [0]
    ck_sq_free = {}
    ck_rows_read = {}
    ck_row_buf = {}
    ck_mrb_buf = {}
    ck_ap_p0 = {}
    ck_lnx2 = {}
    ck_aow_free = {}
    ck_aowld = {}
    ck_sqd = {}
    ck_sig = {}
    ck_gaug = {}
    ck_gate_done = {}
    ck_lnx2_half = {}
    gstate = {"trb_read": {}, "chunk_done": {}, "g2": {}, "n": 0}

    def emit_gate_chunk(tc, j, ck_lgev):
        gp = gstate["n"] % 2
        gstate["n"] += 1
        biasc = biasc_g[0:16, gp * 128:(gp + 1) * 128]
        m8 = m8_g[:, gp * 8:(gp + 1) * 8]
        i8 = i8_g[:, gp * 8:(gp + 1) * 8]
        idxf = idxf_g[:, gp * 2:(gp + 1) * 2]
        wsum_g = idxf_g[:, gp * 2: gp * 2 + 1]
        winv_g = idxf_g[:, gp * 2 + 1: gp * 2 + 2]
        mk1 = mask1_g[:, gp * 16:(gp + 1) * 16]
        mk2 = mask2_g[:, gp * 16:(gp + 1) * 16]
        gg = g_g[:, gp * 17:(gp + 1) * 17]
        trb_ps = trb_p[gp]
        trs_ps = trs_p[gp]
        gtr_ps = gtr_p[gp]

        pl.wait("v", dve_b, ck_lgev)
        pl.wait("v", pe_b, gstate["trb_read"].get(gp, 0))

        def f(v, tc=tc, j=j, biasc=biasc):
            v.tensor_scalar(biasc, logit_sb[0:16, j * 128:(j + 1) * 128],
                            eb_s[0:16, 0:1], None, ALU.add).then_inc(dve_b.h, 1)
        pl.em("v", f); dve_b.inc()
        ck_bfm = dve_b.n

        pl.wait("t", dve_b, ck_bfm)
        pl.wait("t", act_b, ck_sig[tc])
        pl.wait("t", dve_b, gstate["chunk_done"].get(gp, 0))

        def f(t, j=j, biasc=biasc, trb_ps=trb_ps, trs_ps=trs_ps):
            t.transpose(trb_ps, biasc, identf[0:16, 0:16])
            t.transpose(trs_ps, sigall_sb[0:16, j * 128:(j + 1) * 128], identb16[:, :]
                        ).then_inc(pe_b.h, 1)
        pl.em("t", f); pe_b.inc()
        ck_trn = pe_b.n
        gstate["trb_read"][gp] = ck_trn

        pl.wait("v", pe_b, ck_trn)

        def f(v, m8=m8, trb_ps=trb_ps):
            v.max(m8, trb_ps).then_inc(dve_b.h, 1)
        pl.em("v", f); dve_b.inc()
        pl.wait("v", dve_b, dve_b.n)

        def f(v, m8=m8, i8=i8, trb_ps=trb_ps):
            v.max_index(i8, m8, trb_ps).then_inc(dve_b.h, 1)
        pl.em("v", f); dve_b.inc()
        pl.wait("v", dve_b, dve_b.n)

        gate_steps = [
            lambda v: v.tensor_copy(idxf, i8[:, 0:2]),
            lambda v: v.tensor_tensor(mk1, iota16[:, :],
                                      idxf[:, 0:1].broadcast_to([128, 16]), ALU.is_equal),
            lambda v: v.tensor_tensor(mk2, iota16[:, :],
                                      idxf[:, 1:2].broadcast_to([128, 16]), ALU.is_equal),
            lambda v: v.tensor_tensor(mk1, mk1, mk2, ALU.add),
            lambda v: v.tensor_tensor(mk2, trs_ps, mk1, ALU.mult),
            lambda v: v.reduce_sum(wsum_g, mk2, axis=mybir.AxisListType.X),
            lambda v: v.tensor_scalar(wsum_g, wsum_g, 1e-10, None, ALU.add),
            lambda v: v.reciprocal(winv_g, wsum_g),
        ]
        for step in gate_steps:
            def f(v, step=step):
                step(v).then_inc(dve_b.h, 1)
            pl.em("v", f); dve_b.inc()
            pl.wait("v", dve_b, dve_b.n)

        pl.wait("v", pe_b, gstate["g2"].get(gp, 0))

        def f(v, gg=gg, mk2=mk2, winv_g=winv_g):
            v.tensor_tensor(gg[:, 0:16], mk2,
                            winv_g.broadcast_to([128, 16]), ALU.mult).then_inc(dve_b.h, 1)
        pl.em("v", f); dve_b.inc()
        ck_g = dve_b.n

        pl.wait("t", dve_b, ck_g)

        def f(t, gg=gg, gtr_ps=gtr_ps):
            t.transpose(gtr_ps, gg[:, 0:17], identf[:, :]).then_inc(pe_b.h, 1)
        pl.em("t", f); pe_b.inc()
        ck_g2 = pe_b.n
        gstate["g2"][gp] = ck_g2

        pl.wait("v", pe_b, ck_g2)

        def f(v, tc=tc, j=j, gtr_ps=gtr_ps):
            v.tensor_copy(gaugT[:, tc * 512 + j * 128: tc * 512 + (j + 1) * 128], gtr_ps
                          ).then_inc(dve_b.h, 1)
        pl.em("v", f); dve_b.inc()
        gstate["chunk_done"][gp] = dve_b.n

    def emit_b2_sq(tc, p):
        for h2 in range(2):
            u = 4 * tc + p * 2 + h2
            slot = u % 2
            T0 = tc * 512 + h2 * 256
            pl.wait("g", pe_b, ck_sq_free.get(slot, 0))
            if p == 0:
                pl.wait("g", *ck_aowld[(tc, h2)])
            else:
                pl.wait("g", *ck_x2ld[tc])
                pl.wait("g", dve_b, ck_ap_p0[tc][0])
                pl.wait("g", pool_b, ck_ap_p0[tc][1])

            def f(gp, p=p, h2=h2, T0=T0, slot=slot):
                if p == 0:
                    src = bass.AP(aow, h2 * 2048, [[2 * 2048, 128], [256, 8], [1, 256]])
                else:
                    src = bass.AP(x2, T0, [[8 * SQ, 128], [SQ, 8], [1, 256]])
                dst1 = bass.AP(sq2b, slot * 2048, [[2 * 2048, 128], [256, 8], [1, 256]])
                gp.tensor_tensor(dst1, src, src, ALU.mult).then_inc(pool_b.h, 1)
            pl.em("g", f); pool_b.inc()
            ck_sqd[u] = pool_b.n

    def emit_b2_loads(tc):
        for h2 in range(2):
            if (tc - 1, h2) in ck_aow_free:
                for sv in ck_aow_free[(tc - 1, h2)]:
                    pl.wait("y", *sv)

            def f(sy, tc=tc, h2=h2):
                sy.dma_start(out=bass.AP(aow, h2 * 2048, [[2 * 2048, 128], [256, 8], [1, 256]]),
                             in_=bass.AP(ascr.tensor, tc * 512 + h2 * 256,
                                         [[SQ, 128], [128 * SQ, 8], [1, 256]])
                             ).then_inc(ldb2[h2].h, 16)
            pl.em("y", f); ldb2[h2].inc(16)
            ck_aowld[(tc, h2)] = (ldb2[h2], ldb2[h2].n)
        emit_b2_sq(tc, 0)

    def emit_b2_half(tc, p, h2):
        u = 4 * tc + p * 2 + h2
        rpar = u % 2
        slot = u % 2
        T0 = tc * 512 + h2 * 256

        pl.wait("t", pool_b, ck_sqd[u])
        if p == 0:
            pl.wait("t", *ck_aowld[(tc, h2)])
        else:
            pl.wait("t", *ck_x2ld[tc])
            pl.wait("t", dve_b, ck_ap_p0[tc][0])
            pl.wait("t", pool_b, ck_ap_p0[tc][1])
        if u - 1 in ck_rows_read:
            pl.wait("t", act_b, ck_rows_read[u - 1][0])

        def f(t, p=p, h2=h2, T0=T0, slot=slot):
            for c in range(8):
                if p == 0:
                    xsrc = aow[:, h2 * 2048 + c * 256: h2 * 2048 + (c + 1) * 256]
                else:
                    xsrc = x2[:, c * SQ + T0: c * SQ + T0 + 256]
                t.matmul(st_ps[0:1, 0:256], ones128f[:, 0:1], xsrc, start=(c == 0), stop=(c == 7))
            ins = None
            for c in range(8):
                ins = t.matmul(st_ps[0:1, 256:512], ones128b[:, 0:1],
                               sq2b[:, slot * 2048 + c * 256: slot * 2048 + (c + 1) * 256],
                               start=(c == 0), stop=(c == 7))
            ins.then_inc(pe_b.h, 1)
        pl.em("t", f); pe_b.inc()
        ck_stats_u = pe_b.n
        ck_sq_free[slot] = pe_b.n

        pl.wait("s", pe_b, ck_stats_u)
        if u - 2 in ck_row_buf:
            pl.wait("s", *ck_row_buf[u - 2])

        def f(s, rpar=rpar):
            s.mul(negmurow2[0:1, rpar * 256:(rpar + 1) * 256], st_ps[0:1, 0:256], -1.0 / D)
            s.mul(sqrow2[0:1, rpar * 256:(rpar + 1) * 256], st_ps[0:1, 256:512], 1.0 / D
                  ).then_inc(act_b.h, 1)
        pl.em("s", f); act_b.inc()
        ck_negmu = act_b.n

        pl.wait("s", act_b, ck_negmu)
        if u - 2 in ck_mrb_buf:
            for sv in ck_mrb_buf[u - 2]:
                pl.wait("s", *sv)

        def f(s, rpar=rpar):
            s.dma_start(out=mbc_sb[:, rpar * 256:(rpar + 1) * 256],
                        in_=bass.AP(negmurow2, rpar * 256, [[512, 1], [0, 128], [1, 256]])
                        ).then_inc(gbd[rpar].h, 16)
        pl.em("s", f); gbd[rpar].inc(16)

        pl.wait("v", act_b, ck_negmu)

        def f(v, rpar=rpar):
            v.tensor_tensor(musq2[0:1, :], negmurow2[0:1, rpar * 256:(rpar + 1) * 256],
                            negmurow2[0:1, rpar * 256:(rpar + 1) * 256], ALU.mult)
            v.tensor_tensor(varrow2[0:1, rpar * 256:(rpar + 1) * 256],
                            sqrow2[0:1, rpar * 256:(rpar + 1) * 256],
                            musq2[0:1, :], ALU.subtract).then_inc(dve_b.h, 1)
        pl.em("v", f); dve_b.inc()
        ck_var = dve_b.n
        ck_rows_read[u] = (ck_negmu, ck_var)

        pl.wait("s", dve_b, ck_var)

        def f(s, rpar=rpar):
            sl = varrow2[0:1, rpar * 256:(rpar + 1) * 256]
            s.activation(sl, sl, AF.Ln, bias=epsc[0:1, 0:1], scale=1.0)
            s.activation(sl, sl, AF.Exp, bias=0.0, scale=-0.5).then_inc(act_b.h, 1)
        pl.em("s", f); act_b.inc()
        ck_rstd = act_b.n

        pl.wait("s", act_b, ck_rstd)

        def f(s, rpar=rpar):
            s.dma_start(out=rbc_sb[:, rpar * 256:(rpar + 1) * 256],
                        in_=bass.AP(varrow2, rpar * 256, [[512, 1], [0, 128], [1, 256]])
                        ).then_inc(gbd[rpar].h, 16)
        pl.em("s", f); gbd[rpar].inc(16)
        ck_row_buf[u] = (gbd[rpar], gbd[rpar].n)

        esplit = ((("g", tmp2g, tuple(range(8))),) if p == 1 else
                  (("v", tmp2, tuple(range(8))),))
        for eng, tmpb, csel in esplit:
            pl.wait(eng, *ck_row_buf[u])
            pl.wait(eng, *ck_x2ld[tc])

            if p == 0:
                pl.wait(eng, *ck_aowld[(tc, h2)])

            def f(e, p=p, h2=h2, T0=T0, rpar=rpar, tmpb=tmpb, csel=csel):
                mb = mbc_sb[:, rpar * 256:(rpar + 1) * 256]
                rb = rbc_sb[:, rpar * 256:(rpar + 1) * 256]
                ins = None
                for c in csel:
                    x2sl = x2[:, c * SQ + T0: c * SQ + T0 + 256]
                    if p == 0:
                        aowsl = aow[:, h2 * 2048 + c * 256: h2 * 2048 + (c + 1) * 256]
                        e.tensor_tensor(tmpb[:, :], aowsl, mb, ALU.add)
                        e.tensor_tensor(tmpb[:, :], tmpb[:, :], rb, ALU.mult)
                        e.tensor_scalar(tmpb[:, :], tmpb[:, :], gam8s[:, c:c + 1], bet8s[:, c:c + 1],
                                        ALU.mult, ALU.add)
                        ins = e.tensor_tensor(x2sl, x2sl, tmpb[:, :], ALU.add)
                    else:
                        e.tensor_tensor(tmpb[:, :], x2sl, mb, ALU.add)
                        ins = e.tensor_tensor(lnx2[:, c * SQ + T0: c * SQ + T0 + 256],
                                              tmpb[:, :], rb, ALU.mult)
                ins.then_inc(dve_b.h if eng == "v" else pool_b.h, 1)
            pl.em(eng, f)
            if eng == "v":
                dve_b.inc()
            else:
                pool_b.inc()
        ck_mrb_buf[u] = ((dve_b, dve_b.n), (pool_b, pool_b.n))
        if p == 0:
            ck_aow_free[(tc, h2)] = ((dve_b, dve_b.n), (pool_b, pool_b.n))
            if h2 == 1:
                ck_ap_p0[tc] = (dve_b.n, pool_b.n)
                emit_b2_sq(tc, 1)
        else:
            ck_lnx2_half[(tc, h2)] = ((dve_b, dve_b.n), (pool_b, pool_b.n))
            if h2 == 1:
                ck_lnx2[tc] = ((dve_b, dve_b.n), (pool_b, pool_b.n))

    def emit_b2_gate(tc):
        for sv in ck_lnx2[tc]:
            pl.wait("t", *sv)
        pl.wait("t", dve_b, ck_gate_done.get(tc - 1, 0))
        pl.wait("t", act_b, ck_sig.get(tc - 1, 0))

        def f(t, tc=tc):
            ins = None
            for c in range(8):
                ins = t.matmul(lg_ps, gwt_s[:, c * 16:(c + 1) * 16],
                               lnx2[:, c * SQ + tc * 512: c * SQ + tc * 512 + 512],
                               start=(c == 0), stop=(c == 7))
            ins.then_inc(pe_b.h, 1)
        pl.em("t", f); pe_b.inc()
        ck_lg = pe_b.n

        pl.wait("v", pe_b, ck_lg)

        def f(v):
            v.tensor_copy(logit_sb[:, :], lg_ps).then_inc(dve_b.h, 1)
        pl.em("v", f); dve_b.inc()
        ck_lgev = dve_b.n

        pl.wait("s", pe_b, ck_lg)

        def f(s):
            s.activation(sigall_sb[0:16, :], lg_ps, AF.Sigmoid).then_inc(act_b.h, 1)
        pl.em("s", f); act_b.inc()
        ck_sig[tc] = act_b.n

        for j in range(4):
            emit_gate_chunk(tc, j, ck_lgev)
        ck_gate_done[tc] = dve_b.n
        ck_gaug[tc] = dve_b.n

    emit_b2_loads(0)
    for p in range(2):
        for h2 in range(2):
            emit_b2_half(0, p, h2)

    ck_wld_c = dict(ck_wld)
    ck_ydone = {}
    ck_silu_par = {}
    ck_silu_cons = {}
    ck_b_par = {}
    ck_huse = {}
    ck_wuse = {}
    ck_gbd = {}
    ck_gb_free = {}

    pending_y = None

    def emit_y(prev):
        e_p, tc_p, hb_p, wb2_p, hbuf_p, h_ready_p = prev
        pl.wait("t", dve_c, h_ready_p)
        for sv in ck_lnx2[tc_p]:
            pl.wait("v", *sv)
        for dc in range(8):
            ypar = dc % 2
            pl.wait("t", dve_c, ck_ydone.get(ypar, 0))

            def f(t, wb2=wb2_p, dc=dc, hb=hb_p):
                ins = None
                for fc in range(4):
                    ins = t.matmul(y_ps[dc % 2][:, :],
                                   w2_s[:, wb2 + fc * D + dc * 128: wb2 + fc * D + dc * 128 + 128],
                                   h_s[:, hb + fc * 512: hb + (fc + 1) * 512],
                                   start=(fc == 0), stop=(fc == 3))
                ins.then_inc(pe_c.h, 1)
            pl.em("t", f); pe_c.inc()
            ymm = pe_c.n
            if dc == 7:
                ck_huse[hbuf_p] = pe_c.n

            pl.wait("v", pe_c, ymm)

            def f(v, dc=dc, tc=tc_p, ypar=ypar):
                v.tensor_tensor(x2[:, dc * SQ + tc * 512: dc * SQ + tc * 512 + 512],
                                x2[:, dc * SQ + tc * 512: dc * SQ + tc * 512 + 512],
                                y_ps[ypar][:, :], ALU.add).then_inc(dve_c.h, 1)
            pl.em("v", f); dve_c.inc()
            ck_ydone[ypar] = dve_c.n
        if tc_p == 3:
            ck_wuse[e_p] = pe_c.n
            pos = ESEQ.index(e_p)
            if pos + 2 < NE:
                ck_wld_c[ESEQ[pos + 2]] = emit_wload(ESEQ[pos + 2], ck_wuse[e_p])

    kctr = [0]

    def emit_c_unit(e_, tc):
        nonlocal pending_y
        buf = e_ % 2
        wb1 = buf * 8 * DFF
        wb2 = buf * 4 * D
        k = kctr[0]; kctr[0] += 1
        hbuf = k % 2
        hb = hbuf * 4 * 512
        pl.wait("t", *ck_wld_c[e_])
        if e_ < 16:
            gbuf = k % 2
            pl.wait("s", dve_b, ck_gaug[tc])
            pl.wait("s", dve_c, ck_gb_free.get(gbuf, 0))

            def f(s, e_=e_, tc=tc, gbuf=gbuf):
                s.dma_start(out=gb_sb[:, gbuf * 512:(gbuf + 1) * 512],
                            in_=bass.AP(gaugT, e_ * SQ + tc * 512, [[SQ, 1], [0, 128], [1, 512]])
                            ).then_inc(gbd[gbuf].h, 16)
            pl.em("s", f); gbd[gbuf].inc(16)
            ck_gbd[k] = (gbd[gbuf], gbd[gbuf].n)

        for sv in ck_lnx2[tc]:
            pl.wait("t", *sv)
        for fc in range(4):
            spar = fc % 2
            pl.wait("t", act_c, ck_silu_par.get(spar, 0))
            pl.wait("t", dve_c, ck_b_par.get(spar, 0))

            def f(t, wb1=wb1, fc=fc, tc=tc, spar=spar):
                for dc in range(8):
                    t.matmul(a_ps[spar][:, :], w1_s[:, wb1 + dc * DFF + fc * 128: wb1 + dc * DFF + fc * 128 + 128],
                             lnx2[:, dc * SQ + tc * 512: dc * SQ + tc * 512 + 512],
                             start=(dc == 0), stop=(dc == 7))
                ins = None
                for dc in range(8):
                    ins = t.matmul(b_ps[spar][:, :], w3_s[:, wb1 + dc * DFF + fc * 128: wb1 + dc * DFF + fc * 128 + 128],
                                   lnx2[:, dc * SQ + tc * 512: dc * SQ + tc * 512 + 512],
                                   start=(dc == 0), stop=(dc == 7))
                ins.then_inc(pe_c.h, 1)
            pl.em("t", f); pe_c.inc()
            ck_ab = pe_c.n

            pl.wait("s", pe_c, ck_ab)
            pl.wait("s", dve_c, ck_silu_cons.get(spar, 0))

            def f(s, e_=e_, fc=fc, spar=spar):
                s.activation(silu_s[:, spar * 512:(spar + 1) * 512], a_ps[spar][:, :], AF.Silu,
                             bias=b1s_s[:, e_ * 4 + fc: e_ * 4 + fc + 1], scale=1.0
                             ).then_inc(act_c.h, 1)
            pl.em("s", f); act_c.inc()
            ck_silu_par[spar] = act_c.n

            pl.wait("v", act_c, ck_silu_par[spar])
            if e_ < 16:
                pl.wait("v", *ck_gbd[k])
            pl.wait("v", pe_c, ck_huse.get(hbuf, 0))

            def f(v, e_=e_, fc=fc, hb=hb, spar=spar, gbuf=k % 2):
                if e_ < 16:
                    hsl = h_s[:, hb + fc * 512: hb + (fc + 1) * 512]
                    v.scalar_tensor_tensor(hsl, b_ps[spar][:, :], b3s_s[:, e_ * 4 + fc: e_ * 4 + fc + 1],
                                           gb_sb[:, gbuf * 512:(gbuf + 1) * 512], ALU.add, ALU.mult)
                    v.tensor_tensor(hsl, hsl, silu_s[:, spar * 512:(spar + 1) * 512], ALU.mult
                                    ).then_inc(dve_c.h, 1)
                else:
                    v.scalar_tensor_tensor(h_s[:, hb + fc * 512: hb + (fc + 1) * 512],
                                           b_ps[spar][:, :], b3s_s[:, e_ * 4 + fc: e_ * 4 + fc + 1],
                                           silu_s[:, spar * 512:(spar + 1) * 512], ALU.add, ALU.mult
                                           ).then_inc(dve_c.h, 1)
            pl.em("v", f); dve_c.inc()
            ck_b_par[spar] = dve_c.n
            ck_silu_cons[spar] = dve_c.n
            if e_ < 16 and fc == 3:
                ck_gb_free[k % 2] = dve_c.n
        h_ready = dve_c.n

        if pending_y is not None:
            emit_y(pending_y)
        pending_y = (e_, tc, hb, wb2, hbuf, h_ready)

    for tc in range(3):
        emit_b2_loads(tc + 1)
        emit_c_unit(16, tc)
        emit_b2_half(tc + 1, 0, 0)
        emit_b2_half(tc + 1, 0, 1)
        emit_c_unit(17, tc)
        emit_b2_half(tc + 1, 1, 0)
        emit_b2_half(tc + 1, 1, 1)
    emit_c_unit(16, 3)
    emit_b2_gate(0)
    emit_c_unit(17, 3)
    emit_b2_gate(1)
    emit_c_unit(0, 0)
    emit_b2_gate(2)
    emit_c_unit(0, 1)
    emit_b2_gate(3)
    emit_c_unit(0, 2)
    emit_c_unit(0, 3)
    for e_ in range(1, 16):
        for tc in range(4):
            emit_c_unit(e_, tc)

    emit_y(pending_y)
    pending_y = None

    for tc in range(4):
        pl.wait("t", dve_b, ck_gaug[tc])
        for dc in range(8):
            ypar = dc % 2
            pl.wait("t", dve_c, ck_ydone.get(ypar, 0))

            def f(t, dc=dc, tc=tc, ypar=ypar):
                t.matmul(y_ps[ypar][:, :], b2a_s[0:17, dc * 128:(dc + 1) * 128],
                         gaugT[0:17, tc * 512:(tc + 1) * 512], start=True, stop=True
                         ).then_inc(pe_c.h, 1)
            pl.em("t", f); pe_c.inc()
            mmn = pe_c.n

            pl.wait("v", pe_c, mmn)

            def f(v, dc=dc, tc=tc, ypar=ypar):
                v.tensor_tensor(x2[:, dc * SQ + tc * 512: dc * SQ + tc * 512 + 512],
                                x2[:, dc * SQ + tc * 512: dc * SQ + tc * 512 + 512],
                                y_ps[ypar][:, :], ALU.add).then_inc(dve_c.h, 1)
            pl.em("v", f); dve_c.inc()
            ck_ydone[ypar] = dve_c.n

            pl.wait("y", dve_c, ck_ydone[ypar])

            def f(sy, dc=dc, tc=tc):
                sy.dma_start(out=y[dc][:, tc * 512:(tc + 1) * 512],
                             in_=x2[:, dc * SQ + tc * 512: dc * SQ + tc * 512 + 512]
                             ).then_inc(sty.h, 16)
            pl.em("y", f); sty.inc(16)

    C_DVE_FINAL = dve_c.n

    with nc.Block("phaseBC") as blk:
        @blk.sync
        def _(sync):
            pl.run("y", sync)
            for s_ in ldb2:
                sync.wait_ge(s_.h, s_.n)

        @blk.gpsimd
        def _(gp):
            pl.run("g", gp)

        @blk.tensor
        def _(t):
            pl.run("t", t)

        @blk.vector
        def _(v):
            pl.run("v", v)

        @blk.scalar
        def _(s):
            pl.run("s", s)

    psC.close()

    with nc.Block("phaseD") as blk:
        @blk.sync
        def _(sync):
            sync.wait_ge(dve_c.h, C_DVE_FINAL)
            for s_ in ldc + gbd:
                sync.wait_ge(s_.h, s_.n)
            if "k" in DEBUG_TAPS:
                sync.dma_start(out=taps["k"], in_=kscr).then_inc(sty.h, 16); sty.inc(16)
            if "atn" in DEBUG_TAPS:
                sync.dma_start(out=taps["atn"], in_=ascr).then_inc(sty.h, 16); sty.inc(16)
            if "lnx2" in DEBUG_TAPS:
                sync.dma_start(out=taps["lnx2"], in_=lnx2[:, :]).then_inc(sty.h, 16); sty.inc(16)
            if "gate" in DEBUG_TAPS:
                sync.dma_start(out=taps["gate"], in_=gaugT[:, :]).then_inc(sty.h, 16); sty.inc(16)
            sync.wait_ge(sty.h, sty.n)

    ctxC.close()
    ctx.close()
    return nc


def _stage(inputs):
    f32 = np.float32
    x = np.asarray(inputs["L0_attn_x"], f32)
    Wq = np.asarray(inputs["L0_attn_Wq"], f32)
    Wk = np.asarray(inputs["L0_attn_Wk"], f32)
    Wv = np.asarray(inputs["L0_attn_Wv"], f32)
    tbl = np.asarray(inputs["L0_attn_rel_bias_table"], f32)
    gam = np.asarray(inputs["L0_attn_ln_gamma"], f32)
    bet = np.asarray(inputs["L0_attn_ln_beta"], f32)
    gate_w = np.asarray(inputs["L0_ffn_gate_w"], f32)
    eb = np.asarray(inputs["L0_ffn_expert_bias"], f32)
    w1 = np.asarray(inputs["L0_ffn_w1"], f32)
    b1 = np.asarray(inputs["L0_ffn_b1"], f32)
    w2 = np.asarray(inputs["L0_ffn_w2"], f32)
    b2 = np.asarray(inputs["L0_ffn_b2"], f32)
    w3 = np.asarray(inputs["L0_ffn_w3"], f32)
    b3 = np.asarray(inputs["L0_ffn_b3"], f32)
    sw1 = np.asarray(inputs["L0_ffn_sw1"], f32)
    sb1 = np.asarray(inputs["L0_ffn_sb1"], f32)
    sw2 = np.asarray(inputs["L0_ffn_sw2"], f32)
    sb2 = np.asarray(inputs["L0_ffn_sb2"], f32)
    sw3 = np.asarray(inputs["L0_ffn_sw3"], f32)
    sb3 = np.asarray(inputs["L0_ffn_sb3"], f32)

    shared = {}
    shared["wqt"] = np.ascontiguousarray(Wq.T)
    shared["wkt"] = np.ascontiguousarray(Wk.T)
    shared["wvt"] = np.ascontiguousarray(Wv.T)
    shared["gwt"] = np.ascontiguousarray(gate_w.T)
    w1a = np.concatenate([w1, sw1.reshape(2, 512, D)], 0)
    w3a = np.concatenate([w3, sw3.reshape(2, 512, D)], 0)
    w2a = np.concatenate([w2.transpose(0, 2, 1), sw2.T.reshape(2, 512, D)], 0)
    shared["w1t"] = np.ascontiguousarray(w1a.transpose(0, 2, 1))
    shared["w3t"] = np.ascontiguousarray(w3a.transpose(0, 2, 1))
    shared["w2t"] = np.ascontiguousarray(w2a)
    b1a = np.concatenate([b1, sb1.reshape(2, 512)], 0)
    b3a = np.concatenate([b3, sb3.reshape(2, 512)], 0)
    shared["b1s"] = np.ascontiguousarray(b1a.reshape(NE, 4, 128).transpose(0, 2, 1))
    shared["b3s"] = np.ascontiguousarray(b3a.reshape(NE, 4, 128).transpose(0, 2, 1))
    shared["b2a"] = np.ascontiguousarray(np.concatenate([b2, sb2[None]], 0))
    shared["ebias"] = np.ascontiguousarray(eb)
    shared["gam8"] = np.ascontiguousarray(gam.reshape(8, 128).T)
    shared["bet8"] = np.ascontiguousarray(bet.reshape(8, 128).T)

    ext = np.ascontiguousarray(tbl[np.minimum(np.arange(2 * S - 1), tbl.shape[0] - 1)].T)
    extw_by_q0 = {}
    for Q0 in (0, SQ):
        ew = np.empty((NH, 128, EW), f32)
        for h in range(NH):
            sw = np.lib.stride_tricks.sliding_window_view(ext[h], EW)
            ew[h] = sw[Q0 + 127 - np.arange(128)]
        extw_by_q0[Q0] = np.ascontiguousarray(ew)

    in_maps = []
    for core in range(8):
        b = core // 2
        Q0 = (core % 2) * SQ
        xtb = np.ascontiguousarray(x[b].T)
        m = dict(shared)
        m["chs"] = np.ascontiguousarray(tbl[4094, :].reshape(1, 16))
        m["xt"] = xtb
        m["xq"] = np.ascontiguousarray(xtb[:, Q0:Q0 + SQ])
        m["extw"] = extw_by_q0[Q0]
        in_maps.append(m)
    return in_maps


_NC_CACHE = {}


def _get_nc():
    key = (tuple(DEBUG_TAPS), CACHE_SALT)
    if key not in _NC_CACHE:
        _NC_CACHE[key] = build_nc()
    return _NC_CACHE[key]


def run_cores(inputs, core_ids=None, **kw):
    in_maps = _stage(inputs)
    if core_ids is None:
        core_ids = list(range(8))
    nc = _get_nc()
    return run_bass_kernel_spmd(nc, [in_maps[c] for c in core_ids], core_ids=core_ids, **kw)


def kernel(**inputs):
    res = run_cores(inputs)
    out = np.empty((4, S, D), np.float32)
    for core in range(8):
        b = core // 2
        Q0 = (core % 2) * SQ
        yfm = res.results[core]["y"].reshape(D, SQ)
        out[b, Q0:Q0 + SQ, :] = yfm.T
    return out


# version 51
# speedup vs baseline: 1.0321x; 1.0042x over previous
import numpy as np
from contextlib import ExitStack

import concourse.bass as bass
import concourse.mybir as mybir
from concourse.bass_utils import run_bass_kernel_spmd

F32 = mybir.dt.float32
BF16 = mybir.dt.bfloat16
U32 = mybir.dt.uint32
AF = mybir.ActivationFunctionType
ALU = mybir.AluOpType

D = 1024
S = 4096
SQ = 2048
NH = 16
HD = 64
WA = 256
NWKV = S // WA
NWQ = SQ // WA
NKC = S // 128
EW = 6016
NE = 18
DFF = 512
EPS = 1e-5

DEBUG_TAPS = ()
CACHE_SALT = 1


class Sem:
    def __init__(self, ctx, nc, name):
        self.h = ctx.enter_context(nc.semaphore(name))
        self.name = name
        self.n = 0

    def inc(self, k=1):
        self.n += k
        return self.n


class Plan:
    def __init__(self):
        self.items = []
        self.waited = {}

    def em(self, eng, fn):
        self.items.append((eng, fn))

    def wait(self, eng, sem, val):
        if val <= 0:
            return
        key = (eng, sem.name)
        if self.waited.get(key, 0) >= val:
            return
        self.waited[key] = val
        h = sem.h

        def fn(e, h=h, val=val):
            e.wait_ge(h, val)
        self.items.append((eng, fn))

    def run(self, eng, obj):
        for en, fn in self.items:
            if en == eng:
                fn(obj)


def build_nc():
    nc = bass.Bass("TRN2", target_bir_lowering=False)

    xt = nc.dram_tensor("xt", [D, S], F32, kind="ExternalInput").ap()
    xq = nc.dram_tensor("xq", [D, SQ], F32, kind="ExternalInput").ap()
    wqt = nc.dram_tensor("wqt", [D, D], F32, kind="ExternalInput").ap()
    wkt = nc.dram_tensor("wkt", [D, D], F32, kind="ExternalInput").ap()
    wvt = nc.dram_tensor("wvt", [D, D], F32, kind="ExternalInput").ap()
    extw = nc.dram_tensor("extw", [NH, 128, EW], F32, kind="ExternalInput").ap()
    chs = nc.dram_tensor("chs", [1, 16], F32, kind="ExternalInput").ap()
    gwt = nc.dram_tensor("gwt", [D, 16], F32, kind="ExternalInput").ap()
    w1t = nc.dram_tensor("w1t", [NE, D, DFF], F32, kind="ExternalInput").ap()
    w3t = nc.dram_tensor("w3t", [NE, D, DFF], F32, kind="ExternalInput").ap()
    w2t = nc.dram_tensor("w2t", [NE, DFF, D], F32, kind="ExternalInput").ap()
    b1s = nc.dram_tensor("b1s", [NE, 128, 4], F32, kind="ExternalInput").ap()
    b3s = nc.dram_tensor("b3s", [NE, 128, 4], F32, kind="ExternalInput").ap()
    b2a = nc.dram_tensor("b2a", [17, D], F32, kind="ExternalInput").ap()
    ebias = nc.dram_tensor("ebias", [16], F32, kind="ExternalInput").ap()
    gam8 = nc.dram_tensor("gam8", [128, 8], F32, kind="ExternalInput").ap()
    bet8 = nc.dram_tensor("bet8", [128, 8], F32, kind="ExternalInput").ap()

    y = nc.dram_tensor("y", [8, 128, SQ], F32, kind="ExternalOutput").ap()
    taps = {}
    for t_ in DEBUG_TAPS:
        shp, dt = {
            "k": ([8, 128, S], BF16), "atn": ([8, 128, SQ], F32),
            "kdup": ([128, 2 * S], BF16), "qdup": ([128, 2 * SQ], BF16),
            "aow": ([128, 8 * 512], F32), "gdbg": ([128, 36], F32),
            "extb": ([128, 2 * EW], BF16),
            "x2": ([128, 8 * SQ], F32), "lnx2": ([128, 8 * SQ], BF16),
            "gate": ([17, SQ], BF16), "logits": ([16, SQ], F32),
            "q": ([128, 8 * SQ], BF16), "v": ([128, 32 * 1040 + 64], BF16),
        }[t_]
        taps[t_] = nc.dram_tensor("tap_" + t_, shp, dt, kind="ExternalOutput").ap()

    kscr = nc.dram_tensor("kscr", [8, 128, S], BF16, kind="Internal").ap()
    ascr = nc.dram_tensor("ascr", [8, 128, SQ], F32, kind="Internal").ap()

    ctx = ExitStack()
    e0 = ctx.enter_context

    ones1f = e0(nc.sbuf_tensor("ones1f", [1, 128], F32))
    negones1f = e0(nc.sbuf_tensor("negones1f", [1, 128], F32))
    ones128f = e0(nc.sbuf_tensor("ones128f", [128, 1], F32))
    ones128b = e0(nc.sbuf_tensor("ones128b", [128, 1], BF16))
    identf = e0(nc.sbuf_tensor("identf", [128, 128], F32))
    iota16 = e0(nc.sbuf_tensor("iota16", [128, 16], F32))
    gam8s = e0(nc.sbuf_tensor("gam8s", [128, 8], F32))
    bet8s = e0(nc.sbuf_tensor("bet8s", [128, 8], F32))
    eb_s = e0(nc.sbuf_tensor("eb_s", [16, 1], F32))
    gwt_s = e0(nc.sbuf_tensor("gwt_s", [128, 128], BF16))
    epsc = e0(nc.sbuf_tensor("epsc", [1, 1], F32))
    b1s_s = e0(nc.sbuf_tensor("b1s_s", [128, NE * 4], F32))
    b3s_s = e0(nc.sbuf_tensor("b3s_s", [128, NE * 4], F32))

    sems = {}
    for name in ["ldi", "ldig", "gpi", "vvi", "dve_a", "act_a", "pe_a",
                 "pe_sc", "act_x", "pe_av", "pe_bc", "dve_ep", "act_ep",
                 "dve_b", "act_b", "pe_b", "pool_b",
                 "dve_c", "act_c", "pe_c", "sty",
                 "lda0", "lda1", "lda2", "kst0", "kst1", "ldh0", "ldh1",
                 "ldx0", "ldx1", "ast0", "ast1", "ldb0", "ldb1", "ldc0", "ldc1",
                 "gbd0", "gbd1"]:
        sems[name] = Sem(ctx, nc, name)
    ldi = sems["ldi"]
    ldig = sems["ldig"]
    gpi, vvi = sems["gpi"], sems["vvi"]
    dve_a, act_a, pe_a = sems["dve_a"], sems["act_a"], sems["pe_a"]
    pe_sc, act_x, pe_av = sems["pe_sc"], sems["act_x"], sems["pe_av"]
    pe_bc, dve_ep, act_ep = sems["pe_bc"], sems["dve_ep"], sems["act_ep"]
    dve_b, act_b, pe_b = sems["dve_b"], sems["act_b"], sems["pe_b"]
    pool_b = sems["pool_b"]
    dve_c, act_c, pe_c, sty = sems["dve_c"], sems["act_c"], sems["pe_c"], sems["sty"]
    lda = [sems["lda0"], sems["lda1"], sems["lda2"]]
    kst = [sems["kst0"], sems["kst1"]]
    ldh = [sems["ldh0"], sems["ldh1"]]
    ldx = [sems["ldx0"], sems["ldx1"]]
    ast = [sems["ast0"], sems["ast1"]]
    ldb2 = [sems["ldb0"], sems["ldb1"]]
    ldc = [sems["ldc0"], sems["ldc1"]]
    gbd = [sems["gbd0"], sems["gbd1"]]

    ctxAB = ExitStack()
    eAB = ctxAB.enter_context
    v_sb = eAB(nc.sbuf_tensor("v_sb", [128, 32 * 1040 + 64], BF16))
    q_sb = eAB(nc.sbuf_tensor("q_sb", [128, 8 * SQ], BF16))

    ctxA = ExitStack()
    eA = ctxA.enter_context
    wq_s = eA(nc.sbuf_tensor("wq_s", [128, 8 * D], BF16))
    wk_s = eA(nc.sbuf_tensor("wk_s", [128, 8 * D], BF16))
    wv_s = eA(nc.sbuf_tensor("wv_s", [128, 8 * D], BF16))
    xw = eA(nc.sbuf_tensor("xw", [128, 3 * 8 * WA], F32))
    lnw = eA(nc.sbuf_tensor("lnw", [128, 2 * 8 * WA], BF16))
    sqw = eA(nc.sbuf_tensor("sqw", [128, 8 * WA], BF16))
    tmpa = eA(nc.sbuf_tensor("tmpa", [128, 8 * WA], F32))
    kstg = eA(nc.sbuf_tensor("kstg", [128, 2 * 8 * WA], BF16))
    murow = eA(nc.sbuf_tensor("murow", [1, WA], F32))
    musq = eA(nc.sbuf_tensor("musq", [1, WA], F32))
    varrow = eA(nc.sbuf_tensor("varrow", [1, WA], F32))
    rstdrow = eA(nc.sbuf_tensor("rstdrow", [1, WA], F32))

    psA = ExitStack()
    eP = psA.enter_context
    sx_ps = eP(nc.psum_tensor("sx_ps", [1, WA], F32))
    sq_ps = eP(nc.psum_tensor("sq_ps", [1, WA], F32))
    rb_ps = eP(nc.psum_tensor("rb_ps", [128, WA], F32))
    mb_ps = eP(nc.psum_tensor("mb_ps", [128, WA], F32))
    kp_ps = [eP(nc.psum_tensor("kp_ps%d" % i, [128, WA], F32)) for i in range(2)]
    vp_ps = [eP(nc.psum_tensor("vp_ps%d" % i, [128, 512], F32)) for i in range(2)]

    with nc.Block("init") as blk:
        @blk.sync
        def _(sync):
            sync.dma_start(out=gam8s[:, :], in_=gam8).then_inc(ldi.h, 16); ldi.inc(16)
            sync.dma_start(out=bet8s[:, :], in_=bet8).then_inc(ldi.h, 16); ldi.inc(16)
            sync.dma_start(out=eb_s[:, 0:1], in_=ebias[:, None]).then_inc(ldi.h, 16); ldi.inc(16)
            sync.dma_start(out=b1s_s[:, :].rearrange("p (e c) -> p e c", e=NE),
                           in_=b1s.rearrange("e p c -> p e c")).then_inc(ldi.h, 16); ldi.inc(16)
            sync.dma_start(out=b3s_s[:, :].rearrange("p (e c) -> p e c", e=NE),
                           in_=b3s.rearrange("e p c -> p e c")).then_inc(ldi.h, 16); ldi.inc(16)

        @blk.gpsimd
        def _(gp):
            gp.dma_start(out=wq_s[:, :].rearrange("p (dc e) -> p dc e", dc=8),
                         in_=wqt.rearrange("(dc p) e -> p dc e", p=128)).then_inc(ldig.h, 16); ldig.inc(16)
            gp.dma_start(out=wk_s[:, :].rearrange("p (dc e) -> p dc e", dc=8),
                         in_=wkt.rearrange("(dc p) e -> p dc e", p=128)).then_inc(ldig.h, 16); ldig.inc(16)
            gp.dma_start(out=wv_s[:, :].rearrange("p (dc e) -> p dc e", dc=8),
                         in_=wvt.rearrange("(dc p) e -> p dc e", p=128)).then_inc(ldig.h, 16); ldig.inc(16)
            gp.dma_start(out=gwt_s[:, :].rearrange("p (dc e) -> p dc e", dc=8),
                         in_=gwt.rearrange("(dc p) e -> p dc e", p=128)).then_inc(ldig.h, 16); ldig.inc(16)
            gp.memset(ones1f[0:1, :], 1.0)
            gp.memset(negones1f[0:1, :], -1.0)
            gp.memset(epsc[0:1, 0:1], EPS)
            gp.memset(ones128f[:, 0:1], 1.0)
            gp.memset(ones128b[:, 0:1], 1.0)
            gp.memset(identf[:, :], 0.0).then_inc(gpi.h, 1)
            gp.wait_ge(gpi.h, 1)
            gp.affine_select(out=identf[:, :], in_=identf[:, :], compare_op=ALU.not_equal,
                             fill=1.0, base=0, pattern=[[-1, 128]], channel_multiplier=1)
            gp.iota(iota16[:, :], [[1, 16]], channel_multiplier=0,
                    allow_small_or_imprecise_dtypes=True).then_inc(gpi.h, 1)

        @blk.vector
        def _(v):
            ins = None
            for tc in range(32):
                ins = v.memset(bass.AP(v_sb, tc * 1040 + 64, [[32 * 1040 + 64, 128], [65, 16], [1, 1]]), 1.0)
            ins.then_inc(vvi.h, 1)
            for _i in range(CACHE_SALT):
                v.engine_nop()

    gpi.inc(2)
    vvi.inc()
    LDI_FINAL = ldi.n
    LDIG_FINAL = ldig.n

    pl = Plan()
    ck_apply = {}
    ck_stats = {}
    ck_qkv = {}
    ck_var = {}
    ck_bcast = {}
    ck_kevac = {}
    ck_kdma = {}
    ck_evac_kq = {}
    ck_vevac = {}

    for eng in ("t", "v", "s", "y"):
        pl.wait(eng, ldi, LDI_FINAL)
        pl.wait(eng, ldig, LDIG_FINAL)
        pl.wait(eng, gpi, 2)
        pl.wait(eng, vvi, 1)

    NW = NWKV + NWQ

    def st_dma(w):
        isq = w >= NWKV
        src_ = xq if isq else xt
        col0 = ((w - NWKV) if isq else w) * WA
        ncols = SQ if isq else S
        xbuf = (w % 3) * 8 * WA
        pl.wait("y", dve_a, ck_apply.get(w - 3, 0))

        def f(sy, src_=src_, col0=col0, ncols=ncols, xbuf=xbuf):
            sy.dma_start(out=bass.AP(xw, xbuf, [[3 * 8 * WA, 128], [WA, 8], [1, WA]]),
                         in_=bass.AP(src_.tensor, col0, [[ncols, 128], [128 * ncols, 8], [1, WA]])
                         ).then_inc(lda[w % 3].h, 16)
        pl.em("y", f); lda[w % 3].inc(16)
        ck_lda[w] = (lda[w % 3], lda[w % 3].n)

    def st_chain(w):
        xbuf = (w % 3) * 8 * WA
        pl.wait("v", *ck_lda[w])
        pl.wait("v", pe_a, ck_stats.get(w - 1, 0))

        def f(v, xbuf=xbuf):
            v.tensor_tensor(sqw[:, :], xw[:, xbuf:xbuf + 8 * WA],
                            xw[:, xbuf:xbuf + 8 * WA], ALU.mult).then_inc(dve_a.h, 1)
        pl.em("v", f); dve_a.inc()
        ck_sq = dve_a.n

        pl.wait("t", *ck_lda[w])
        pl.wait("t", dve_a, ck_sq)
        pl.wait("t", dve_a, ck_var.get(w - 1, 0))

        def f(t, xbuf=xbuf):
            for c in range(8):
                t.matmul(sx_ps[0:1, :], ones128f[:, 0:1], xw[:, xbuf + c * WA: xbuf + (c + 1) * WA],
                         start=(c == 0), stop=(c == 7))
            ins = None
            for c in range(8):
                ins = t.matmul(sq_ps[0:1, :], ones128b[:, 0:1], sqw[:, c * WA:(c + 1) * WA],
                               start=(c == 0), stop=(c == 7))
            ins.then_inc(pe_a.h, 1)
        pl.em("t", f); pe_a.inc()
        ck_stats[w] = pe_a.n

        pl.wait("v", pe_a, ck_stats[w])
        pl.wait("v", pe_a, ck_bcast.get(w - 1, 0))

        def f(v):
            v.tensor_scalar(murow[0:1, :], sx_ps[0:1, :], 1.0 / D, None, ALU.mult).then_inc(dve_a.h, 1)
        pl.em("v", f); dve_a.inc()
        pl.wait("v", dve_a, dve_a.n)

        def f(v):
            v.tensor_tensor(musq[0:1, :], murow[0:1, :], murow[0:1, :], ALU.mult).then_inc(dve_a.h, 1)
        pl.em("v", f); dve_a.inc()
        pl.wait("v", dve_a, dve_a.n)

        def f(v):
            v.scalar_tensor_tensor(varrow[0:1, :], sq_ps[0:1, :], 1.0 / D, musq[0:1, :],
                                   ALU.mult, ALU.subtract).then_inc(dve_a.h, 1)
        pl.em("v", f); dve_a.inc()
        ck_var[w] = dve_a.n

        pl.wait("s", dve_a, ck_var[w])

        def f(s):
            s.activation(varrow[0:1, :], varrow[0:1, :], AF.Ln, bias=epsc[0:1, 0:1], scale=1.0
                         ).then_inc(act_a.h, 1)
        pl.em("s", f); act_a.inc()
        pl.wait("s", act_a, act_a.n)

        def f(s):
            s.activation(rstdrow[0:1, :], varrow[0:1, :], AF.Exp, bias=0.0, scale=-0.5
                         ).then_inc(act_a.h, 1)
        pl.em("s", f); act_a.inc()
        ck_negmur[w] = act_a.n

    def st_apply(w):
        xbuf = (w % 3) * 8 * WA
        lbuf = (w % 2) * 8 * WA
        pl.wait("t", act_a, ck_negmur[w])
        pl.wait("t", dve_a, ck_apply.get(w - 1, 0))

        def f(t):
            t.matmul(rb_ps[:, :], ones1f[0:1, :], rstdrow[0:1, :], start=True, stop=True)
            t.matmul(mb_ps[:, :], negones1f[0:1, :], murow[0:1, :], start=True, stop=True
                     ).then_inc(pe_a.h, 1)
        pl.em("t", f); pe_a.inc()
        ck_bcast[w] = pe_a.n

        pl.wait("v", pe_a, ck_bcast[w])
        pl.wait("v", pe_a, ck_qkv.get(w - 2, 0))

        def f(v, xbuf=xbuf, lbuf=lbuf):
            v.tensor_tensor(bass.AP(tmpa, 0, [[8 * WA, 128], [WA, 8], [1, WA]]),
                            bass.AP(xw, xbuf, [[3 * 8 * WA, 128], [WA, 8], [1, WA]]),
                            bass.AP(mb_ps, 0, [[WA, 128], [0, 8], [1, WA]]), ALU.add)
            v.tensor_tensor(bass.AP(lnw, lbuf, [[2 * 8 * WA, 128], [WA, 8], [1, WA]]),
                            bass.AP(tmpa, 0, [[8 * WA, 128], [WA, 8], [1, WA]]),
                            bass.AP(rb_ps, 0, [[WA, 128], [0, 8], [1, WA]]), ALU.mult
                            ).then_inc(dve_a.h, 1)
        pl.em("v", f); dve_a.inc()
        ck_apply[w] = dve_a.n

    def st_qkv(w):
        isq = w >= NWKV
        wq_i = w - NWKV
        lbuf = (w % 2) * 8 * WA
        pl.wait("t", dve_a, ck_apply[w])
        if isq:
            for ec in range(8):
                par = ec % 2
                pl.wait("t", act_a, ck_evac_kq.get(par, 0))

                def f(t, ec=ec, lbuf=lbuf, par=par):
                    ins = None
                    for dc in range(8):
                        ins = t.matmul(kp_ps[par][:, :], wq_s[:, dc * D + ec * 128: dc * D + ec * 128 + 128],
                                       lnw[:, lbuf + dc * WA: lbuf + (dc + 1) * WA],
                                       start=(dc == 0), stop=(dc == 7))
                    ins.then_inc(pe_a.h, 1)
                pl.em("t", f); pe_a.inc()
                mmn = pe_a.n

                pl.wait("s", pe_a, mmn)

                def f(s, ec=ec, wq_i=wq_i, par=par):
                    s.copy(q_sb[:, ec * SQ + wq_i * WA: ec * SQ + (wq_i + 1) * WA],
                           kp_ps[par][:, :]).then_inc(act_a.h, 1)
                pl.em("s", f); act_a.inc()
                ck_evac_kq[par] = act_a.n
            ck_qkv[w] = pe_a.n
        else:
            kbuf = (w % 2) * 8 * WA
            if w - 2 in ck_kdma:
                pl.wait("s", *ck_kdma[w - 2])
            for ec in range(8):
                par = ec % 2
                pl.wait("t", act_a, ck_evac_kq.get(par, 0))

                def f(t, ec=ec, lbuf=lbuf, par=par):
                    ins = None
                    for dc in range(8):
                        ins = t.matmul(kp_ps[par][:, :], wk_s[:, dc * D + ec * 128: dc * D + ec * 128 + 128],
                                       lnw[:, lbuf + dc * WA: lbuf + (dc + 1) * WA],
                                       start=(dc == 0), stop=(dc == 7))
                    ins.then_inc(pe_a.h, 1)
                pl.em("t", f); pe_a.inc()
                mmn = pe_a.n

                pl.wait("s", pe_a, mmn)

                def f(s, ec=ec, kbuf=kbuf, par=par):
                    s.copy(kstg[:, kbuf + ec * WA: kbuf + (ec + 1) * WA],
                           kp_ps[par][:, :]).then_inc(act_a.h, 1)
                pl.em("s", f); act_a.inc()
                ck_evac_kq[par] = act_a.n
            ck_kevac = act_a.n

            pl.wait("y", act_a, ck_kevac)

            def f(sy, w=w, kbuf=kbuf):
                sy.dma_start(out=bass.AP(kscr.tensor, w * WA, [[S, 128], [128 * S, 8], [1, WA]]),
                             in_=bass.AP(kstg, kbuf, [[2 * 8 * WA, 128], [WA, 8], [1, WA]])
                             ).then_inc(kst[w % 2].h, 16)
            pl.em("y", f); kst[w % 2].inc(16)
            ck_kdma[w] = (kst[w % 2], kst[w % 2].n)

            for tc2 in range(2):
                tile = 2 * w + tc2
                for half in range(2):
                    pl.wait("t", act_a, ck_vevac.get(half, 0))

                    def f(t, lbuf=lbuf, tc2=tc2, half=half):
                        ins = None
                        for dc in range(8):
                            ins = t.matmul(vp_ps[half][:, :],
                                           lnw[:, lbuf + dc * WA + tc2 * 128: lbuf + dc * WA + tc2 * 128 + 128],
                                           wv_s[:, dc * D + half * 512: dc * D + (half + 1) * 512],
                                           start=(dc == 0), stop=(dc == 7))
                        ins.then_inc(pe_a.h, 1)
                    pl.em("t", f); pe_a.inc()
                    mmn = pe_a.n

                    pl.wait("s", pe_a, mmn)

                    def f(s, tile=tile, half=half):
                        s.copy(
                            bass.AP(v_sb, tile * 1040 + half * 8 * 65, [[32 * 1040 + 64, 128], [65, 8], [1, 64]]),
                            bass.AP(vp_ps[half], 0, [[512, 128], [64, 8], [1, 64]])
                        ).then_inc(act_a.h, 1)
                    pl.em("s", f); act_a.inc()
                    ck_vevac[half] = act_a.n
            ck_qkv[w] = pe_a.n

    ck_lda = {}
    ck_negmur = {}
    for it in range(NW + 3):
        if it < NW:
            st_dma(it)
        if 2 <= it <= NW + 1:
            st_apply(it - 2)
        if 3 <= it <= NW + 2:
            st_qkv(it - 3)
        if 1 <= it <= NW:
            st_chain(it - 1)

    A_PE_FINAL = pe_a.n
    A_DVE_FINAL = dve_a.n
    A_ACT_FINAL = act_a.n

    with nc.Block("phaseA") as blk:
        @blk.sync
        def _(sync):
            pl.run("y", sync)
            for s_ in lda:
                sync.wait_ge(s_.h, s_.n)
            for s_ in kst:
                sync.wait_ge(s_.h, s_.n)

        @blk.tensor
        def _(t):
            pl.run("t", t)

        @blk.vector
        def _(v):
            pl.run("v", v)

        @blk.scalar
        def _(s):
            pl.run("s", s)

    ctxA.close()
    psA.close()

    ctxB = ExitStack()
    eB = ctxB.enter_context
    kdup = eB(nc.sbuf_tensor("kdup", [128, 2 * S], BF16))
    qdup = eB(nc.sbuf_tensor("qdup", [128, 2 * SQ], BF16))
    extb = eB(nc.sbuf_tensor("extb", [128, 2 * EW], BF16))
    pbuf = eB(nc.sbuf_tensor("pbuf", [128, 4 * 1024], BF16))
    avsb = eB(nc.sbuf_tensor("avsb", [65, 512], F32))
    lrow = eB(nc.sbuf_tensor("lrow", [1, 512], F32))
    lrinv = eB(nc.sbuf_tensor("lrinv", [64, 512], F32))
    atn = eB(nc.sbuf_tensor("atn", [64, 2 * SQ], F32))
    pgs = eB(nc.sbuf_tensor("pgs", [1, 3 * 2048], BF16))
    chs_sb = eB(nc.sbuf_tensor("chs_sb", [1, 16], F32))
    ones512 = eB(nc.sbuf_tensor("ones512", [1, 512], BF16))

    psB = ExitStack()
    eP = psB.enter_context
    sc_ps = [eP(nc.psum_tensor("sc_ps%d" % i, [128, 1024], F32)) for i in range(2)]
    av_ps = [eP(nc.psum_tensor("av_ps%d" % i, [128, 512], F32)) for i in range(2)]
    bv_ps = [eP(nc.psum_tensor("bv_ps%d" % i, [128, 512], F32)) for i in range(2)]

    pl = Plan()
    for eng in ("t", "v", "s", "y", "g"):
        for s_ in kst:
            pl.wait(eng, s_, s_.n)
        pl.wait(eng, dve_a, A_DVE_FINAL)
        pl.wait(eng, act_a, A_ACT_FINAL)
        pl.wait(eng, pe_a, A_PE_FINAL)
        pl.wait(eng, gpi, 2)
        pl.wait(eng, vvi, 1)

    pending_ep = None
    ck_sc_done_h = {}
    ck_av_done_h = {}
    ck_ldh = {}
    exp_of_scbuf = {}
    avbv_of_pbuf = {}
    ck_ep_dve = {}
    ck_ep_act = {}
    ck_ast = {}
    ck_ldx = {}

    def emit_loads(h):
        pl.wait("y", pe_sc, ck_sc_done_h.get(h - 2, 0))
        pl.wait("y", pe_av, ck_av_done_h.get(h - 2, 0))
        kb = (h % 2) * S
        qb = (h % 2) * SQ

        def f(sy, h=h, kb=kb, qb=qb):
            src = kscr.tensor
            sy.dma_start(out=kdup[0:64, kb:kb + S],
                         in_=kscr[h // 2, (h % 2) * 64:(h % 2) * 64 + 64, :]).then_inc(ldh[h % 2].h, 16)
            sy.dma_start(out=kdup[64:128, kb:kb + S],
                         in_=kscr[h // 2, (h % 2) * 64:(h % 2) * 64 + 64, :]).then_inc(ldh[h % 2].h, 16)
            qsrc = q_sb[(h % 2) * 64:(h % 2) * 64 + 64, (h // 2) * SQ:(h // 2 + 1) * SQ]
            sy.dma_start(out=qdup[0:64, qb:qb + SQ], in_=qsrc).then_inc(ldh[h % 2].h, 16)
            sy.dma_start(out=qdup[64:128, qb:qb + SQ], in_=qsrc).then_inc(ldh[h % 2].h, 16)
        pl.em("y", f)
        ldh[h % 2].inc(64)

        pl.wait("g", pe_av, ck_av_done_h.get(h - 2, 0))

        def f(gp, h=h):
            gp.dma_start(out=extb[:, (h % 2) * EW:(h % 2 + 1) * EW], in_=extw[h]).then_inc(ldx[h % 2].h, 16)
        pl.em("g", f)
        ldx[h % 2].inc(16)
        ck_ldh[h] = (ldh[h % 2], ldh[h % 2].n)
        ck_ldx[h] = (ldx[h % 2], ldx[h % 2].n)

    if "q" in DEBUG_TAPS:
        def f(sy):
            sy.dma_start(out=taps["q"], in_=q_sb[:, :]).then_inc(sty.h, 16)
        pl.em("y", f); sty.inc(16)
    if "v" in DEBUG_TAPS:
        def f(sy):
            sy.dma_start(out=taps["v"], in_=v_sb[:, :]).then_inc(sty.h, 16)
        pl.em("y", f); sty.inc(16)

    def f(gp):
        gp.dma_start(out=chs_sb[0:1, :], in_=chs).then_inc(ldx[0].h, 16)
        gp.memset(ones512[0:1, :], 1.0).then_inc(ldx[0].h, 1)
    pl.em("g", f); ldx[0].inc(17)
    for eng in ("t", "v"):
        pl.wait(eng, ldx[0], ldx[0].n)

    pp_evac = {}
    ppc = 0
    for h in range(NH):
        off = min(65 * h, 1040 - 256)
        for j in range(3):
            par = ppc % 2
            pl.wait("t", dve_ep, pp_evac.get(ppc - 2, 0))

            def f(t, j=j, off=off, par=par):
                ins = None
                for c in range(4):
                    kc = 4 * j + c
                    ins = t.matmul(bv_ps[par][0:1, 0:256], ones128b[:, 0:1],
                                   v_sb[:, kc * 1040 + off: kc * 1040 + off + 256],
                                   start=(c == 0), stop=(c == 3))
                ins.then_inc(pe_av.h, 1)
            pl.em("t", f); pe_av.inc()
            pl.wait("v", pe_av, pe_av.n)

            def f(v, h=h, j=j, off=off, par=par):
                src_ = bv_ps[par][0:1, 65 * h - off: 65 * h - off + 128]
                dst = pgs[0:1, j * 2048 + h * 128: j * 2048 + (h + 1) * 128]
                if j == 0:
                    v.tensor_scalar(dst, src_, chs_sb[0:1, h:h + 1], None,
                                    ALU.mult).then_inc(dve_ep.h, 1)
                else:
                    v.scalar_tensor_tensor(dst, src_, chs_sb[0:1, h:h + 1],
                                           pgs[0:1, (j - 1) * 2048 + h * 128: (j - 1) * 2048 + (h + 1) * 128],
                                           ALU.mult, ALU.add).then_inc(dve_ep.h, 1)
            pl.em("v", f); dve_ep.inc()
            pl.wait("v", dve_ep, dve_ep.n)
            pp_evac[ppc] = dve_ep.n
            ppc += 1
    PREPASS_DVE = dve_ep.n

    emit_loads(0)
    emit_loads(1)
    if "kdup" in DEBUG_TAPS:
        pl.wait("y", *ck_ldh[1])

        def f(sy):
            sy.dma_start(out=taps["kdup"], in_=kdup[:, :]).then_inc(sty.h, 16)
            sy.dma_start(out=taps["qdup"], in_=qdup[:, :]).then_inc(sty.h, 16)
            sy.dma_start(out=taps["extb"], in_=extb[:, :]).then_inc(sty.h, 16)
        pl.em("y", f); sty.inc(48)

    units = [(h, qc, s) for h in range(NH) for qc in range(4) for s in range(16)]
    NU = len(units)
    sc_count = {}

    def emit_sc(u):
        h, qc, s = units[u]
        par = s % 2
        kb = (h % 2) * S
        qb = (h % 2) * SQ
        pl.wait("t", *ck_ldh[h])
        pl.wait("t", *ck_ldx[h])
        pl.wait("t", act_x, exp_of_scbuf.get(par, 0))
        kcA, kcB = 2 * s, 2 * s + 1

        def f(t, par=par, kcA=kcA, kcB=kcB, qc=qc, kb=kb, qb=qb):
            t.matmul(sc_ps[par][:, 0:512],
                     kdup[0:64, kb + kcA * 128: kb + kcA * 128 + 128],
                     qdup[0:64, qb + qc * 512: qb + (qc + 1) * 512],
                     start=True, stop=True, tile_position=(0, 0))
            t.matmul(sc_ps[par][:, 512:1024],
                     kdup[64:128, kb + kcB * 128: kb + kcB * 128 + 128],
                     qdup[64:128, qb + qc * 512: qb + (qc + 1) * 512],
                     start=True, stop=True, tile_position=(64, 0)
                     ).then_inc(pe_sc.h, 1)
        pl.em("t", f)
        pe_sc.inc()
        sc_count[u] = pe_sc.n
        if qc == 3 and s == 15:
            ck_sc_done_h[h] = pe_sc.n

    emit_sc(0)
    emit_sc(1)

    for u in range(NU):
        h, qc, s = units[u]
        par = s % 2
        par4 = s % 4
        xb = (h % 2) * EW
        bvp = qc % 2

        pl.wait("s", pe_sc, sc_count[u])
        pl.wait("s", pe_av, avbv_of_pbuf.get(par4, 0))

        def f(sc, par=par, par4=par4):
            sc.activation(pbuf[:, par4 * 1024: par4 * 1024 + 1024],
                          sc_ps[par][:, :], AF.Exp, bias=0.0, scale=1.0 / 32.0
                          ).then_inc(act_x.h, 1)
        pl.em("s", f)
        act_x.inc()
        expn = act_x.n
        exp_of_scbuf[par] = expn

        if u + 2 < NU:
            emit_sc(u + 2)
        if s == 0 and pending_ep is not None:
            pending_ep()
            pending_ep = None

        pl.wait("t", act_x, expn)
        if s == 0:
            prev2key = (h, qc - 2) if qc >= 2 else ((h - 1, 2 + qc) if h else None)
            if prev2key is not None:
                pl.wait("t", dve_ep, ck_ep_dve.get(prev2key, 0))
        kcA, kcB = 2 * s, 2 * s + 1

        if u == 0:
            pl.wait("t", dve_ep, PREPASS_DVE)

        def f(t, s_=s, par4=par4, kcA=kcA, kcB=kcB, qc=qc, xb=xb, h=h, bvp=bvp):
            ins = None
            if s_ == 2 * qc and qc > 0:
                t.matmul(bv_ps[bvp][:, :],
                         pgs[0:1, (qc - 1) * 2048 + h * 128: (qc - 1) * 2048 + (h + 1) * 128],
                         ones512[0:1, :], start=True, stop=False)
            for j, kc in enumerate((kcA, kcB)):
                ins = t.matmul(av_ps[bvp][:, :],
                               v_sb[:, kc * 1040 + 65 * h: kc * 1040 + 65 * h + 128],
                               pbuf[:, par4 * 1024 + j * 512: par4 * 1024 + (j + 1) * 512],
                               start=(s_ == 0 and j == 0), stop=(s_ == 15 and j == 1))
                if s_ >= 2 * qc:
                    ins = t.matmul(bv_ps[bvp][:, :],
                                   v_sb[:, kc * 1040 + 65 * h: kc * 1040 + 65 * h + 128],
                                   extb[:, xb + 3968 - 128 * kc + 512 * qc: xb + 3968 - 128 * kc + 512 * qc + 512],
                                   start=(qc == 0 and s_ == 0 and j == 0), stop=(s_ == 15 and j == 1))
            ins.then_inc(pe_av.h, 1)
        pl.em("t", f)
        pe_av.inc()
        avbv_of_pbuf[par4] = pe_av.n

        if s == 15:
            av_end = pe_av.n
            pl.wait("v", pe_av, av_end)
            pl.wait("v", dve_ep, ck_ep_dve.get((h, qc - 1) if qc else (h - 1, 3), 0))

            def f(v, bvp=bvp):
                v.tensor_copy(avsb[0:64, :], av_ps[bvp][0:64, :])
                v.tensor_copy(lrow[0:1, :], av_ps[bvp][64:65, :]).then_inc(act_ep.h, 1)
            pl.em("v", f); act_ep.inc()
            ck_ep_act[(h, qc)] = act_ep.n

            def mk_bc(ck_copy=ck_ep_act[(h, qc)], hq=(h, qc)):
                def emit_bc():
                    pl.wait("t", act_ep, ck_copy)

                    def f(t, bvp=hq[1] % 2):
                        t.matmul(av_ps[bvp][0:64, :], ones1f[0:1, 0:64], lrow[0:1, :], start=True, stop=True
                                 ).then_inc(pe_bc.h, 1)
                    pl.em("t", f); pe_bc.inc()
                    ck_bcmm = pe_bc.n

                    pl.wait("v", pe_bc, ck_bcmm)
                    if hq[1] == 0 and hq[0] - 2 in ck_ast:
                        pl.wait("v", *ck_ast[hq[0] - 2])

                    def f(v, bvp=hq[1] % 2):
                        v.reciprocal(lrinv[:, :], av_ps[bvp][0:64, :]).then_inc(dve_ep.h, 1)
                    pl.em("v", f); dve_ep.inc()
                    pl.wait("v", dve_ep, dve_ep.n)

                    def f(v):
                        v.tensor_tensor(avsb[0:64, :], avsb[0:64, :], lrinv[:, :], ALU.mult).then_inc(dve_ep.h, 1)
                    pl.em("v", f); dve_ep.inc()
                    pl.wait("v", dve_ep, dve_ep.n)

                    def f(v, ab=(hq[0] % 2) * SQ, qc=hq[1], bvp=hq[1] % 2):
                        v.tensor_tensor(atn[:, ab + qc * 512: ab + (qc + 1) * 512],
                                        avsb[0:64, :], bv_ps[bvp][0:64, :], ALU.add).then_inc(dve_ep.h, 1)
                    pl.em("v", f); dve_ep.inc()
                    ck_ep_dve[hq] = dve_ep.n
                    if hq[1] == 3:
                        pl.wait("y", dve_ep, ck_ep_dve[hq])

                        def f(sy, h2=hq[0]):
                            sy.dma_start(out=ascr[h2 // 2, (h2 % 2) * 64:(h2 % 2) * 64 + 64, :],
                                         in_=atn[:, (h2 % 2) * SQ:(h2 % 2) * SQ + SQ]).then_inc(ast[h2 % 2].h, 16)
                        pl.em("y", f); ast[hq[0] % 2].inc(16)
                        ck_ast[hq[0]] = (ast[hq[0] % 2], ast[hq[0] % 2].n)
                return emit_bc
            pending_ep = mk_bc()

            if qc == 3:
                ck_av_done_h[h] = pe_av.n
                if h == NH - 1:
                    pending_ep()
                    pending_ep = None
                if h + 2 < NH:
                    emit_loads(h + 2)



    with nc.Block("phaseB") as blk:
        @blk.sync
        def _(sync):
            pl.run("y", sync)
            for s_ in ast + ldh + ldx:
                sync.wait_ge(s_.h, s_.n)

        @blk.gpsimd
        def _(gp):
            pl.run("g", gp)

        @blk.tensor
        def _(t):
            pl.run("t", t)

        @blk.vector
        def _(v):
            pl.run("v", v)

        @blk.scalar
        def _(s):
            pl.run("s", s)

    psB.close()
    ctxB.close()
    ctxAB.close()

    ctxC = ExitStack()
    eC = ctxC.enter_context
    x2 = eC(nc.sbuf_tensor("x2", [128, 8 * SQ], F32))
    b2a_s = eC(nc.sbuf_tensor("b2a_s", [17, D], BF16))
    gaugT = eC(nc.sbuf_tensor("gaugT", [17, SQ], BF16))
    sigall_sb = eC(nc.sbuf_tensor("sigall_sb", [16, 512], BF16))
    identb16 = eC(nc.sbuf_tensor("identb16", [16, 16], BF16))
    logit_sb = eC(nc.sbuf_tensor("logit_sb", [16, 512], F32))
    lnx2 = eC(nc.sbuf_tensor("lnx2", [128, 8 * SQ], BF16))
    w1_s = eC(nc.sbuf_tensor("w1_s", [128, 2 * 8 * DFF], BF16))
    w3_s = eC(nc.sbuf_tensor("w3_s", [128, 2 * 8 * DFF], BF16))
    w2_s = eC(nc.sbuf_tensor("w2_s", [128, 2 * 4 * D], BF16))
    h_s = eC(nc.sbuf_tensor("h_s", [128, 2 * 4 * 512], BF16))
    silu_s = eC(nc.sbuf_tensor("silu_s", [128, 2 * 512], F32))
    gb_sb = eC(nc.sbuf_tensor("gb_sb", [128, 2 * 512], BF16))
    aow = eC(nc.sbuf_tensor("aow", [128, 2 * 8 * 256], F32))
    sq2b = eC(nc.sbuf_tensor("sq2b", [128, 2 * 8 * 256], BF16))
    tmp2 = eC(nc.sbuf_tensor("tmp2", [128, 256], F32))
    tmp2g = eC(nc.sbuf_tensor("tmp2g", [128, 256], F32))
    negmurow2 = eC(nc.sbuf_tensor("negmurow2", [1, 2 * 256], F32))
    varrow2 = eC(nc.sbuf_tensor("varrow2", [1, 2 * 256], F32))
    sqrow2 = eC(nc.sbuf_tensor("sqrow2", [1, 2 * 256], F32))
    mbc_sb = eC(nc.sbuf_tensor("mbc_sb", [128, 2 * 256], F32))
    rbc_sb = eC(nc.sbuf_tensor("rbc_sb", [128, 2 * 256], F32))
    biasc_g = eC(nc.sbuf_tensor("biasc_g", [16, 2 * 128], F32))
    m8_g = eC(nc.sbuf_tensor("m8_g", [128, 2 * 8], F32))
    i8_g = eC(nc.sbuf_tensor("i8_g", [128, 2 * 8], U32))
    idxf_g = eC(nc.sbuf_tensor("idxf_g", [128, 2 * 2], F32))
    mask1_g = eC(nc.sbuf_tensor("mask1_g", [128, 2 * 16], F32))
    mask2_g = eC(nc.sbuf_tensor("mask2_g", [128, 2 * 16], F32))
    g_g = eC(nc.sbuf_tensor("g_g", [128, 2 * 17], F32))

    psC = ExitStack()
    eP = psC.enter_context
    a_ps = [eP(nc.psum_tensor("a_ps%d" % i, [128, 512], F32)) for i in range(2)]
    b_ps = [eP(nc.psum_tensor("b_ps%d" % i, [128, 512], F32)) for i in range(2)]
    y_ps = [eP(nc.psum_tensor("y_ps%d" % i, [128, 512], F32)) for i in range(2)]
    gate_ps = eP(nc.psum_tensor("gate_ps", [128, 512], F32))
    st_ps = eP(nc.psum_tensor("st_ps", [1, 512], F32))
    lg_ps = gate_ps[0:16, 0:512]
    trb_p = [gate_ps[:, 0:16], gate_ps[:, 16:32]]
    trs_p = [gate_ps[:, 256:264].bitcast(BF16), gate_ps[:, 264:272].bitcast(BF16)]
    gtr_p = [gate_ps[0:17, 128:256], gate_ps[0:17, 384:512]]


    pl = Plan()
    for eng in ("t", "v", "s", "y", "g"):
        for s_ in ast:
            pl.wait(eng, s_, s_.n)
        pl.wait(eng, gpi, 2)
        pl.wait(eng, vvi, 1)
        pl.wait(eng, ldi, LDI_FINAL)

    def f(gp):
        gp.dma_start(out=b2a_s[:, :], in_=b2a).then_inc(ldc[0].h, 16)
    pl.em("g", f); ldc[0].inc(16)

    def f(v):
        v.tensor_copy(identb16[:, :], identf[0:16, 0:16])
        v.memset(g_g[:, 16:17], 1.0)
        v.memset(g_g[:, 33:34], 1.0).then_inc(dve_b.h, 1)
    pl.em("v", f); dve_b.inc()

    ck_x2ld = {}
    for tc in range(4):
        def f(sy, tc=tc):
            sy.dma_start(out=bass.AP(x2, tc * 512, [[8 * SQ, 128], [SQ, 8], [1, 512]]),
                         in_=bass.AP(xq.tensor, tc * 512, [[SQ, 128], [128 * SQ, 8], [1, 512]])
                         ).then_inc(ldb2[tc % 2].h, 16)
        pl.em("y", f); ldb2[tc % 2].inc(16)
        ck_x2ld[tc] = (ldb2[tc % 2], ldb2[tc % 2].n)

    def emit_wload(e_, pe_gate):
        buf = e_ % 2
        pl.wait("g", pe_c, pe_gate)

        def f(gp, e_=e_, buf=buf):
            gp.dma_start(out=bass.AP(w1_s, buf * 8 * DFF, [[2 * 8 * DFF, 128], [DFF, 8], [1, DFF]]),
                         in_=w1t[e_].rearrange("(dc p) f -> p dc f", p=128)).then_inc(ldc[buf].h, 16)
            gp.dma_start(out=bass.AP(w3_s, buf * 8 * DFF, [[2 * 8 * DFF, 128], [DFF, 8], [1, DFF]]),
                         in_=w3t[e_].rearrange("(dc p) f -> p dc f", p=128)).then_inc(ldc[buf].h, 16)
            gp.dma_start(out=bass.AP(w2_s, buf * 4 * D, [[2 * 4 * D, 128], [D, 4], [1, D]]),
                         in_=w2t[e_].rearrange("(fc p) d -> p fc d", p=128)).then_inc(ldc[buf].h, 16)
        pl.em("g", f)
        ldc[buf].inc(48)
        return (ldc[buf], ldc[buf].n)

    ESEQ = [16, 17] + list(range(16))
    ck_wld = {16: emit_wload(16, 0), 17: emit_wload(17, 0)}

    uctr = [0]
    ck_sq_free = {}
    ck_rows_read = {}
    ck_row_buf = {}
    ck_mrb_buf = {}
    ck_ap_p0 = {}
    ck_lnx2 = {}
    ck_aow_free = {}
    ck_aowld = {}
    ck_sqd = {}
    ck_sig = {}
    ck_gaug = {}
    ck_gate_done = {}
    ck_lnx2_half = {}
    gstate = {"trb_read": {}, "chunk_done": {}, "g2": {}, "n": 0}

    def emit_gate_chunk(tc, j, ck_lgev):
        gp = gstate["n"] % 2
        gstate["n"] += 1
        biasc = biasc_g[0:16, gp * 128:(gp + 1) * 128]
        m8 = m8_g[:, gp * 8:(gp + 1) * 8]
        i8 = i8_g[:, gp * 8:(gp + 1) * 8]
        idxf = idxf_g[:, gp * 2:(gp + 1) * 2]
        wsum_g = idxf_g[:, gp * 2: gp * 2 + 1]
        winv_g = idxf_g[:, gp * 2 + 1: gp * 2 + 2]
        mk1 = mask1_g[:, gp * 16:(gp + 1) * 16]
        mk2 = mask2_g[:, gp * 16:(gp + 1) * 16]
        gg = g_g[:, gp * 17:(gp + 1) * 17]
        trb_ps = trb_p[gp]
        trs_ps = trs_p[gp]
        gtr_ps = gtr_p[gp]

        pl.wait("v", dve_b, ck_lgev)
        pl.wait("v", pe_b, gstate["trb_read"].get(gp, 0))

        def f(v, tc=tc, j=j, biasc=biasc):
            v.tensor_scalar(biasc, logit_sb[0:16, j * 128:(j + 1) * 128],
                            eb_s[0:16, 0:1], None, ALU.add).then_inc(dve_b.h, 1)
        pl.em("v", f); dve_b.inc()
        ck_bfm = dve_b.n

        pl.wait("t", dve_b, ck_bfm)
        pl.wait("t", act_b, ck_sig[tc])
        pl.wait("t", dve_b, gstate["chunk_done"].get(gp, 0))

        def f(t, j=j, biasc=biasc, trb_ps=trb_ps, trs_ps=trs_ps):
            t.transpose(trb_ps, biasc, identf[0:16, 0:16])
            t.transpose(trs_ps, sigall_sb[0:16, j * 128:(j + 1) * 128], identb16[:, :]
                        ).then_inc(pe_b.h, 1)
        pl.em("t", f); pe_b.inc()
        ck_trn = pe_b.n
        gstate["trb_read"][gp] = ck_trn

        pl.wait("v", pe_b, ck_trn)

        def f(v, m8=m8, trb_ps=trb_ps):
            v.max(m8, trb_ps).then_inc(dve_b.h, 1)
        pl.em("v", f); dve_b.inc()
        pl.wait("v", dve_b, dve_b.n)

        def f(v, m8=m8, i8=i8, trb_ps=trb_ps):
            v.max_index(i8, m8, trb_ps).then_inc(dve_b.h, 1)
        pl.em("v", f); dve_b.inc()
        pl.wait("v", dve_b, dve_b.n)

        gate_steps = [
            lambda v: v.tensor_copy(idxf, i8[:, 0:2]),
            lambda v: v.tensor_tensor(mk1, iota16[:, :],
                                      idxf[:, 0:1].broadcast_to([128, 16]), ALU.is_equal),
            lambda v: v.tensor_tensor(mk2, iota16[:, :],
                                      idxf[:, 1:2].broadcast_to([128, 16]), ALU.is_equal),
            lambda v: v.tensor_tensor(mk1, mk1, mk2, ALU.add),
            lambda v: v.tensor_tensor(mk2, trs_ps, mk1, ALU.mult),
            lambda v: v.reduce_sum(wsum_g, mk2, axis=mybir.AxisListType.X),
            lambda v: v.tensor_scalar(wsum_g, wsum_g, 1e-10, None, ALU.add),
            lambda v: v.reciprocal(winv_g, wsum_g),
        ]
        for step in gate_steps:
            def f(v, step=step):
                step(v).then_inc(dve_b.h, 1)
            pl.em("v", f); dve_b.inc()
            pl.wait("v", dve_b, dve_b.n)

        pl.wait("v", pe_b, gstate["g2"].get(gp, 0))

        def f(v, gg=gg, mk2=mk2, winv_g=winv_g):
            v.tensor_tensor(gg[:, 0:16], mk2,
                            winv_g.broadcast_to([128, 16]), ALU.mult).then_inc(dve_b.h, 1)
        pl.em("v", f); dve_b.inc()
        ck_g = dve_b.n

        pl.wait("t", dve_b, ck_g)

        def f(t, gg=gg, gtr_ps=gtr_ps):
            t.transpose(gtr_ps, gg[:, 0:17], identf[:, :]).then_inc(pe_b.h, 1)
        pl.em("t", f); pe_b.inc()
        ck_g2 = pe_b.n
        gstate["g2"][gp] = ck_g2

        pl.wait("v", pe_b, ck_g2)

        def f(v, tc=tc, j=j, gtr_ps=gtr_ps):
            v.tensor_copy(gaugT[:, tc * 512 + j * 128: tc * 512 + (j + 1) * 128], gtr_ps
                          ).then_inc(dve_b.h, 1)
        pl.em("v", f); dve_b.inc()
        gstate["chunk_done"][gp] = dve_b.n

    def emit_b2_sq(tc, p):
        for h2 in range(2):
            u = 4 * tc + p * 2 + h2
            slot = u % 2
            T0 = tc * 512 + h2 * 256
            pl.wait("g", pe_b, ck_sq_free.get(slot, 0))
            if p == 0:
                pl.wait("g", *ck_aowld[(tc, h2)])
            else:
                pl.wait("g", *ck_x2ld[tc])
                pl.wait("g", dve_b, ck_ap_p0[tc][0])
                pl.wait("g", pool_b, ck_ap_p0[tc][1])

            def f(gp, p=p, h2=h2, T0=T0, slot=slot):
                if p == 0:
                    src = bass.AP(aow, h2 * 2048, [[2 * 2048, 128], [256, 8], [1, 256]])
                else:
                    src = bass.AP(x2, T0, [[8 * SQ, 128], [SQ, 8], [1, 256]])
                dst1 = bass.AP(sq2b, slot * 2048, [[2 * 2048, 128], [256, 8], [1, 256]])
                gp.tensor_tensor(dst1, src, src, ALU.mult).then_inc(pool_b.h, 1)
            pl.em("g", f); pool_b.inc()
            ck_sqd[u] = pool_b.n

    def emit_b2_loads(tc):
        for h2 in range(2):
            if (tc - 1, h2) in ck_aow_free:
                for sv in ck_aow_free[(tc - 1, h2)]:
                    pl.wait("y", *sv)

            def f(sy, tc=tc, h2=h2):
                sy.dma_start(out=bass.AP(aow, h2 * 2048, [[2 * 2048, 128], [256, 8], [1, 256]]),
                             in_=bass.AP(ascr.tensor, tc * 512 + h2 * 256,
                                         [[SQ, 128], [128 * SQ, 8], [1, 256]])
                             ).then_inc(ldb2[h2].h, 16)
            pl.em("y", f); ldb2[h2].inc(16)
            ck_aowld[(tc, h2)] = (ldb2[h2], ldb2[h2].n)
        emit_b2_sq(tc, 0)

    def emit_b2_half(tc, p, h2):
        u = 4 * tc + p * 2 + h2
        rpar = u % 2
        slot = u % 2
        T0 = tc * 512 + h2 * 256

        pl.wait("t", pool_b, ck_sqd[u])
        if p == 0:
            pl.wait("t", *ck_aowld[(tc, h2)])
        else:
            pl.wait("t", *ck_x2ld[tc])
            pl.wait("t", dve_b, ck_ap_p0[tc][0])
            pl.wait("t", pool_b, ck_ap_p0[tc][1])
        if u - 1 in ck_rows_read:
            pl.wait("t", act_b, ck_rows_read[u - 1][0])

        def f(t, p=p, h2=h2, T0=T0, slot=slot):
            for c in range(8):
                if p == 0:
                    xsrc = aow[:, h2 * 2048 + c * 256: h2 * 2048 + (c + 1) * 256]
                else:
                    xsrc = x2[:, c * SQ + T0: c * SQ + T0 + 256]
                t.matmul(st_ps[0:1, 0:256], ones128f[:, 0:1], xsrc, start=(c == 0), stop=(c == 7))
            ins = None
            for c in range(8):
                ins = t.matmul(st_ps[0:1, 256:512], ones128b[:, 0:1],
                               sq2b[:, slot * 2048 + c * 256: slot * 2048 + (c + 1) * 256],
                               start=(c == 0), stop=(c == 7))
            ins.then_inc(pe_b.h, 1)
        pl.em("t", f); pe_b.inc()
        ck_stats_u = pe_b.n
        ck_sq_free[slot] = pe_b.n

        pl.wait("s", pe_b, ck_stats_u)
        if u - 2 in ck_row_buf:
            pl.wait("s", *ck_row_buf[u - 2])

        def f(s, rpar=rpar):
            s.mul(negmurow2[0:1, rpar * 256:(rpar + 1) * 256], st_ps[0:1, 0:256], -1.0 / D)
            s.mul(sqrow2[0:1, rpar * 256:(rpar + 1) * 256], st_ps[0:1, 256:512], 1.0 / D
                  ).then_inc(act_b.h, 1)
        pl.em("s", f); act_b.inc()
        ck_negmu = act_b.n

        pl.wait("s", act_b, ck_negmu)
        if u - 2 in ck_mrb_buf:
            for sv in ck_mrb_buf[u - 2]:
                pl.wait("s", *sv)

        def f(s, rpar=rpar):
            s.dma_start(out=mbc_sb[:, rpar * 256:(rpar + 1) * 256],
                        in_=bass.AP(negmurow2, rpar * 256, [[512, 1], [0, 128], [1, 256]])
                        ).then_inc(gbd[rpar].h, 16)
        pl.em("s", f); gbd[rpar].inc(16)

        pl.wait("v", act_b, ck_negmu)

        def f(v, rpar=rpar):
            vr = varrow2[0:1, rpar * 256:(rpar + 1) * 256]
            v.tensor_tensor(vr, negmurow2[0:1, rpar * 256:(rpar + 1) * 256],
                            negmurow2[0:1, rpar * 256:(rpar + 1) * 256], ALU.mult)
            v.tensor_tensor(vr, sqrow2[0:1, rpar * 256:(rpar + 1) * 256],
                            vr, ALU.subtract).then_inc(dve_b.h, 1)
        pl.em("v", f); dve_b.inc()
        ck_var = dve_b.n
        ck_rows_read[u] = (ck_negmu, ck_var)

        pl.wait("s", dve_b, ck_var)

        def f(s, rpar=rpar):
            sl = varrow2[0:1, rpar * 256:(rpar + 1) * 256]
            s.activation(sl, sl, AF.Ln, bias=epsc[0:1, 0:1], scale=1.0)
            s.activation(sl, sl, AF.Exp, bias=0.0, scale=-0.5).then_inc(act_b.h, 1)
        pl.em("s", f); act_b.inc()
        ck_rstd = act_b.n

        pl.wait("s", act_b, ck_rstd)

        def f(s, rpar=rpar):
            s.dma_start(out=rbc_sb[:, rpar * 256:(rpar + 1) * 256],
                        in_=bass.AP(varrow2, rpar * 256, [[512, 1], [0, 128], [1, 256]])
                        ).then_inc(gbd[rpar].h, 16)
        pl.em("s", f); gbd[rpar].inc(16)
        ck_row_buf[u] = (gbd[rpar], gbd[rpar].n)

        esplit = ((("g", tmp2g, tuple(range(8))),) if p == 1 else
                  (("v", tmp2, tuple(range(8))),))
        for eng, tmpb, csel in esplit:
            pl.wait(eng, *ck_row_buf[u])
            pl.wait(eng, *ck_x2ld[tc])

            if p == 0:
                pl.wait(eng, *ck_aowld[(tc, h2)])

            def f(e, p=p, h2=h2, T0=T0, rpar=rpar, tmpb=tmpb, csel=csel):
                mb = mbc_sb[:, rpar * 256:(rpar + 1) * 256]
                rb = rbc_sb[:, rpar * 256:(rpar + 1) * 256]
                ins = None
                for c in csel:
                    x2sl = x2[:, c * SQ + T0: c * SQ + T0 + 256]
                    if p == 0:
                        aowsl = aow[:, h2 * 2048 + c * 256: h2 * 2048 + (c + 1) * 256]
                        e.tensor_tensor(tmpb[:, :], aowsl, mb, ALU.add)
                        e.tensor_tensor(tmpb[:, :], tmpb[:, :], rb, ALU.mult)
                        e.tensor_scalar(tmpb[:, :], tmpb[:, :], gam8s[:, c:c + 1], bet8s[:, c:c + 1],
                                        ALU.mult, ALU.add)
                        ins = e.tensor_tensor(x2sl, x2sl, tmpb[:, :], ALU.add)
                    else:
                        e.tensor_tensor(tmpb[:, :], x2sl, mb, ALU.add)
                        ins = e.tensor_tensor(lnx2[:, c * SQ + T0: c * SQ + T0 + 256],
                                              tmpb[:, :], rb, ALU.mult)
                ins.then_inc(dve_b.h if eng == "v" else pool_b.h, 1)
            pl.em(eng, f)
            if eng == "v":
                dve_b.inc()
            else:
                pool_b.inc()
        ck_mrb_buf[u] = ((dve_b, dve_b.n), (pool_b, pool_b.n))
        if p == 0:
            ck_aow_free[(tc, h2)] = ((dve_b, dve_b.n), (pool_b, pool_b.n))
            if h2 == 1:
                ck_ap_p0[tc] = (dve_b.n, pool_b.n)
                emit_b2_sq(tc, 1)
        else:
            ck_lnx2_half[(tc, h2)] = ((dve_b, dve_b.n), (pool_b, pool_b.n))
            if h2 == 1:
                ck_lnx2[tc] = ((dve_b, dve_b.n), (pool_b, pool_b.n))

    def emit_b2_gate(tc):
        for sv in ck_lnx2[tc]:
            pl.wait("t", *sv)
        pl.wait("t", dve_b, ck_gate_done.get(tc - 1, 0))
        pl.wait("t", act_b, ck_sig.get(tc - 1, 0))

        def f(t, tc=tc):
            ins = None
            for c in range(8):
                ins = t.matmul(lg_ps, gwt_s[:, c * 16:(c + 1) * 16],
                               lnx2[:, c * SQ + tc * 512: c * SQ + tc * 512 + 512],
                               start=(c == 0), stop=(c == 7))
            ins.then_inc(pe_b.h, 1)
        pl.em("t", f); pe_b.inc()
        ck_lg = pe_b.n

        pl.wait("v", pe_b, ck_lg)

        def f(v):
            v.tensor_copy(logit_sb[:, :], lg_ps).then_inc(dve_b.h, 1)
        pl.em("v", f); dve_b.inc()
        ck_lgev = dve_b.n

        pl.wait("s", pe_b, ck_lg)

        def f(s):
            s.activation(sigall_sb[0:16, :], lg_ps, AF.Sigmoid).then_inc(act_b.h, 1)
        pl.em("s", f); act_b.inc()
        ck_sig[tc] = act_b.n

        for j in range(4):
            emit_gate_chunk(tc, j, ck_lgev)
        ck_gate_done[tc] = dve_b.n
        ck_gaug[tc] = dve_b.n

    emit_b2_loads(0)
    for p in range(2):
        for h2 in range(2):
            emit_b2_half(0, p, h2)

    ck_wld_c = dict(ck_wld)
    ck_ydone = {}
    ck_silu_par = {}
    ck_silu_cons = {}
    ck_b_par = {}
    ck_huse = {}
    ck_wuse = {}
    ck_gbd = {}
    ck_gb_free = {}

    pending_y = None

    def emit_y(prev):
        e_p, tc_p, hb_p, wb2_p, hbuf_p, h_ready_p = prev
        pl.wait("t", dve_c, h_ready_p)
        for sv in ck_lnx2[tc_p]:
            pl.wait("v", *sv)
        for dc in range(8):
            ypar = dc % 2
            pl.wait("t", dve_c, ck_ydone.get(ypar, 0))

            def f(t, wb2=wb2_p, dc=dc, hb=hb_p):
                ins = None
                for fc in range(4):
                    ins = t.matmul(y_ps[dc % 2][:, :],
                                   w2_s[:, wb2 + fc * D + dc * 128: wb2 + fc * D + dc * 128 + 128],
                                   h_s[:, hb + fc * 512: hb + (fc + 1) * 512],
                                   start=(fc == 0), stop=(fc == 3))
                ins.then_inc(pe_c.h, 1)
            pl.em("t", f); pe_c.inc()
            ymm = pe_c.n
            if dc == 7:
                ck_huse[hbuf_p] = pe_c.n

            pl.wait("v", pe_c, ymm)

            def f(v, dc=dc, tc=tc_p, ypar=ypar):
                v.tensor_tensor(x2[:, dc * SQ + tc * 512: dc * SQ + tc * 512 + 512],
                                x2[:, dc * SQ + tc * 512: dc * SQ + tc * 512 + 512],
                                y_ps[ypar][:, :], ALU.add).then_inc(dve_c.h, 1)
            pl.em("v", f); dve_c.inc()
            ck_ydone[ypar] = dve_c.n
        if tc_p == 3:
            ck_wuse[e_p] = pe_c.n
            pos = ESEQ.index(e_p)
            if pos + 2 < NE:
                ck_wld_c[ESEQ[pos + 2]] = emit_wload(ESEQ[pos + 2], ck_wuse[e_p])

    kctr = [0]

    def emit_c_unit(e_, tc):
        nonlocal pending_y
        buf = e_ % 2
        wb1 = buf * 8 * DFF
        wb2 = buf * 4 * D
        k = kctr[0]; kctr[0] += 1
        hbuf = k % 2
        hb = hbuf * 4 * 512
        pl.wait("t", *ck_wld_c[e_])
        if e_ < 16:
            gbuf = k % 2
            pl.wait("s", dve_b, ck_gaug[tc])
            pl.wait("s", dve_c, ck_gb_free.get(gbuf, 0))

            def f(s, e_=e_, tc=tc, gbuf=gbuf):
                s.dma_start(out=gb_sb[:, gbuf * 512:(gbuf + 1) * 512],
                            in_=bass.AP(gaugT, e_ * SQ + tc * 512, [[SQ, 1], [0, 128], [1, 512]])
                            ).then_inc(gbd[gbuf].h, 16)
            pl.em("s", f); gbd[gbuf].inc(16)
            ck_gbd[k] = (gbd[gbuf], gbd[gbuf].n)

        for sv in ck_lnx2[tc]:
            pl.wait("t", *sv)
        for fc in range(4):
            spar = fc % 2
            pl.wait("t", act_c, ck_silu_par.get(spar, 0))
            pl.wait("t", dve_c, ck_b_par.get(spar, 0))

            def f(t, wb1=wb1, fc=fc, tc=tc, spar=spar):
                for dc in range(8):
                    t.matmul(a_ps[spar][:, :], w1_s[:, wb1 + dc * DFF + fc * 128: wb1 + dc * DFF + fc * 128 + 128],
                             lnx2[:, dc * SQ + tc * 512: dc * SQ + tc * 512 + 512],
                             start=(dc == 0), stop=(dc == 7))
                ins = None
                for dc in range(8):
                    ins = t.matmul(b_ps[spar][:, :], w3_s[:, wb1 + dc * DFF + fc * 128: wb1 + dc * DFF + fc * 128 + 128],
                                   lnx2[:, dc * SQ + tc * 512: dc * SQ + tc * 512 + 512],
                                   start=(dc == 0), stop=(dc == 7))
                ins.then_inc(pe_c.h, 1)
            pl.em("t", f); pe_c.inc()
            ck_ab = pe_c.n

            pl.wait("s", pe_c, ck_ab)
            pl.wait("s", dve_c, ck_silu_cons.get(spar, 0))

            def f(s, e_=e_, fc=fc, spar=spar):
                s.activation(silu_s[:, spar * 512:(spar + 1) * 512], a_ps[spar][:, :], AF.Silu,
                             bias=b1s_s[:, e_ * 4 + fc: e_ * 4 + fc + 1], scale=1.0
                             ).then_inc(act_c.h, 1)
            pl.em("s", f); act_c.inc()
            ck_silu_par[spar] = act_c.n

            pl.wait("v", act_c, ck_silu_par[spar])
            if e_ < 16:
                pl.wait("v", *ck_gbd[k])
            pl.wait("v", pe_c, ck_huse.get(hbuf, 0))

            def f(v, e_=e_, fc=fc, hb=hb, spar=spar, gbuf=k % 2):
                if e_ < 16:
                    hsl = h_s[:, hb + fc * 512: hb + (fc + 1) * 512]
                    v.scalar_tensor_tensor(hsl, b_ps[spar][:, :], b3s_s[:, e_ * 4 + fc: e_ * 4 + fc + 1],
                                           gb_sb[:, gbuf * 512:(gbuf + 1) * 512], ALU.add, ALU.mult)
                    v.tensor_tensor(hsl, hsl, silu_s[:, spar * 512:(spar + 1) * 512], ALU.mult
                                    ).then_inc(dve_c.h, 1)
                else:
                    v.scalar_tensor_tensor(h_s[:, hb + fc * 512: hb + (fc + 1) * 512],
                                           b_ps[spar][:, :], b3s_s[:, e_ * 4 + fc: e_ * 4 + fc + 1],
                                           silu_s[:, spar * 512:(spar + 1) * 512], ALU.add, ALU.mult
                                           ).then_inc(dve_c.h, 1)
            pl.em("v", f); dve_c.inc()
            ck_b_par[spar] = dve_c.n
            ck_silu_cons[spar] = dve_c.n
            if e_ < 16 and fc == 3:
                ck_gb_free[k % 2] = dve_c.n
        h_ready = dve_c.n

        if pending_y is not None:
            emit_y(pending_y)
        pending_y = (e_, tc, hb, wb2, hbuf, h_ready)

    for tc in range(3):
        emit_b2_loads(tc + 1)
        emit_c_unit(16, tc)
        emit_b2_half(tc + 1, 0, 0)
        emit_b2_half(tc + 1, 0, 1)
        emit_c_unit(17, tc)
        emit_b2_half(tc + 1, 1, 0)
        emit_b2_half(tc + 1, 1, 1)
    emit_c_unit(16, 3)
    emit_b2_gate(0)
    emit_c_unit(17, 3)
    emit_b2_gate(1)
    emit_c_unit(0, 0)
    emit_b2_gate(2)
    emit_c_unit(0, 1)
    emit_b2_gate(3)
    emit_c_unit(0, 2)
    emit_c_unit(0, 3)
    for e_ in range(1, 16):
        for tc in range(4):
            emit_c_unit(e_, tc)

    emit_y(pending_y)
    pending_y = None

    for tc in range(4):
        pl.wait("t", dve_b, ck_gaug[tc])
        for dc in range(8):
            ypar = dc % 2
            pl.wait("t", dve_c, ck_ydone.get(ypar, 0))

            def f(t, dc=dc, tc=tc, ypar=ypar):
                t.matmul(y_ps[ypar][:, :], b2a_s[0:17, dc * 128:(dc + 1) * 128],
                         gaugT[0:17, tc * 512:(tc + 1) * 512], start=True, stop=True
                         ).then_inc(pe_c.h, 1)
            pl.em("t", f); pe_c.inc()
            mmn = pe_c.n

            pl.wait("v", pe_c, mmn)

            def f(v, dc=dc, tc=tc, ypar=ypar):
                v.tensor_tensor(x2[:, dc * SQ + tc * 512: dc * SQ + tc * 512 + 512],
                                x2[:, dc * SQ + tc * 512: dc * SQ + tc * 512 + 512],
                                y_ps[ypar][:, :], ALU.add).then_inc(dve_c.h, 1)
            pl.em("v", f); dve_c.inc()
            ck_ydone[ypar] = dve_c.n

            pl.wait("y", dve_c, ck_ydone[ypar])

            def f(sy, dc=dc, tc=tc):
                sy.dma_start(out=y[dc][:, tc * 512:(tc + 1) * 512],
                             in_=x2[:, dc * SQ + tc * 512: dc * SQ + tc * 512 + 512]
                             ).then_inc(sty.h, 16)
            pl.em("y", f); sty.inc(16)

    C_DVE_FINAL = dve_c.n

    with nc.Block("phaseBC") as blk:
        @blk.sync
        def _(sync):
            pl.run("y", sync)
            for s_ in ldb2:
                sync.wait_ge(s_.h, s_.n)

        @blk.gpsimd
        def _(gp):
            pl.run("g", gp)

        @blk.tensor
        def _(t):
            pl.run("t", t)

        @blk.vector
        def _(v):
            pl.run("v", v)

        @blk.scalar
        def _(s):
            pl.run("s", s)

    psC.close()

    with nc.Block("phaseD") as blk:
        @blk.sync
        def _(sync):
            sync.wait_ge(dve_c.h, C_DVE_FINAL)
            for s_ in ldc + gbd:
                sync.wait_ge(s_.h, s_.n)
            if "k" in DEBUG_TAPS:
                sync.dma_start(out=taps["k"], in_=kscr).then_inc(sty.h, 16); sty.inc(16)
            if "atn" in DEBUG_TAPS:
                sync.dma_start(out=taps["atn"], in_=ascr).then_inc(sty.h, 16); sty.inc(16)
            if "lnx2" in DEBUG_TAPS:
                sync.dma_start(out=taps["lnx2"], in_=lnx2[:, :]).then_inc(sty.h, 16); sty.inc(16)
            if "gate" in DEBUG_TAPS:
                sync.dma_start(out=taps["gate"], in_=gaugT[:, :]).then_inc(sty.h, 16); sty.inc(16)
            sync.wait_ge(sty.h, sty.n)

    ctxC.close()
    ctx.close()
    return nc


def _stage(inputs):
    f32 = np.float32
    x = np.asarray(inputs["L0_attn_x"], f32)
    Wq = np.asarray(inputs["L0_attn_Wq"], f32)
    Wk = np.asarray(inputs["L0_attn_Wk"], f32)
    Wv = np.asarray(inputs["L0_attn_Wv"], f32)
    tbl = np.asarray(inputs["L0_attn_rel_bias_table"], f32)
    gam = np.asarray(inputs["L0_attn_ln_gamma"], f32)
    bet = np.asarray(inputs["L0_attn_ln_beta"], f32)
    gate_w = np.asarray(inputs["L0_ffn_gate_w"], f32)
    eb = np.asarray(inputs["L0_ffn_expert_bias"], f32)
    w1 = np.asarray(inputs["L0_ffn_w1"], f32)
    b1 = np.asarray(inputs["L0_ffn_b1"], f32)
    w2 = np.asarray(inputs["L0_ffn_w2"], f32)
    b2 = np.asarray(inputs["L0_ffn_b2"], f32)
    w3 = np.asarray(inputs["L0_ffn_w3"], f32)
    b3 = np.asarray(inputs["L0_ffn_b3"], f32)
    sw1 = np.asarray(inputs["L0_ffn_sw1"], f32)
    sb1 = np.asarray(inputs["L0_ffn_sb1"], f32)
    sw2 = np.asarray(inputs["L0_ffn_sw2"], f32)
    sb2 = np.asarray(inputs["L0_ffn_sb2"], f32)
    sw3 = np.asarray(inputs["L0_ffn_sw3"], f32)
    sb3 = np.asarray(inputs["L0_ffn_sb3"], f32)

    shared = {}
    shared["wqt"] = np.ascontiguousarray(Wq.T)
    shared["wkt"] = np.ascontiguousarray(Wk.T)
    shared["wvt"] = np.ascontiguousarray(Wv.T)
    shared["gwt"] = np.ascontiguousarray(gate_w.T)
    w1a = np.concatenate([w1, sw1.reshape(2, 512, D)], 0)
    w3a = np.concatenate([w3, sw3.reshape(2, 512, D)], 0)
    w2a = np.concatenate([w2.transpose(0, 2, 1), sw2.T.reshape(2, 512, D)], 0)
    shared["w1t"] = np.ascontiguousarray(w1a.transpose(0, 2, 1))
    shared["w3t"] = np.ascontiguousarray(w3a.transpose(0, 2, 1))
    shared["w2t"] = np.ascontiguousarray(w2a)
    b1a = np.concatenate([b1, sb1.reshape(2, 512)], 0)
    b3a = np.concatenate([b3, sb3.reshape(2, 512)], 0)
    shared["b1s"] = np.ascontiguousarray(b1a.reshape(NE, 4, 128).transpose(0, 2, 1))
    shared["b3s"] = np.ascontiguousarray(b3a.reshape(NE, 4, 128).transpose(0, 2, 1))
    shared["b2a"] = np.ascontiguousarray(np.concatenate([b2, sb2[None]], 0))
    shared["ebias"] = np.ascontiguousarray(eb)
    shared["gam8"] = np.ascontiguousarray(gam.reshape(8, 128).T)
    shared["bet8"] = np.ascontiguousarray(bet.reshape(8, 128).T)

    ext = np.ascontiguousarray(tbl[np.minimum(np.arange(2 * S - 1), tbl.shape[0] - 1)].T)
    extw_by_q0 = {}
    for Q0 in (0, SQ):
        ew = np.empty((NH, 128, EW), f32)
        for h in range(NH):
            sw = np.lib.stride_tricks.sliding_window_view(ext[h], EW)
            ew[h] = sw[Q0 + 127 - np.arange(128)]
        extw_by_q0[Q0] = np.ascontiguousarray(ew)

    in_maps = []
    for core in range(8):
        b = core // 2
        Q0 = (core % 2) * SQ
        xtb = np.ascontiguousarray(x[b].T)
        m = dict(shared)
        m["chs"] = np.ascontiguousarray(tbl[4094, :].reshape(1, 16))
        m["xt"] = xtb
        m["xq"] = np.ascontiguousarray(xtb[:, Q0:Q0 + SQ])
        m["extw"] = extw_by_q0[Q0]
        in_maps.append(m)
    return in_maps


_NC_CACHE = {}


def _get_nc():
    key = (tuple(DEBUG_TAPS), CACHE_SALT)
    if key not in _NC_CACHE:
        _NC_CACHE[key] = build_nc()
    return _NC_CACHE[key]


def run_cores(inputs, core_ids=None, **kw):
    in_maps = _stage(inputs)
    if core_ids is None:
        core_ids = list(range(8))
    nc = _get_nc()
    return run_bass_kernel_spmd(nc, [in_maps[c] for c in core_ids], core_ids=core_ids, **kw)


def kernel(**inputs):
    res = run_cores(inputs)
    out = np.empty((4, S, D), np.float32)
    for core in range(8):
        b = core // 2
        Q0 = (core % 2) * SQ
        yfm = res.results[core]["y"].reshape(D, SQ)
        out[b, Q0:Q0 + SQ, :] = yfm.T
    return out


# version 52
# speedup vs baseline: 1.0514x; 1.0187x over previous
import numpy as np
from contextlib import ExitStack

import concourse.bass as bass
import concourse.mybir as mybir
from concourse.bass_utils import run_bass_kernel_spmd

F32 = mybir.dt.float32
BF16 = mybir.dt.bfloat16
U32 = mybir.dt.uint32
AF = mybir.ActivationFunctionType
ALU = mybir.AluOpType

D = 1024
S = 4096
SQ = 2048
NH = 16
HD = 64
WA = 256
NWKV = S // WA
NWQ = SQ // WA
NKC = S // 128
EW = 6016
NE = 18
DFF = 512
EPS = 1e-5

DEBUG_TAPS = ()
CACHE_SALT = 1


class Sem:
    def __init__(self, ctx, nc, name):
        self.h = ctx.enter_context(nc.semaphore(name))
        self.name = name
        self.n = 0

    def inc(self, k=1):
        self.n += k
        return self.n


class Plan:
    def __init__(self):
        self.items = []
        self.waited = {}

    def em(self, eng, fn):
        self.items.append((eng, fn))

    def wait(self, eng, sem, val):
        if val <= 0:
            return
        key = (eng, sem.name)
        if self.waited.get(key, 0) >= val:
            return
        self.waited[key] = val
        h = sem.h

        def fn(e, h=h, val=val):
            e.wait_ge(h, val)
        self.items.append((eng, fn))

    def run(self, eng, obj):
        for en, fn in self.items:
            if en == eng:
                fn(obj)


def build_nc():
    nc = bass.Bass("TRN2", target_bir_lowering=False)

    xt = nc.dram_tensor("xt", [D, S], F32, kind="ExternalInput").ap()
    xq = nc.dram_tensor("xq", [D, SQ], F32, kind="ExternalInput").ap()
    wqt = nc.dram_tensor("wqt", [D, D], F32, kind="ExternalInput").ap()
    wkt = nc.dram_tensor("wkt", [D, D], F32, kind="ExternalInput").ap()
    wvt = nc.dram_tensor("wvt", [D, D], F32, kind="ExternalInput").ap()
    extw = nc.dram_tensor("extw", [NH, 128, EW], F32, kind="ExternalInput").ap()
    chs = nc.dram_tensor("chs", [1, 16], F32, kind="ExternalInput").ap()
    gwt = nc.dram_tensor("gwt", [D, 16], F32, kind="ExternalInput").ap()
    w1t = nc.dram_tensor("w1t", [NE, D, DFF], F32, kind="ExternalInput").ap()
    w3t = nc.dram_tensor("w3t", [NE, D, DFF], F32, kind="ExternalInput").ap()
    w2t = nc.dram_tensor("w2t", [NE, DFF, D], F32, kind="ExternalInput").ap()
    b1s = nc.dram_tensor("b1s", [NE, 128, 4], F32, kind="ExternalInput").ap()
    b3s = nc.dram_tensor("b3s", [NE, 128, 4], F32, kind="ExternalInput").ap()
    b2a = nc.dram_tensor("b2a", [17, D], F32, kind="ExternalInput").ap()
    ebias = nc.dram_tensor("ebias", [16], F32, kind="ExternalInput").ap()
    gam8 = nc.dram_tensor("gam8", [128, 8], F32, kind="ExternalInput").ap()
    bet8 = nc.dram_tensor("bet8", [128, 8], F32, kind="ExternalInput").ap()

    y = nc.dram_tensor("y", [8, 128, SQ], F32, kind="ExternalOutput").ap()
    taps = {}
    for t_ in DEBUG_TAPS:
        shp, dt = {
            "k": ([8, 128, S], BF16), "atn": ([8, 128, SQ], F32),
            "kdup": ([128, 2 * S], BF16), "qdup": ([128, 2 * SQ], BF16),
            "aow": ([128, 8 * 512], F32), "gdbg": ([128, 36], F32),
            "extb": ([128, 2 * EW], BF16),
            "x2": ([128, 8 * SQ], F32), "lnx2": ([128, 8 * SQ], BF16),
            "gate": ([17, SQ], BF16), "logits": ([16, SQ], F32),
            "q": ([128, 8 * SQ], BF16), "v": ([128, 32 * 1040 + 64], BF16),
        }[t_]
        taps[t_] = nc.dram_tensor("tap_" + t_, shp, dt, kind="ExternalOutput").ap()

    kscr = nc.dram_tensor("kscr", [8, 128, S], BF16, kind="Internal").ap()
    ascr = nc.dram_tensor("ascr", [8, 128, SQ], F32, kind="Internal").ap()

    ctx = ExitStack()
    e0 = ctx.enter_context

    ones1f = e0(nc.sbuf_tensor("ones1f", [1, 128], F32))
    negones1f = e0(nc.sbuf_tensor("negones1f", [1, 128], F32))
    ones128f = e0(nc.sbuf_tensor("ones128f", [128, 1], F32))
    ones128b = e0(nc.sbuf_tensor("ones128b", [128, 1], BF16))
    identf = e0(nc.sbuf_tensor("identf", [128, 128], F32))
    iota16 = e0(nc.sbuf_tensor("iota16", [128, 16], F32))
    gam8s = e0(nc.sbuf_tensor("gam8s", [128, 8], F32))
    bet8s = e0(nc.sbuf_tensor("bet8s", [128, 8], F32))
    eb_s = e0(nc.sbuf_tensor("eb_s", [16, 1], F32))
    gwt_s = e0(nc.sbuf_tensor("gwt_s", [128, 128], BF16))
    epsc = e0(nc.sbuf_tensor("epsc", [1, 1], F32))
    b1s_s = e0(nc.sbuf_tensor("b1s_s", [128, NE * 4], F32))
    b3s_s = e0(nc.sbuf_tensor("b3s_s", [128, NE * 4], F32))

    sems = {}
    for name in ["ldi", "ldig", "gpi", "vvi", "dve_a", "act_a", "pe_a",
                 "pe_sc", "act_x", "pe_av", "pe_bc", "dve_ep", "act_ep",
                 "dve_b", "act_b", "pe_b", "pool_b",
                 "dve_c", "act_c", "pe_c", "sty",
                 "lda0", "lda1", "lda2", "kst0", "kst1", "ldh0", "ldh1",
                 "ldx0", "ldx1", "ast0", "ast1", "ldb0", "ldb1", "ldc0", "ldc1",
                 "gbd0", "gbd1"]:
        sems[name] = Sem(ctx, nc, name)
    ldi = sems["ldi"]
    ldig = sems["ldig"]
    gpi, vvi = sems["gpi"], sems["vvi"]
    dve_a, act_a, pe_a = sems["dve_a"], sems["act_a"], sems["pe_a"]
    pe_sc, act_x, pe_av = sems["pe_sc"], sems["act_x"], sems["pe_av"]
    pe_bc, dve_ep, act_ep = sems["pe_bc"], sems["dve_ep"], sems["act_ep"]
    dve_b, act_b, pe_b = sems["dve_b"], sems["act_b"], sems["pe_b"]
    pool_b = sems["pool_b"]
    dve_c, act_c, pe_c, sty = sems["dve_c"], sems["act_c"], sems["pe_c"], sems["sty"]
    lda = [sems["lda0"], sems["lda1"], sems["lda2"]]
    kst = [sems["kst0"], sems["kst1"]]
    ldh = [sems["ldh0"], sems["ldh1"]]
    ldx = [sems["ldx0"], sems["ldx1"]]
    ast = [sems["ast0"], sems["ast1"]]
    ldb2 = [sems["ldb0"], sems["ldb1"]]
    ldc = [sems["ldc0"], sems["ldc1"]]
    gbd = [sems["gbd0"], sems["gbd1"]]

    ctxAB = ExitStack()
    eAB = ctxAB.enter_context
    v_sb = eAB(nc.sbuf_tensor("v_sb", [128, 32 * 1040 + 64], BF16))
    q_sb = eAB(nc.sbuf_tensor("q_sb", [128, 8 * SQ], BF16))

    ctxA = ExitStack()
    eA = ctxA.enter_context
    wq_s = eA(nc.sbuf_tensor("wq_s", [128, 8 * D], BF16))
    wk_s = eA(nc.sbuf_tensor("wk_s", [128, 8 * D], BF16))
    wv_s = eA(nc.sbuf_tensor("wv_s", [128, 8 * D], BF16))
    xw = eA(nc.sbuf_tensor("xw", [128, 3 * 8 * WA], F32))
    lnw = eA(nc.sbuf_tensor("lnw", [128, 2 * 8 * WA], BF16))
    sqw = eA(nc.sbuf_tensor("sqw", [128, 8 * WA], BF16))
    tmpa = eA(nc.sbuf_tensor("tmpa", [128, 8 * WA], F32))
    kstg = eA(nc.sbuf_tensor("kstg", [128, 2 * 8 * WA], BF16))
    murow = eA(nc.sbuf_tensor("murow", [1, WA], F32))
    musq = eA(nc.sbuf_tensor("musq", [1, WA], F32))
    varrow = eA(nc.sbuf_tensor("varrow", [1, WA], F32))
    rstdrow = eA(nc.sbuf_tensor("rstdrow", [1, WA], F32))

    psA = ExitStack()
    eP = psA.enter_context
    sx_ps = eP(nc.psum_tensor("sx_ps", [1, WA], F32))
    sq_ps = eP(nc.psum_tensor("sq_ps", [1, WA], F32))
    rb_ps = eP(nc.psum_tensor("rb_ps", [128, WA], F32))
    mb_ps = eP(nc.psum_tensor("mb_ps", [128, WA], F32))
    kp_ps = [eP(nc.psum_tensor("kp_ps%d" % i, [128, WA], F32)) for i in range(2)]
    vp_ps = [eP(nc.psum_tensor("vp_ps%d" % i, [128, 512], F32)) for i in range(2)]

    with nc.Block("init") as blk:
        @blk.sync
        def _(sync):
            sync.dma_start(out=gam8s[:, :], in_=gam8).then_inc(ldi.h, 16); ldi.inc(16)
            sync.dma_start(out=bet8s[:, :], in_=bet8).then_inc(ldi.h, 16); ldi.inc(16)
            sync.dma_start(out=eb_s[:, 0:1], in_=ebias[:, None]).then_inc(ldi.h, 16); ldi.inc(16)
            sync.dma_start(out=b1s_s[:, :].rearrange("p (e c) -> p e c", e=NE),
                           in_=b1s.rearrange("e p c -> p e c")).then_inc(ldi.h, 16); ldi.inc(16)
            sync.dma_start(out=b3s_s[:, :].rearrange("p (e c) -> p e c", e=NE),
                           in_=b3s.rearrange("e p c -> p e c")).then_inc(ldi.h, 16); ldi.inc(16)

        @blk.gpsimd
        def _(gp):
            gp.dma_start(out=wq_s[:, :].rearrange("p (dc e) -> p dc e", dc=8),
                         in_=wqt.rearrange("(dc p) e -> p dc e", p=128)).then_inc(ldig.h, 16); ldig.inc(16)
            gp.dma_start(out=wk_s[:, :].rearrange("p (dc e) -> p dc e", dc=8),
                         in_=wkt.rearrange("(dc p) e -> p dc e", p=128)).then_inc(ldig.h, 16); ldig.inc(16)
            gp.dma_start(out=wv_s[:, :].rearrange("p (dc e) -> p dc e", dc=8),
                         in_=wvt.rearrange("(dc p) e -> p dc e", p=128)).then_inc(ldig.h, 16); ldig.inc(16)
            gp.dma_start(out=gwt_s[:, :].rearrange("p (dc e) -> p dc e", dc=8),
                         in_=gwt.rearrange("(dc p) e -> p dc e", p=128)).then_inc(ldig.h, 16); ldig.inc(16)
            gp.memset(ones1f[0:1, :], 1.0)
            gp.memset(negones1f[0:1, :], -1.0)
            gp.memset(epsc[0:1, 0:1], EPS)
            gp.memset(ones128f[:, 0:1], 1.0)
            gp.memset(ones128b[:, 0:1], 1.0)
            gp.memset(identf[:, :], 0.0).then_inc(gpi.h, 1)
            gp.wait_ge(gpi.h, 1)
            gp.affine_select(out=identf[:, :], in_=identf[:, :], compare_op=ALU.not_equal,
                             fill=1.0, base=0, pattern=[[-1, 128]], channel_multiplier=1)
            gp.iota(iota16[:, :], [[1, 16]], channel_multiplier=0,
                    allow_small_or_imprecise_dtypes=True).then_inc(gpi.h, 1)

        @blk.vector
        def _(v):
            ins = None
            for tc in range(32):
                ins = v.memset(bass.AP(v_sb, tc * 1040 + 64, [[32 * 1040 + 64, 128], [65, 16], [1, 1]]), 1.0)
            ins.then_inc(vvi.h, 1)
            for _i in range(CACHE_SALT):
                v.engine_nop()

    gpi.inc(2)
    vvi.inc()
    LDI_FINAL = ldi.n
    LDIG_FINAL = ldig.n

    pl = Plan()
    ck_apply = {}
    ck_stats = {}
    ck_qkv = {}
    ck_var = {}
    ck_bcast = {}
    ck_kevac = {}
    ck_kdma = {}
    ck_evac_kq = {}
    ck_vevac = {}

    for eng in ("t", "v", "s", "y"):
        pl.wait(eng, ldi, LDI_FINAL)
        pl.wait(eng, ldig, LDIG_FINAL)
        pl.wait(eng, gpi, 2)
        pl.wait(eng, vvi, 1)

    NW = NWKV + NWQ

    def st_dma(w):
        isq = w >= NWKV
        src_ = xq if isq else xt
        col0 = ((w - NWKV) if isq else w) * WA
        ncols = SQ if isq else S
        xbuf = (w % 3) * 8 * WA
        pl.wait("y", dve_a, ck_apply.get(w - 3, 0))

        def f(sy, src_=src_, col0=col0, ncols=ncols, xbuf=xbuf):
            sy.dma_start(out=bass.AP(xw, xbuf, [[3 * 8 * WA, 128], [WA, 8], [1, WA]]),
                         in_=bass.AP(src_.tensor, col0, [[ncols, 128], [128 * ncols, 8], [1, WA]])
                         ).then_inc(lda[w % 3].h, 16)
        pl.em("y", f); lda[w % 3].inc(16)
        ck_lda[w] = (lda[w % 3], lda[w % 3].n)

    def st_chain(w):
        xbuf = (w % 3) * 8 * WA
        pl.wait("v", *ck_lda[w])
        pl.wait("v", pe_a, ck_stats.get(w - 1, 0))

        def f(v, xbuf=xbuf):
            v.tensor_tensor(sqw[:, :], xw[:, xbuf:xbuf + 8 * WA],
                            xw[:, xbuf:xbuf + 8 * WA], ALU.mult).then_inc(dve_a.h, 1)
        pl.em("v", f); dve_a.inc()
        ck_sq = dve_a.n

        pl.wait("t", *ck_lda[w])
        pl.wait("t", dve_a, ck_sq)
        pl.wait("t", dve_a, ck_var.get(w - 1, 0))

        def f(t, xbuf=xbuf):
            for c in range(8):
                t.matmul(sx_ps[0:1, :], ones128f[:, 0:1], xw[:, xbuf + c * WA: xbuf + (c + 1) * WA],
                         start=(c == 0), stop=(c == 7))
            ins = None
            for c in range(8):
                ins = t.matmul(sq_ps[0:1, :], ones128b[:, 0:1], sqw[:, c * WA:(c + 1) * WA],
                               start=(c == 0), stop=(c == 7))
            ins.then_inc(pe_a.h, 1)
        pl.em("t", f); pe_a.inc()
        ck_stats[w] = pe_a.n

        pl.wait("v", pe_a, ck_stats[w])
        pl.wait("v", pe_a, ck_bcast.get(w - 1, 0))

        def f(v):
            v.tensor_scalar(murow[0:1, :], sx_ps[0:1, :], 1.0 / D, None, ALU.mult).then_inc(dve_a.h, 1)
        pl.em("v", f); dve_a.inc()
        pl.wait("v", dve_a, dve_a.n)

        def f(v):
            v.tensor_tensor(musq[0:1, :], murow[0:1, :], murow[0:1, :], ALU.mult).then_inc(dve_a.h, 1)
        pl.em("v", f); dve_a.inc()
        pl.wait("v", dve_a, dve_a.n)

        def f(v):
            v.scalar_tensor_tensor(varrow[0:1, :], sq_ps[0:1, :], 1.0 / D, musq[0:1, :],
                                   ALU.mult, ALU.subtract).then_inc(dve_a.h, 1)
        pl.em("v", f); dve_a.inc()
        ck_var[w] = dve_a.n

        pl.wait("s", dve_a, ck_var[w])

        def f(s):
            s.activation(varrow[0:1, :], varrow[0:1, :], AF.Ln, bias=epsc[0:1, 0:1], scale=1.0
                         ).then_inc(act_a.h, 1)
        pl.em("s", f); act_a.inc()
        pl.wait("s", act_a, act_a.n)

        def f(s):
            s.activation(rstdrow[0:1, :], varrow[0:1, :], AF.Exp, bias=0.0, scale=-0.5
                         ).then_inc(act_a.h, 1)
        pl.em("s", f); act_a.inc()
        ck_negmur[w] = act_a.n

    def st_apply(w):
        xbuf = (w % 3) * 8 * WA
        lbuf = (w % 2) * 8 * WA
        pl.wait("t", act_a, ck_negmur[w])
        pl.wait("t", dve_a, ck_apply.get(w - 1, 0))

        def f(t):
            t.matmul(rb_ps[:, :], ones1f[0:1, :], rstdrow[0:1, :], start=True, stop=True)
            t.matmul(mb_ps[:, :], negones1f[0:1, :], murow[0:1, :], start=True, stop=True
                     ).then_inc(pe_a.h, 1)
        pl.em("t", f); pe_a.inc()
        ck_bcast[w] = pe_a.n

        pl.wait("v", pe_a, ck_bcast[w])
        pl.wait("v", pe_a, ck_qkv.get(w - 2, 0))

        def f(v, xbuf=xbuf, lbuf=lbuf):
            v.tensor_tensor(bass.AP(tmpa, 0, [[8 * WA, 128], [WA, 8], [1, WA]]),
                            bass.AP(xw, xbuf, [[3 * 8 * WA, 128], [WA, 8], [1, WA]]),
                            bass.AP(mb_ps, 0, [[WA, 128], [0, 8], [1, WA]]), ALU.add)
            v.tensor_tensor(bass.AP(lnw, lbuf, [[2 * 8 * WA, 128], [WA, 8], [1, WA]]),
                            bass.AP(tmpa, 0, [[8 * WA, 128], [WA, 8], [1, WA]]),
                            bass.AP(rb_ps, 0, [[WA, 128], [0, 8], [1, WA]]), ALU.mult
                            ).then_inc(dve_a.h, 1)
        pl.em("v", f); dve_a.inc()
        ck_apply[w] = dve_a.n

    def st_qkv(w):
        isq = w >= NWKV
        wq_i = w - NWKV
        lbuf = (w % 2) * 8 * WA
        pl.wait("t", dve_a, ck_apply[w])
        if isq:
            for ec in range(8):
                par = ec % 2
                pl.wait("t", act_a, ck_evac_kq.get(par, 0))

                def f(t, ec=ec, lbuf=lbuf, par=par):
                    ins = None
                    for dc in range(8):
                        ins = t.matmul(kp_ps[par][:, :], wq_s[:, dc * D + ec * 128: dc * D + ec * 128 + 128],
                                       lnw[:, lbuf + dc * WA: lbuf + (dc + 1) * WA],
                                       start=(dc == 0), stop=(dc == 7))
                    ins.then_inc(pe_a.h, 1)
                pl.em("t", f); pe_a.inc()
                mmn = pe_a.n

                pl.wait("s", pe_a, mmn)

                def f(s, ec=ec, wq_i=wq_i, par=par):
                    s.copy(q_sb[:, ec * SQ + wq_i * WA: ec * SQ + (wq_i + 1) * WA],
                           kp_ps[par][:, :]).then_inc(act_a.h, 1)
                pl.em("s", f); act_a.inc()
                ck_evac_kq[par] = act_a.n
            ck_qkv[w] = pe_a.n
        else:
            kbuf = (w % 2) * 8 * WA
            if w - 2 in ck_kdma:
                pl.wait("s", *ck_kdma[w - 2])
            for ec in range(8):
                par = ec % 2
                pl.wait("t", act_a, ck_evac_kq.get(par, 0))

                def f(t, ec=ec, lbuf=lbuf, par=par):
                    ins = None
                    for dc in range(8):
                        ins = t.matmul(kp_ps[par][:, :], wk_s[:, dc * D + ec * 128: dc * D + ec * 128 + 128],
                                       lnw[:, lbuf + dc * WA: lbuf + (dc + 1) * WA],
                                       start=(dc == 0), stop=(dc == 7))
                    ins.then_inc(pe_a.h, 1)
                pl.em("t", f); pe_a.inc()
                mmn = pe_a.n

                pl.wait("s", pe_a, mmn)

                def f(s, ec=ec, kbuf=kbuf, par=par):
                    s.copy(kstg[:, kbuf + ec * WA: kbuf + (ec + 1) * WA],
                           kp_ps[par][:, :]).then_inc(act_a.h, 1)
                pl.em("s", f); act_a.inc()
                ck_evac_kq[par] = act_a.n
            ck_kevac = act_a.n

            pl.wait("y", act_a, ck_kevac)

            def f(sy, w=w, kbuf=kbuf):
                sy.dma_start(out=bass.AP(kscr.tensor, w * WA, [[S, 128], [128 * S, 8], [1, WA]]),
                             in_=bass.AP(kstg, kbuf, [[2 * 8 * WA, 128], [WA, 8], [1, WA]])
                             ).then_inc(kst[w % 2].h, 16)
            pl.em("y", f); kst[w % 2].inc(16)
            ck_kdma[w] = (kst[w % 2], kst[w % 2].n)

            for tc2 in range(2):
                tile = 2 * w + tc2
                for half in range(2):
                    pl.wait("t", act_a, ck_vevac.get(half, 0))

                    def f(t, lbuf=lbuf, tc2=tc2, half=half):
                        ins = None
                        for dc in range(8):
                            ins = t.matmul(vp_ps[half][:, :],
                                           lnw[:, lbuf + dc * WA + tc2 * 128: lbuf + dc * WA + tc2 * 128 + 128],
                                           wv_s[:, dc * D + half * 512: dc * D + (half + 1) * 512],
                                           start=(dc == 0), stop=(dc == 7))
                        ins.then_inc(pe_a.h, 1)
                    pl.em("t", f); pe_a.inc()
                    mmn = pe_a.n

                    pl.wait("s", pe_a, mmn)

                    def f(s, tile=tile, half=half):
                        s.copy(
                            bass.AP(v_sb, tile * 1040 + half * 8 * 65, [[32 * 1040 + 64, 128], [65, 8], [1, 64]]),
                            bass.AP(vp_ps[half], 0, [[512, 128], [64, 8], [1, 64]])
                        ).then_inc(act_a.h, 1)
                    pl.em("s", f); act_a.inc()
                    ck_vevac[half] = act_a.n
            ck_qkv[w] = pe_a.n

    ck_lda = {}
    ck_negmur = {}
    for it in range(NW + 3):
        if it < NW:
            st_dma(it)
        if 2 <= it <= NW + 1:
            st_apply(it - 2)
        if 3 <= it <= NW + 2:
            st_qkv(it - 3)
        if 1 <= it <= NW:
            st_chain(it - 1)

    A_PE_FINAL = pe_a.n
    A_DVE_FINAL = dve_a.n
    A_ACT_FINAL = act_a.n

    with nc.Block("phaseA") as blk:
        @blk.sync
        def _(sync):
            pl.run("y", sync)
            for s_ in lda:
                sync.wait_ge(s_.h, s_.n)
            for s_ in kst:
                sync.wait_ge(s_.h, s_.n)

        @blk.tensor
        def _(t):
            pl.run("t", t)

        @blk.vector
        def _(v):
            pl.run("v", v)

        @blk.scalar
        def _(s):
            pl.run("s", s)

    ctxA.close()
    psA.close()

    ctxB = ExitStack()
    eB = ctxB.enter_context
    kdup = eB(nc.sbuf_tensor("kdup", [128, 2 * S], BF16))
    qdup = eB(nc.sbuf_tensor("qdup", [128, 2 * SQ], BF16))
    extb = eB(nc.sbuf_tensor("extb", [128, 2 * EW], BF16))
    pbuf = eB(nc.sbuf_tensor("pbuf", [128, 4 * 1024], BF16))
    avsb = eB(nc.sbuf_tensor("avsb", [65, 512], F32))
    lrow = eB(nc.sbuf_tensor("lrow", [1, 512], F32))
    lrinv = eB(nc.sbuf_tensor("lrinv", [64, 512], F32))
    atn = eB(nc.sbuf_tensor("atn", [64, 2 * SQ], F32))
    pgs = eB(nc.sbuf_tensor("pgs", [1, 3 * 2048], BF16))
    chs_sb = eB(nc.sbuf_tensor("chs_sb", [1, 16], F32))
    ones512 = eB(nc.sbuf_tensor("ones512", [1, 512], BF16))

    psB = ExitStack()
    eP = psB.enter_context
    sc_ps = [eP(nc.psum_tensor("sc_ps%d" % i, [128, 1024], F32)) for i in range(2)]
    av_ps = [eP(nc.psum_tensor("av_ps%d" % i, [128, 512], F32)) for i in range(2)]
    bv_ps = [eP(nc.psum_tensor("bv_ps%d" % i, [128, 512], F32)) for i in range(2)]

    pl = Plan()
    for eng in ("t", "v", "s", "y", "g"):
        for s_ in kst:
            pl.wait(eng, s_, s_.n)
        pl.wait(eng, dve_a, A_DVE_FINAL)
        pl.wait(eng, act_a, A_ACT_FINAL)
        pl.wait(eng, pe_a, A_PE_FINAL)
        pl.wait(eng, gpi, 2)
        pl.wait(eng, vvi, 1)

    pending_ep = None
    ck_sc_done_h = {}
    ck_av_done_h = {}
    ck_ldh = {}
    exp_of_scbuf = {}
    avbv_of_pbuf = {}
    ck_ep_dve = {}
    ck_ep_act = {}
    ck_ast = {}
    ck_ldx = {}

    def emit_loads(h):
        pl.wait("y", pe_sc, ck_sc_done_h.get(h - 2, 0))
        pl.wait("y", pe_av, ck_av_done_h.get(h - 2, 0))
        kb = (h % 2) * S
        qb = (h % 2) * SQ

        def f(sy, h=h, kb=kb, qb=qb):
            src = kscr.tensor
            sy.dma_start(out=kdup[0:64, kb:kb + S],
                         in_=kscr[h // 2, (h % 2) * 64:(h % 2) * 64 + 64, :]).then_inc(ldh[h % 2].h, 16)
            sy.dma_start(out=kdup[64:128, kb:kb + S],
                         in_=kscr[h // 2, (h % 2) * 64:(h % 2) * 64 + 64, :]).then_inc(ldh[h % 2].h, 16)
            qsrc = q_sb[(h % 2) * 64:(h % 2) * 64 + 64, (h // 2) * SQ:(h // 2 + 1) * SQ]
            sy.dma_start(out=qdup[0:64, qb:qb + SQ], in_=qsrc).then_inc(ldh[h % 2].h, 16)
            sy.dma_start(out=qdup[64:128, qb:qb + SQ], in_=qsrc).then_inc(ldh[h % 2].h, 16)
        pl.em("y", f)
        ldh[h % 2].inc(64)

        pl.wait("g", pe_av, ck_av_done_h.get(h - 2, 0))

        def f(gp, h=h):
            gp.dma_start(out=extb[:, (h % 2) * EW:(h % 2 + 1) * EW], in_=extw[h]).then_inc(ldx[h % 2].h, 16)
        pl.em("g", f)
        ldx[h % 2].inc(16)
        ck_ldh[h] = (ldh[h % 2], ldh[h % 2].n)
        ck_ldx[h] = (ldx[h % 2], ldx[h % 2].n)

    if "q" in DEBUG_TAPS:
        def f(sy):
            sy.dma_start(out=taps["q"], in_=q_sb[:, :]).then_inc(sty.h, 16)
        pl.em("y", f); sty.inc(16)
    if "v" in DEBUG_TAPS:
        def f(sy):
            sy.dma_start(out=taps["v"], in_=v_sb[:, :]).then_inc(sty.h, 16)
        pl.em("y", f); sty.inc(16)

    def f(gp):
        gp.dma_start(out=chs_sb[0:1, :], in_=chs).then_inc(ldx[0].h, 16)
        gp.memset(ones512[0:1, :], 1.0).then_inc(ldx[0].h, 1)
    pl.em("g", f); ldx[0].inc(17)
    for eng in ("t", "v"):
        pl.wait(eng, ldx[0], ldx[0].n)

    pp_evac = {}
    ppc = 0
    for h in range(NH):
        off = min(65 * h, 1040 - 256)
        for j in range(3):
            par = ppc % 2
            pl.wait("t", dve_ep, pp_evac.get(ppc - 2, 0))

            def f(t, j=j, off=off, par=par):
                ins = None
                for c in range(4):
                    kc = 4 * j + c
                    ins = t.matmul(bv_ps[par][0:1, 0:256], ones128b[:, 0:1],
                                   v_sb[:, kc * 1040 + off: kc * 1040 + off + 256],
                                   start=(c == 0), stop=(c == 3))
                ins.then_inc(pe_av.h, 1)
            pl.em("t", f); pe_av.inc()
            pl.wait("v", pe_av, pe_av.n)

            def f(v, h=h, j=j, off=off, par=par):
                src_ = bv_ps[par][0:1, 65 * h - off: 65 * h - off + 128]
                dst = pgs[0:1, j * 2048 + h * 128: j * 2048 + (h + 1) * 128]
                if j == 0:
                    v.tensor_scalar(dst, src_, chs_sb[0:1, h:h + 1], None,
                                    ALU.mult).then_inc(dve_ep.h, 1)
                else:
                    v.scalar_tensor_tensor(dst, src_, chs_sb[0:1, h:h + 1],
                                           pgs[0:1, (j - 1) * 2048 + h * 128: (j - 1) * 2048 + (h + 1) * 128],
                                           ALU.mult, ALU.add).then_inc(dve_ep.h, 1)
            pl.em("v", f); dve_ep.inc()
            pl.wait("v", dve_ep, dve_ep.n)
            pp_evac[ppc] = dve_ep.n
            ppc += 1
    PREPASS_DVE = dve_ep.n

    emit_loads(0)
    emit_loads(1)
    if "kdup" in DEBUG_TAPS:
        pl.wait("y", *ck_ldh[1])

        def f(sy):
            sy.dma_start(out=taps["kdup"], in_=kdup[:, :]).then_inc(sty.h, 16)
            sy.dma_start(out=taps["qdup"], in_=qdup[:, :]).then_inc(sty.h, 16)
            sy.dma_start(out=taps["extb"], in_=extb[:, :]).then_inc(sty.h, 16)
        pl.em("y", f); sty.inc(48)

    units = [(h, qc, s) for h in range(NH) for qc in range(4) for s in range(16)]
    NU = len(units)
    sc_count = {}

    def emit_sc(u):
        h, qc, s = units[u]
        par = s % 2
        kb = (h % 2) * S
        qb = (h % 2) * SQ
        pl.wait("t", *ck_ldh[h])
        pl.wait("t", *ck_ldx[h])
        pl.wait("t", act_x, exp_of_scbuf.get(par, 0))
        kcA, kcB = 2 * s, 2 * s + 1

        def f(t, par=par, kcA=kcA, kcB=kcB, qc=qc, kb=kb, qb=qb):
            t.matmul(sc_ps[par][:, 0:512],
                     kdup[0:64, kb + kcA * 128: kb + kcA * 128 + 128],
                     qdup[0:64, qb + qc * 512: qb + (qc + 1) * 512],
                     start=True, stop=True, tile_position=(0, 0))
            t.matmul(sc_ps[par][:, 512:1024],
                     kdup[64:128, kb + kcB * 128: kb + kcB * 128 + 128],
                     qdup[64:128, qb + qc * 512: qb + (qc + 1) * 512],
                     start=True, stop=True, tile_position=(64, 0)
                     ).then_inc(pe_sc.h, 1)
        pl.em("t", f)
        pe_sc.inc()
        sc_count[u] = pe_sc.n
        if qc == 3 and s == 15:
            ck_sc_done_h[h] = pe_sc.n

    emit_sc(0)
    emit_sc(1)

    for u in range(NU):
        h, qc, s = units[u]
        par = s % 2
        par4 = s % 4
        xb = (h % 2) * EW
        bvp = qc % 2

        pl.wait("s", pe_sc, sc_count[u])
        pl.wait("s", pe_av, avbv_of_pbuf.get(par4, 0))

        def f(sc, par=par, par4=par4):
            sc.activation(pbuf[:, par4 * 1024: par4 * 1024 + 1024],
                          sc_ps[par][:, :], AF.Exp, bias=0.0, scale=1.0 / 32.0
                          ).then_inc(act_x.h, 1)
        pl.em("s", f)
        act_x.inc()
        expn = act_x.n
        exp_of_scbuf[par] = expn

        if u + 2 < NU:
            emit_sc(u + 2)
        if s == 0 and pending_ep is not None:
            pending_ep()
            pending_ep = None

        pl.wait("t", act_x, expn)
        if s == 0:
            prev2key = (h, qc - 2) if qc >= 2 else ((h - 1, 2 + qc) if h else None)
            if prev2key is not None:
                pl.wait("t", dve_ep, ck_ep_dve.get(prev2key, 0))
        kcA, kcB = 2 * s, 2 * s + 1

        if u == 0:
            pl.wait("t", dve_ep, PREPASS_DVE)

        def f(t, s_=s, par4=par4, kcA=kcA, kcB=kcB, qc=qc, xb=xb, h=h, bvp=bvp):
            ins = None
            if s_ == 2 * qc and qc > 0:
                t.matmul(bv_ps[bvp][:, :],
                         pgs[0:1, (qc - 1) * 2048 + h * 128: (qc - 1) * 2048 + (h + 1) * 128],
                         ones512[0:1, :], start=True, stop=False)
            for j, kc in enumerate((kcA, kcB)):
                ins = t.matmul(av_ps[bvp][:, :],
                               v_sb[:, kc * 1040 + 65 * h: kc * 1040 + 65 * h + 128],
                               pbuf[:, par4 * 1024 + j * 512: par4 * 1024 + (j + 1) * 512],
                               start=(s_ == 0 and j == 0), stop=(s_ == 15 and j == 1))
                if s_ >= 2 * qc:
                    ins = t.matmul(bv_ps[bvp][:, :],
                                   v_sb[:, kc * 1040 + 65 * h: kc * 1040 + 65 * h + 128],
                                   extb[:, xb + 3968 - 128 * kc + 512 * qc: xb + 3968 - 128 * kc + 512 * qc + 512],
                                   start=(qc == 0 and s_ == 0 and j == 0), stop=(s_ == 15 and j == 1))
            ins.then_inc(pe_av.h, 1)
        pl.em("t", f)
        pe_av.inc()
        avbv_of_pbuf[par4] = pe_av.n

        if s == 15:
            av_end = pe_av.n
            pl.wait("v", pe_av, av_end)
            pl.wait("v", dve_ep, ck_ep_dve.get((h, qc - 1) if qc else (h - 1, 3), 0))

            def f(v, bvp=bvp):
                v.tensor_copy(lrow[0:1, :], av_ps[bvp][64:65, :]).then_inc(act_ep.h, 1)
            pl.em("v", f); act_ep.inc()
            ck_lcopy = act_ep.n

            def f(v, bvp=bvp):
                v.tensor_copy(avsb[0:64, :], av_ps[bvp][0:64, :]).then_inc(act_ep.h, 1)
            pl.em("v", f); act_ep.inc()
            ck_ep_act[(h, qc)] = act_ep.n

            def mk_bc(ck_copy=ck_lcopy, ck_avsb=ck_ep_act[(h, qc)], hq=(h, qc)):
                def emit_bc():
                    pl.wait("t", act_ep, ck_copy)

                    def f(t, bvp=hq[1] % 2):
                        t.matmul(bv_ps[bvp][64:128, :], ones1f[0:1, 0:64], lrow[0:1, :], start=True, stop=True
                                 ).then_inc(pe_bc.h, 1)
                    pl.em("t", f); pe_bc.inc()
                    ck_bcmm = pe_bc.n

                    pl.wait("v", pe_bc, ck_bcmm)
                    pl.wait("v", act_ep, ck_avsb)
                    if hq[1] == 0 and hq[0] - 2 in ck_ast:
                        pl.wait("v", *ck_ast[hq[0] - 2])

                    def f(v, bvp=hq[1] % 2):
                        v.reciprocal(lrinv[:, :], bv_ps[bvp][64:128, :]).then_inc(dve_ep.h, 1)
                    pl.em("v", f); dve_ep.inc()
                    pl.wait("v", dve_ep, dve_ep.n)

                    def f(v):
                        v.tensor_tensor(avsb[0:64, :], avsb[0:64, :], lrinv[:, :], ALU.mult).then_inc(dve_ep.h, 1)
                    pl.em("v", f); dve_ep.inc()
                    pl.wait("v", dve_ep, dve_ep.n)

                    def f(v, ab=(hq[0] % 2) * SQ, qc=hq[1], bvp=hq[1] % 2):
                        v.tensor_tensor(atn[:, ab + qc * 512: ab + (qc + 1) * 512],
                                        avsb[0:64, :], bv_ps[bvp][0:64, :], ALU.add).then_inc(dve_ep.h, 1)
                    pl.em("v", f); dve_ep.inc()
                    ck_ep_dve[hq] = dve_ep.n
                    if hq[1] == 3:
                        pl.wait("y", dve_ep, ck_ep_dve[hq])

                        def f(sy, h2=hq[0]):
                            sy.dma_start(out=ascr[h2 // 2, (h2 % 2) * 64:(h2 % 2) * 64 + 64, :],
                                         in_=atn[:, (h2 % 2) * SQ:(h2 % 2) * SQ + SQ]).then_inc(ast[h2 % 2].h, 16)
                        pl.em("y", f); ast[hq[0] % 2].inc(16)
                        ck_ast[hq[0]] = (ast[hq[0] % 2], ast[hq[0] % 2].n)
                return emit_bc
            pending_ep = mk_bc()

            if qc == 3:
                ck_av_done_h[h] = pe_av.n
                if h == NH - 1:
                    pending_ep()
                    pending_ep = None
                if h + 2 < NH:
                    emit_loads(h + 2)



    with nc.Block("phaseB") as blk:
        @blk.sync
        def _(sync):
            pl.run("y", sync)
            for s_ in ast + ldh + ldx:
                sync.wait_ge(s_.h, s_.n)

        @blk.gpsimd
        def _(gp):
            pl.run("g", gp)

        @blk.tensor
        def _(t):
            pl.run("t", t)

        @blk.vector
        def _(v):
            pl.run("v", v)

        @blk.scalar
        def _(s):
            pl.run("s", s)

    psB.close()
    ctxB.close()
    ctxAB.close()

    ctxC = ExitStack()
    eC = ctxC.enter_context
    x2 = eC(nc.sbuf_tensor("x2", [128, 8 * SQ], F32))
    b2a_s = eC(nc.sbuf_tensor("b2a_s", [17, D], BF16))
    gaugT = eC(nc.sbuf_tensor("gaugT", [17, SQ], BF16))
    sigall_sb = eC(nc.sbuf_tensor("sigall_sb", [16, 512], BF16))
    identb16 = eC(nc.sbuf_tensor("identb16", [16, 16], BF16))
    logit_sb = eC(nc.sbuf_tensor("logit_sb", [16, 512], F32))
    lnx2 = eC(nc.sbuf_tensor("lnx2", [128, 8 * SQ], BF16))
    w1_s = eC(nc.sbuf_tensor("w1_s", [128, 2 * 8 * DFF], BF16))
    w3_s = eC(nc.sbuf_tensor("w3_s", [128, 2 * 8 * DFF], BF16))
    w2_s = eC(nc.sbuf_tensor("w2_s", [128, 2 * 4 * D], BF16))
    h_s = eC(nc.sbuf_tensor("h_s", [128, 2 * 4 * 512], BF16))
    silu_s = eC(nc.sbuf_tensor("silu_s", [128, 2 * 512], F32))
    gb_sb = eC(nc.sbuf_tensor("gb_sb", [128, 2 * 512], BF16))
    aow = eC(nc.sbuf_tensor("aow", [128, 2 * 8 * 256], F32))
    sq2b = eC(nc.sbuf_tensor("sq2b", [128, 2 * 8 * 256], BF16))
    tmp2 = eC(nc.sbuf_tensor("tmp2", [128, 256], F32))
    tmp2g = eC(nc.sbuf_tensor("tmp2g", [128, 256], F32))
    negmurow2 = eC(nc.sbuf_tensor("negmurow2", [1, 2 * 256], F32))
    varrow2 = eC(nc.sbuf_tensor("varrow2", [1, 2 * 256], F32))
    sqrow2 = eC(nc.sbuf_tensor("sqrow2", [1, 2 * 256], F32))
    mbc_sb = eC(nc.sbuf_tensor("mbc_sb", [128, 2 * 256], F32))
    rbc_sb = eC(nc.sbuf_tensor("rbc_sb", [128, 2 * 256], F32))
    biasc_g = eC(nc.sbuf_tensor("biasc_g", [16, 2 * 128], F32))
    m8_g = eC(nc.sbuf_tensor("m8_g", [128, 2 * 8], F32))
    i8_g = eC(nc.sbuf_tensor("i8_g", [128, 2 * 8], U32))
    idxf_g = eC(nc.sbuf_tensor("idxf_g", [128, 2 * 2], F32))
    mask1_g = eC(nc.sbuf_tensor("mask1_g", [128, 2 * 16], F32))
    mask2_g = eC(nc.sbuf_tensor("mask2_g", [128, 2 * 16], F32))
    g_g = eC(nc.sbuf_tensor("g_g", [128, 2 * 17], F32))

    psC = ExitStack()
    eP = psC.enter_context
    a_ps = [eP(nc.psum_tensor("a_ps%d" % i, [128, 512], F32)) for i in range(2)]
    b_ps = [eP(nc.psum_tensor("b_ps%d" % i, [128, 512], F32)) for i in range(2)]
    y_ps = [eP(nc.psum_tensor("y_ps%d" % i, [128, 512], F32)) for i in range(2)]
    gate_ps = eP(nc.psum_tensor("gate_ps", [128, 512], F32))
    st_ps = eP(nc.psum_tensor("st_ps", [1, 512], F32))
    lg_ps = gate_ps[0:16, 0:512]
    trb_p = [gate_ps[:, 0:16], gate_ps[:, 16:32]]
    trs_p = [gate_ps[:, 256:264].bitcast(BF16), gate_ps[:, 264:272].bitcast(BF16)]
    gtr_p = [gate_ps[0:17, 128:256], gate_ps[0:17, 384:512]]


    pl = Plan()
    for eng in ("t", "v", "s", "y", "g"):
        for s_ in ast:
            pl.wait(eng, s_, s_.n)
        pl.wait(eng, gpi, 2)
        pl.wait(eng, vvi, 1)
        pl.wait(eng, ldi, LDI_FINAL)

    def f(gp):
        gp.dma_start(out=b2a_s[:, :], in_=b2a).then_inc(ldc[0].h, 16)
    pl.em("g", f); ldc[0].inc(16)

    def f(v):
        v.tensor_copy(identb16[:, :], identf[0:16, 0:16])
        v.memset(g_g[:, 16:17], 1.0)
        v.memset(g_g[:, 33:34], 1.0).then_inc(dve_b.h, 1)
    pl.em("v", f); dve_b.inc()

    ck_x2ld = {}
    for tc in range(4):
        def f(sy, tc=tc):
            sy.dma_start(out=bass.AP(x2, tc * 512, [[8 * SQ, 128], [SQ, 8], [1, 512]]),
                         in_=bass.AP(xq.tensor, tc * 512, [[SQ, 128], [128 * SQ, 8], [1, 512]])
                         ).then_inc(ldb2[tc % 2].h, 16)
        pl.em("y", f); ldb2[tc % 2].inc(16)
        ck_x2ld[tc] = (ldb2[tc % 2], ldb2[tc % 2].n)

    def emit_wload(e_, pe_gate):
        buf = e_ % 2
        pl.wait("g", pe_c, pe_gate)

        def f(gp, e_=e_, buf=buf):
            gp.dma_start(out=bass.AP(w1_s, buf * 8 * DFF, [[2 * 8 * DFF, 128], [DFF, 8], [1, DFF]]),
                         in_=w1t[e_].rearrange("(dc p) f -> p dc f", p=128)).then_inc(ldc[buf].h, 16)
            gp.dma_start(out=bass.AP(w3_s, buf * 8 * DFF, [[2 * 8 * DFF, 128], [DFF, 8], [1, DFF]]),
                         in_=w3t[e_].rearrange("(dc p) f -> p dc f", p=128)).then_inc(ldc[buf].h, 16)
            gp.dma_start(out=bass.AP(w2_s, buf * 4 * D, [[2 * 4 * D, 128], [D, 4], [1, D]]),
                         in_=w2t[e_].rearrange("(fc p) d -> p fc d", p=128)).then_inc(ldc[buf].h, 16)
        pl.em("g", f)
        ldc[buf].inc(48)
        return (ldc[buf], ldc[buf].n)

    ESEQ = [16, 17] + list(range(16))
    ck_wld = {16: emit_wload(16, 0), 17: emit_wload(17, 0)}

    uctr = [0]
    ck_sq_free = {}
    ck_rows_read = {}
    ck_row_buf = {}
    ck_mrb_buf = {}
    ck_ap_p0 = {}
    ck_lnx2 = {}
    ck_aow_free = {}
    ck_aowld = {}
    ck_sqd = {}
    ck_sig = {}
    ck_gaug = {}
    ck_gate_done = {}
    ck_lnx2_half = {}
    gstate = {"trb_read": {}, "chunk_done": {}, "g2": {}, "n": 0}

    def emit_gate_chunk(tc, j, ck_lgev):
        gp = gstate["n"] % 2
        gstate["n"] += 1
        biasc = biasc_g[0:16, gp * 128:(gp + 1) * 128]
        m8 = m8_g[:, gp * 8:(gp + 1) * 8]
        i8 = i8_g[:, gp * 8:(gp + 1) * 8]
        idxf = idxf_g[:, gp * 2:(gp + 1) * 2]
        wsum_g = idxf_g[:, gp * 2: gp * 2 + 1]
        winv_g = idxf_g[:, gp * 2 + 1: gp * 2 + 2]
        mk1 = mask1_g[:, gp * 16:(gp + 1) * 16]
        mk2 = mask2_g[:, gp * 16:(gp + 1) * 16]
        gg = g_g[:, gp * 17:(gp + 1) * 17]
        trb_ps = trb_p[gp]
        trs_ps = trs_p[gp]
        gtr_ps = gtr_p[gp]

        pl.wait("v", dve_b, ck_lgev)
        pl.wait("v", pe_b, gstate["trb_read"].get(gp, 0))

        def f(v, tc=tc, j=j, biasc=biasc):
            v.tensor_scalar(biasc, logit_sb[0:16, j * 128:(j + 1) * 128],
                            eb_s[0:16, 0:1], None, ALU.add).then_inc(dve_b.h, 1)
        pl.em("v", f); dve_b.inc()
        ck_bfm = dve_b.n

        pl.wait("t", dve_b, ck_bfm)
        pl.wait("t", act_b, ck_sig[tc])
        pl.wait("t", dve_b, gstate["chunk_done"].get(gp, 0))

        def f(t, j=j, biasc=biasc, trb_ps=trb_ps, trs_ps=trs_ps):
            t.transpose(trb_ps, biasc, identf[0:16, 0:16])
            t.transpose(trs_ps, sigall_sb[0:16, j * 128:(j + 1) * 128], identb16[:, :]
                        ).then_inc(pe_b.h, 1)
        pl.em("t", f); pe_b.inc()
        ck_trn = pe_b.n
        gstate["trb_read"][gp] = ck_trn

        pl.wait("v", pe_b, ck_trn)

        def f(v, m8=m8, trb_ps=trb_ps):
            v.max(m8, trb_ps).then_inc(dve_b.h, 1)
        pl.em("v", f); dve_b.inc()
        pl.wait("v", dve_b, dve_b.n)

        def f(v, m8=m8, i8=i8, trb_ps=trb_ps):
            v.max_index(i8, m8, trb_ps).then_inc(dve_b.h, 1)
        pl.em("v", f); dve_b.inc()
        pl.wait("v", dve_b, dve_b.n)

        gate_steps = [
            lambda v: v.tensor_copy(idxf, i8[:, 0:2]),
            lambda v: v.tensor_tensor(mk1, iota16[:, :],
                                      idxf[:, 0:1].broadcast_to([128, 16]), ALU.is_equal),
            lambda v: v.tensor_tensor(mk2, iota16[:, :],
                                      idxf[:, 1:2].broadcast_to([128, 16]), ALU.is_equal),
            lambda v: v.tensor_tensor(mk1, mk1, mk2, ALU.add),
            lambda v: v.tensor_tensor(mk2, trs_ps, mk1, ALU.mult),
            lambda v: v.reduce_sum(wsum_g, mk2, axis=mybir.AxisListType.X),
            lambda v: v.tensor_scalar(wsum_g, wsum_g, 1e-10, None, ALU.add),
            lambda v: v.reciprocal(winv_g, wsum_g),
        ]
        for step in gate_steps:
            def f(v, step=step):
                step(v).then_inc(dve_b.h, 1)
            pl.em("v", f); dve_b.inc()
            pl.wait("v", dve_b, dve_b.n)

        pl.wait("v", pe_b, gstate["g2"].get(gp, 0))

        def f(v, gg=gg, mk2=mk2, winv_g=winv_g):
            v.tensor_tensor(gg[:, 0:16], mk2,
                            winv_g.broadcast_to([128, 16]), ALU.mult).then_inc(dve_b.h, 1)
        pl.em("v", f); dve_b.inc()
        ck_g = dve_b.n

        pl.wait("t", dve_b, ck_g)

        def f(t, gg=gg, gtr_ps=gtr_ps):
            t.transpose(gtr_ps, gg[:, 0:17], identf[:, :]).then_inc(pe_b.h, 1)
        pl.em("t", f); pe_b.inc()
        ck_g2 = pe_b.n
        gstate["g2"][gp] = ck_g2

        pl.wait("v", pe_b, ck_g2)

        def f(v, tc=tc, j=j, gtr_ps=gtr_ps):
            v.tensor_copy(gaugT[:, tc * 512 + j * 128: tc * 512 + (j + 1) * 128], gtr_ps
                          ).then_inc(dve_b.h, 1)
        pl.em("v", f); dve_b.inc()
        gstate["chunk_done"][gp] = dve_b.n

    def emit_b2_sq(tc, p):
        for h2 in range(2):
            u = 4 * tc + p * 2 + h2
            slot = u % 2
            T0 = tc * 512 + h2 * 256
            pl.wait("g", pe_b, ck_sq_free.get(slot, 0))
            if p == 0:
                pl.wait("g", *ck_aowld[(tc, h2)])
            else:
                pl.wait("g", *ck_x2ld[tc])
                pl.wait("g", dve_b, ck_ap_p0[tc][0])
                pl.wait("g", pool_b, ck_ap_p0[tc][1])

            def f(gp, p=p, h2=h2, T0=T0, slot=slot):
                if p == 0:
                    src = bass.AP(aow, h2 * 2048, [[2 * 2048, 128], [256, 8], [1, 256]])
                else:
                    src = bass.AP(x2, T0, [[8 * SQ, 128], [SQ, 8], [1, 256]])
                dst1 = bass.AP(sq2b, slot * 2048, [[2 * 2048, 128], [256, 8], [1, 256]])
                gp.tensor_tensor(dst1, src, src, ALU.mult).then_inc(pool_b.h, 1)
            pl.em("g", f); pool_b.inc()
            ck_sqd[u] = pool_b.n

    def emit_b2_loads(tc):
        for h2 in range(2):
            if (tc - 1, h2) in ck_aow_free:
                for sv in ck_aow_free[(tc - 1, h2)]:
                    pl.wait("y", *sv)

            def f(sy, tc=tc, h2=h2):
                sy.dma_start(out=bass.AP(aow, h2 * 2048, [[2 * 2048, 128], [256, 8], [1, 256]]),
                             in_=bass.AP(ascr.tensor, tc * 512 + h2 * 256,
                                         [[SQ, 128], [128 * SQ, 8], [1, 256]])
                             ).then_inc(ldb2[h2].h, 16)
            pl.em("y", f); ldb2[h2].inc(16)
            ck_aowld[(tc, h2)] = (ldb2[h2], ldb2[h2].n)
        emit_b2_sq(tc, 0)

    def emit_b2_half(tc, p, h2):
        u = 4 * tc + p * 2 + h2
        rpar = u % 2
        slot = u % 2
        T0 = tc * 512 + h2 * 256

        pl.wait("t", pool_b, ck_sqd[u])
        if p == 0:
            pl.wait("t", *ck_aowld[(tc, h2)])
        else:
            pl.wait("t", *ck_x2ld[tc])
            pl.wait("t", dve_b, ck_ap_p0[tc][0])
            pl.wait("t", pool_b, ck_ap_p0[tc][1])
        if u - 1 in ck_rows_read:
            pl.wait("t", act_b, ck_rows_read[u - 1][0])

        def f(t, p=p, h2=h2, T0=T0, slot=slot):
            for c in range(8):
                if p == 0:
                    xsrc = aow[:, h2 * 2048 + c * 256: h2 * 2048 + (c + 1) * 256]
                else:
                    xsrc = x2[:, c * SQ + T0: c * SQ + T0 + 256]
                t.matmul(st_ps[0:1, 0:256], ones128f[:, 0:1], xsrc, start=(c == 0), stop=(c == 7))
            ins = None
            for c in range(8):
                ins = t.matmul(st_ps[0:1, 256:512], ones128b[:, 0:1],
                               sq2b[:, slot * 2048 + c * 256: slot * 2048 + (c + 1) * 256],
                               start=(c == 0), stop=(c == 7))
            ins.then_inc(pe_b.h, 1)
        pl.em("t", f); pe_b.inc()
        ck_stats_u = pe_b.n
        ck_sq_free[slot] = pe_b.n

        pl.wait("s", pe_b, ck_stats_u)
        if u - 2 in ck_row_buf:
            pl.wait("s", *ck_row_buf[u - 2])

        def f(s, rpar=rpar):
            s.mul(negmurow2[0:1, rpar * 256:(rpar + 1) * 256], st_ps[0:1, 0:256], -1.0 / D)
            s.mul(sqrow2[0:1, rpar * 256:(rpar + 1) * 256], st_ps[0:1, 256:512], 1.0 / D
                  ).then_inc(act_b.h, 1)
        pl.em("s", f); act_b.inc()
        ck_negmu = act_b.n

        pl.wait("s", act_b, ck_negmu)
        if u - 2 in ck_mrb_buf:
            for sv in ck_mrb_buf[u - 2]:
                pl.wait("s", *sv)

        def f(s, rpar=rpar):
            s.dma_start(out=mbc_sb[:, rpar * 256:(rpar + 1) * 256],
                        in_=bass.AP(negmurow2, rpar * 256, [[512, 1], [0, 128], [1, 256]])
                        ).then_inc(gbd[rpar].h, 16)
        pl.em("s", f); gbd[rpar].inc(16)

        pl.wait("v", act_b, ck_negmu)

        def f(v, rpar=rpar):
            vr = varrow2[0:1, rpar * 256:(rpar + 1) * 256]
            v.tensor_tensor(vr, negmurow2[0:1, rpar * 256:(rpar + 1) * 256],
                            negmurow2[0:1, rpar * 256:(rpar + 1) * 256], ALU.mult)
            v.tensor_tensor(vr, sqrow2[0:1, rpar * 256:(rpar + 1) * 256],
                            vr, ALU.subtract).then_inc(dve_b.h, 1)
        pl.em("v", f); dve_b.inc()
        ck_var = dve_b.n
        ck_rows_read[u] = (ck_negmu, ck_var)

        pl.wait("s", dve_b, ck_var)

        def f(s, rpar=rpar):
            sl = varrow2[0:1, rpar * 256:(rpar + 1) * 256]
            s.activation(sl, sl, AF.Ln, bias=epsc[0:1, 0:1], scale=1.0)
            s.activation(sl, sl, AF.Exp, bias=0.0, scale=-0.5).then_inc(act_b.h, 1)
        pl.em("s", f); act_b.inc()
        ck_rstd = act_b.n

        pl.wait("s", act_b, ck_rstd)

        def f(s, rpar=rpar):
            s.dma_start(out=rbc_sb[:, rpar * 256:(rpar + 1) * 256],
                        in_=bass.AP(varrow2, rpar * 256, [[512, 1], [0, 128], [1, 256]])
                        ).then_inc(gbd[rpar].h, 16)
        pl.em("s", f); gbd[rpar].inc(16)
        ck_row_buf[u] = (gbd[rpar], gbd[rpar].n)

        esplit = ((("g", tmp2g, tuple(range(8))),) if p == 1 else
                  (("v", tmp2, tuple(range(8))),))
        for eng, tmpb, csel in esplit:
            pl.wait(eng, *ck_row_buf[u])
            pl.wait(eng, *ck_x2ld[tc])

            if p == 0:
                pl.wait(eng, *ck_aowld[(tc, h2)])

            def f(e, p=p, h2=h2, T0=T0, rpar=rpar, tmpb=tmpb, csel=csel):
                mb = mbc_sb[:, rpar * 256:(rpar + 1) * 256]
                rb = rbc_sb[:, rpar * 256:(rpar + 1) * 256]
                ins = None
                for c in csel:
                    x2sl = x2[:, c * SQ + T0: c * SQ + T0 + 256]
                    if p == 0:
                        aowsl = aow[:, h2 * 2048 + c * 256: h2 * 2048 + (c + 1) * 256]
                        e.tensor_tensor(tmpb[:, :], aowsl, mb, ALU.add)
                        e.tensor_tensor(tmpb[:, :], tmpb[:, :], rb, ALU.mult)
                        e.tensor_scalar(tmpb[:, :], tmpb[:, :], gam8s[:, c:c + 1], bet8s[:, c:c + 1],
                                        ALU.mult, ALU.add)
                        ins = e.tensor_tensor(x2sl, x2sl, tmpb[:, :], ALU.add)
                    else:
                        e.tensor_tensor(tmpb[:, :], x2sl, mb, ALU.add)
                        ins = e.tensor_tensor(lnx2[:, c * SQ + T0: c * SQ + T0 + 256],
                                              tmpb[:, :], rb, ALU.mult)
                ins.then_inc(dve_b.h if eng == "v" else pool_b.h, 1)
            pl.em(eng, f)
            if eng == "v":
                dve_b.inc()
            else:
                pool_b.inc()
        ck_mrb_buf[u] = ((dve_b, dve_b.n), (pool_b, pool_b.n))
        if p == 0:
            ck_aow_free[(tc, h2)] = ((dve_b, dve_b.n), (pool_b, pool_b.n))
            if h2 == 1:
                ck_ap_p0[tc] = (dve_b.n, pool_b.n)
                emit_b2_sq(tc, 1)
        else:
            ck_lnx2_half[(tc, h2)] = ((dve_b, dve_b.n), (pool_b, pool_b.n))
            if h2 == 1:
                ck_lnx2[tc] = ((dve_b, dve_b.n), (pool_b, pool_b.n))

    def emit_b2_gate(tc):
        for sv in ck_lnx2[tc]:
            pl.wait("t", *sv)
        pl.wait("t", dve_b, ck_gate_done.get(tc - 1, 0))
        pl.wait("t", act_b, ck_sig.get(tc - 1, 0))

        def f(t, tc=tc):
            ins = None
            for c in range(8):
                ins = t.matmul(lg_ps, gwt_s[:, c * 16:(c + 1) * 16],
                               lnx2[:, c * SQ + tc * 512: c * SQ + tc * 512 + 512],
                               start=(c == 0), stop=(c == 7))
            ins.then_inc(pe_b.h, 1)
        pl.em("t", f); pe_b.inc()
        ck_lg = pe_b.n

        pl.wait("v", pe_b, ck_lg)

        def f(v):
            v.tensor_copy(logit_sb[:, :], lg_ps).then_inc(dve_b.h, 1)
        pl.em("v", f); dve_b.inc()
        ck_lgev = dve_b.n

        pl.wait("s", pe_b, ck_lg)

        def f(s):
            s.activation(sigall_sb[0:16, :], lg_ps, AF.Sigmoid).then_inc(act_b.h, 1)
        pl.em("s", f); act_b.inc()
        ck_sig[tc] = act_b.n

        for j in range(4):
            emit_gate_chunk(tc, j, ck_lgev)
        ck_gate_done[tc] = dve_b.n
        ck_gaug[tc] = dve_b.n

    emit_b2_loads(0)
    for p in range(2):
        for h2 in range(2):
            emit_b2_half(0, p, h2)

    ck_wld_c = dict(ck_wld)
    ck_ydone = {}
    ck_silu_par = {}
    ck_silu_cons = {}
    ck_b_par = {}
    ck_huse = {}
    ck_wuse = {}
    ck_gbd = {}
    ck_gb_free = {}

    pending_y = None

    def emit_y(prev):
        e_p, tc_p, hb_p, wb2_p, hbuf_p, h_ready_p = prev
        pl.wait("t", dve_c, h_ready_p)
        for sv in ck_lnx2[tc_p]:
            pl.wait("v", *sv)
        for dc in range(8):
            ypar = dc % 2
            pl.wait("t", dve_c, ck_ydone.get(ypar, 0))

            def f(t, wb2=wb2_p, dc=dc, hb=hb_p):
                ins = None
                for fc in range(4):
                    ins = t.matmul(y_ps[dc % 2][:, :],
                                   w2_s[:, wb2 + fc * D + dc * 128: wb2 + fc * D + dc * 128 + 128],
                                   h_s[:, hb + fc * 512: hb + (fc + 1) * 512],
                                   start=(fc == 0), stop=(fc == 3))
                ins.then_inc(pe_c.h, 1)
            pl.em("t", f); pe_c.inc()
            ymm = pe_c.n
            if dc == 7:
                ck_huse[hbuf_p] = pe_c.n

            pl.wait("v", pe_c, ymm)

            def f(v, dc=dc, tc=tc_p, ypar=ypar):
                v.tensor_tensor(x2[:, dc * SQ + tc * 512: dc * SQ + tc * 512 + 512],
                                x2[:, dc * SQ + tc * 512: dc * SQ + tc * 512 + 512],
                                y_ps[ypar][:, :], ALU.add).then_inc(dve_c.h, 1)
            pl.em("v", f); dve_c.inc()
            ck_ydone[ypar] = dve_c.n
        if tc_p == 3:
            ck_wuse[e_p] = pe_c.n
            pos = ESEQ.index(e_p)
            if pos + 2 < NE:
                ck_wld_c[ESEQ[pos + 2]] = emit_wload(ESEQ[pos + 2], ck_wuse[e_p])

    kctr = [0]

    def emit_c_unit(e_, tc):
        nonlocal pending_y
        buf = e_ % 2
        wb1 = buf * 8 * DFF
        wb2 = buf * 4 * D
        k = kctr[0]; kctr[0] += 1
        hbuf = k % 2
        hb = hbuf * 4 * 512
        pl.wait("t", *ck_wld_c[e_])
        if e_ < 16:
            gbuf = k % 2
            pl.wait("s", dve_b, ck_gaug[tc])
            pl.wait("s", dve_c, ck_gb_free.get(gbuf, 0))

            def f(s, e_=e_, tc=tc, gbuf=gbuf):
                s.dma_start(out=gb_sb[:, gbuf * 512:(gbuf + 1) * 512],
                            in_=bass.AP(gaugT, e_ * SQ + tc * 512, [[SQ, 1], [0, 128], [1, 512]])
                            ).then_inc(gbd[gbuf].h, 16)
            pl.em("s", f); gbd[gbuf].inc(16)
            ck_gbd[k] = (gbd[gbuf], gbd[gbuf].n)

        for sv in ck_lnx2[tc]:
            pl.wait("t", *sv)
        for fc in range(4):
            spar = fc % 2
            pl.wait("t", act_c, ck_silu_par.get(spar, 0))
            pl.wait("t", dve_c, ck_b_par.get(spar, 0))

            def f(t, wb1=wb1, fc=fc, tc=tc, spar=spar):
                for dc in range(8):
                    t.matmul(a_ps[spar][:, :], w1_s[:, wb1 + dc * DFF + fc * 128: wb1 + dc * DFF + fc * 128 + 128],
                             lnx2[:, dc * SQ + tc * 512: dc * SQ + tc * 512 + 512],
                             start=(dc == 0), stop=(dc == 7))
                ins = None
                for dc in range(8):
                    ins = t.matmul(b_ps[spar][:, :], w3_s[:, wb1 + dc * DFF + fc * 128: wb1 + dc * DFF + fc * 128 + 128],
                                   lnx2[:, dc * SQ + tc * 512: dc * SQ + tc * 512 + 512],
                                   start=(dc == 0), stop=(dc == 7))
                ins.then_inc(pe_c.h, 1)
            pl.em("t", f); pe_c.inc()
            ck_ab = pe_c.n

            pl.wait("s", pe_c, ck_ab)
            pl.wait("s", dve_c, ck_silu_cons.get(spar, 0))

            def f(s, e_=e_, fc=fc, spar=spar):
                s.activation(silu_s[:, spar * 512:(spar + 1) * 512], a_ps[spar][:, :], AF.Silu,
                             bias=b1s_s[:, e_ * 4 + fc: e_ * 4 + fc + 1], scale=1.0
                             ).then_inc(act_c.h, 1)
            pl.em("s", f); act_c.inc()
            ck_silu_par[spar] = act_c.n

            pl.wait("v", act_c, ck_silu_par[spar])
            if e_ < 16:
                pl.wait("v", *ck_gbd[k])
            pl.wait("v", pe_c, ck_huse.get(hbuf, 0))

            def f(v, e_=e_, fc=fc, hb=hb, spar=spar, gbuf=k % 2):
                if e_ < 16:
                    hsl = h_s[:, hb + fc * 512: hb + (fc + 1) * 512]
                    v.scalar_tensor_tensor(hsl, b_ps[spar][:, :], b3s_s[:, e_ * 4 + fc: e_ * 4 + fc + 1],
                                           gb_sb[:, gbuf * 512:(gbuf + 1) * 512], ALU.add, ALU.mult)
                    v.tensor_tensor(hsl, hsl, silu_s[:, spar * 512:(spar + 1) * 512], ALU.mult
                                    ).then_inc(dve_c.h, 1)
                else:
                    v.scalar_tensor_tensor(h_s[:, hb + fc * 512: hb + (fc + 1) * 512],
                                           b_ps[spar][:, :], b3s_s[:, e_ * 4 + fc: e_ * 4 + fc + 1],
                                           silu_s[:, spar * 512:(spar + 1) * 512], ALU.add, ALU.mult
                                           ).then_inc(dve_c.h, 1)
            pl.em("v", f); dve_c.inc()
            ck_b_par[spar] = dve_c.n
            ck_silu_cons[spar] = dve_c.n
            if e_ < 16 and fc == 3:
                ck_gb_free[k % 2] = dve_c.n
        h_ready = dve_c.n

        if pending_y is not None:
            emit_y(pending_y)
        pending_y = (e_, tc, hb, wb2, hbuf, h_ready)

    for tc in range(3):
        emit_b2_loads(tc + 1)
        emit_c_unit(16, tc)
        emit_b2_half(tc + 1, 0, 0)
        emit_b2_half(tc + 1, 0, 1)
        emit_c_unit(17, tc)
        emit_b2_half(tc + 1, 1, 0)
        emit_b2_half(tc + 1, 1, 1)
    emit_c_unit(16, 3)
    emit_b2_gate(0)
    emit_c_unit(17, 3)
    emit_b2_gate(1)
    emit_c_unit(0, 0)
    emit_b2_gate(2)
    emit_c_unit(0, 1)
    emit_b2_gate(3)
    emit_c_unit(0, 2)
    emit_c_unit(0, 3)
    for e_ in range(1, 16):
        for tc in range(4):
            emit_c_unit(e_, tc)

    emit_y(pending_y)
    pending_y = None

    for tc in range(4):
        pl.wait("t", dve_b, ck_gaug[tc])
        for dc in range(8):
            ypar = dc % 2
            pl.wait("t", dve_c, ck_ydone.get(ypar, 0))

            def f(t, dc=dc, tc=tc, ypar=ypar):
                t.matmul(y_ps[ypar][:, :], b2a_s[0:17, dc * 128:(dc + 1) * 128],
                         gaugT[0:17, tc * 512:(tc + 1) * 512], start=True, stop=True
                         ).then_inc(pe_c.h, 1)
            pl.em("t", f); pe_c.inc()
            mmn = pe_c.n

            pl.wait("v", pe_c, mmn)

            def f(v, dc=dc, tc=tc, ypar=ypar):
                v.tensor_tensor(x2[:, dc * SQ + tc * 512: dc * SQ + tc * 512 + 512],
                                x2[:, dc * SQ + tc * 512: dc * SQ + tc * 512 + 512],
                                y_ps[ypar][:, :], ALU.add).then_inc(dve_c.h, 1)
            pl.em("v", f); dve_c.inc()
            ck_ydone[ypar] = dve_c.n

            pl.wait("y", dve_c, ck_ydone[ypar])

            def f(sy, dc=dc, tc=tc):
                sy.dma_start(out=y[dc][:, tc * 512:(tc + 1) * 512],
                             in_=x2[:, dc * SQ + tc * 512: dc * SQ + tc * 512 + 512]
                             ).then_inc(sty.h, 16)
            pl.em("y", f); sty.inc(16)

    C_DVE_FINAL = dve_c.n

    with nc.Block("phaseBC") as blk:
        @blk.sync
        def _(sync):
            pl.run("y", sync)
            for s_ in ldb2:
                sync.wait_ge(s_.h, s_.n)

        @blk.gpsimd
        def _(gp):
            pl.run("g", gp)

        @blk.tensor
        def _(t):
            pl.run("t", t)

        @blk.vector
        def _(v):
            pl.run("v", v)

        @blk.scalar
        def _(s):
            pl.run("s", s)

    psC.close()

    with nc.Block("phaseD") as blk:
        @blk.sync
        def _(sync):
            sync.wait_ge(dve_c.h, C_DVE_FINAL)
            for s_ in ldc + gbd:
                sync.wait_ge(s_.h, s_.n)
            if "k" in DEBUG_TAPS:
                sync.dma_start(out=taps["k"], in_=kscr).then_inc(sty.h, 16); sty.inc(16)
            if "atn" in DEBUG_TAPS:
                sync.dma_start(out=taps["atn"], in_=ascr).then_inc(sty.h, 16); sty.inc(16)
            if "lnx2" in DEBUG_TAPS:
                sync.dma_start(out=taps["lnx2"], in_=lnx2[:, :]).then_inc(sty.h, 16); sty.inc(16)
            if "gate" in DEBUG_TAPS:
                sync.dma_start(out=taps["gate"], in_=gaugT[:, :]).then_inc(sty.h, 16); sty.inc(16)
            sync.wait_ge(sty.h, sty.n)

    ctxC.close()
    ctx.close()
    return nc


def _stage(inputs):
    f32 = np.float32
    x = np.asarray(inputs["L0_attn_x"], f32)
    Wq = np.asarray(inputs["L0_attn_Wq"], f32)
    Wk = np.asarray(inputs["L0_attn_Wk"], f32)
    Wv = np.asarray(inputs["L0_attn_Wv"], f32)
    tbl = np.asarray(inputs["L0_attn_rel_bias_table"], f32)
    gam = np.asarray(inputs["L0_attn_ln_gamma"], f32)
    bet = np.asarray(inputs["L0_attn_ln_beta"], f32)
    gate_w = np.asarray(inputs["L0_ffn_gate_w"], f32)
    eb = np.asarray(inputs["L0_ffn_expert_bias"], f32)
    w1 = np.asarray(inputs["L0_ffn_w1"], f32)
    b1 = np.asarray(inputs["L0_ffn_b1"], f32)
    w2 = np.asarray(inputs["L0_ffn_w2"], f32)
    b2 = np.asarray(inputs["L0_ffn_b2"], f32)
    w3 = np.asarray(inputs["L0_ffn_w3"], f32)
    b3 = np.asarray(inputs["L0_ffn_b3"], f32)
    sw1 = np.asarray(inputs["L0_ffn_sw1"], f32)
    sb1 = np.asarray(inputs["L0_ffn_sb1"], f32)
    sw2 = np.asarray(inputs["L0_ffn_sw2"], f32)
    sb2 = np.asarray(inputs["L0_ffn_sb2"], f32)
    sw3 = np.asarray(inputs["L0_ffn_sw3"], f32)
    sb3 = np.asarray(inputs["L0_ffn_sb3"], f32)

    shared = {}
    shared["wqt"] = np.ascontiguousarray(Wq.T)
    shared["wkt"] = np.ascontiguousarray(Wk.T)
    shared["wvt"] = np.ascontiguousarray(Wv.T)
    shared["gwt"] = np.ascontiguousarray(gate_w.T)
    w1a = np.concatenate([w1, sw1.reshape(2, 512, D)], 0)
    w3a = np.concatenate([w3, sw3.reshape(2, 512, D)], 0)
    w2a = np.concatenate([w2.transpose(0, 2, 1), sw2.T.reshape(2, 512, D)], 0)
    shared["w1t"] = np.ascontiguousarray(w1a.transpose(0, 2, 1))
    shared["w3t"] = np.ascontiguousarray(w3a.transpose(0, 2, 1))
    shared["w2t"] = np.ascontiguousarray(w2a)
    b1a = np.concatenate([b1, sb1.reshape(2, 512)], 0)
    b3a = np.concatenate([b3, sb3.reshape(2, 512)], 0)
    shared["b1s"] = np.ascontiguousarray(b1a.reshape(NE, 4, 128).transpose(0, 2, 1))
    shared["b3s"] = np.ascontiguousarray(b3a.reshape(NE, 4, 128).transpose(0, 2, 1))
    shared["b2a"] = np.ascontiguousarray(np.concatenate([b2, sb2[None]], 0))
    shared["ebias"] = np.ascontiguousarray(eb)
    shared["gam8"] = np.ascontiguousarray(gam.reshape(8, 128).T)
    shared["bet8"] = np.ascontiguousarray(bet.reshape(8, 128).T)

    ext = np.ascontiguousarray(tbl[np.minimum(np.arange(2 * S - 1), tbl.shape[0] - 1)].T)
    extw_by_q0 = {}
    for Q0 in (0, SQ):
        ew = np.empty((NH, 128, EW), f32)
        for h in range(NH):
            sw = np.lib.stride_tricks.sliding_window_view(ext[h], EW)
            ew[h] = sw[Q0 + 127 - np.arange(128)]
        extw_by_q0[Q0] = np.ascontiguousarray(ew)

    in_maps = []
    for core in range(8):
        b = core // 2
        Q0 = (core % 2) * SQ
        xtb = np.ascontiguousarray(x[b].T)
        m = dict(shared)
        m["chs"] = np.ascontiguousarray(tbl[4094, :].reshape(1, 16))
        m["xt"] = xtb
        m["xq"] = np.ascontiguousarray(xtb[:, Q0:Q0 + SQ])
        m["extw"] = extw_by_q0[Q0]
        in_maps.append(m)
    return in_maps


_NC_CACHE = {}


def _get_nc():
    key = (tuple(DEBUG_TAPS), CACHE_SALT)
    if key not in _NC_CACHE:
        _NC_CACHE[key] = build_nc()
    return _NC_CACHE[key]


def run_cores(inputs, core_ids=None, **kw):
    in_maps = _stage(inputs)
    if core_ids is None:
        core_ids = list(range(8))
    nc = _get_nc()
    return run_bass_kernel_spmd(nc, [in_maps[c] for c in core_ids], core_ids=core_ids, **kw)


def kernel(**inputs):
    res = run_cores(inputs)
    out = np.empty((4, S, D), np.float32)
    for core in range(8):
        b = core // 2
        Q0 = (core % 2) * SQ
        yfm = res.results[core]["y"].reshape(D, SQ)
        out[b, Q0:Q0 + SQ, :] = yfm.T
    return out
